# version 11
# speedup vs baseline: 1.0693x; 1.0570x over previous
from contextlib import ExitStack
import numpy as np
import ml_dtypes
import concourse.bass as bass
import concourse.mybir as mybir
from concourse.bass_utils import run_bass_kernel_spmd
from concourse.bass_core_rust import IndirectOffsetOnAxis

F32 = mybir.dt.float32
BF16 = mybir.dt.bfloat16
I32 = mybir.dt.int32
U32 = mybir.dt.uint32
AF = mybir.ActivationFunctionType
ALU = mybir.AluOpType
AX = mybir.AxisListType

NCORE = 8
T = 4096
D = 1024
BF = ml_dtypes.bfloat16

ENGINES = ("tensor", "scalar", "vector", "gpsimd", "sync")

class P:

    def __init__(self, nc):
        self.nc = nc
        self.actions = {e: [] for e in ENGINES}
        self.sem = {}
        for e in ENGINES:
            self.sem["E:" + e] = nc.alloc_semaphore("sem_" + e)
        self.cnt = {k: 0 for k in self.sem}
        self.waited = {e: {} for e in ENGINES}
        self.last_w = {}
        self.readers = {}

    def _bsem(self, key, pool="H"):
        key = (pool, str(key))
        if not hasattr(self, "bkey2sem"):
            self.bkey2sem = {}
            self.bpool = {"H": [], "S": [], "C": []}
            self.bn = 0
        if key in self.bkey2sem:
            return self.bkey2sem[key]
        if self.bpool[pool]:
            k = self.bpool[pool].pop()
        else:
            k = "B%s:%d" % (pool, self.bn)
            self.bn += 1
            self.sem[k] = self.nc.alloc_semaphore(k.replace(":", "_"))
            self.cnt[k] = 0
        self.bkey2sem[key] = k
        return k

    def alloc_sem(self, name):
        h = self.nc.alloc_semaphore(name)
        self.sem[name] = h
        self.cnt[name] = 0
        return name

    def _wait(self, eng, semkey, val):
        w = self.waited[eng]
        if w.get(semkey, -1) >= val:
            return
        w[semkey] = val
        h = self.sem[semkey]
        self.actions[eng].append(("w", h, val))

    def op(self, eng, fn, reads=(), writes=(), dma=False, coll=False):
        if dma or coll:
            kname = writes[0] if len(list(writes)) else ("st_" + str(list(reads)[0]) if len(list(reads)) else eng)
            pool = "C" if coll else ("S" if eng == "gpsimd" else "H")
            semkey = self._bsem(kname, pool)
        else:
            semkey = "E:" + eng
        for r in reads:
            t = self.last_w.get(r)
            if t is not None:
                self._wait(eng, t[0], t[1])
            if "ps" in str(r):
                for t2 in self.readers.get(r, ()):
                    self._wait(eng, t2[0], t2[1])
        for wbuf in writes:
            t = self.last_w.get(wbuf)
            if t is not None:
                self._wait(eng, t[0], t[1])
            for t in self.readers.get(wbuf, ()):
                self._wait(eng, t[0], t[1])
        if (dma or coll) and self.cnt[semkey] > 0:
            self._wait(eng, semkey, self.cnt[semkey])
        self.cnt[semkey] += 16 if dma else 1
        tok = (semkey, self.cnt[semkey])
        self.actions[eng].append(("i", fn, self.sem[semkey], 16 if dma else 1))
        for r in reads:
            self.readers.setdefault(r, []).append(tok)
        for wbuf in writes:
            self.last_w[wbuf] = tok
            self.readers[wbuf] = []
        return tok

    def wait_token(self, eng, tok):
        self._wait(eng, tok[0], tok[1])

    def barrier_wait(self, eng, bufs):
        for b in bufs:
            t = self.last_w.get(b)
            if t is not None:
                self._wait(eng, t[0], t[1])

    def fence(self):
        for e in ENGINES:
            for k, h in self.sem.items():
                if k == "E:" + e:
                    continue
                if self.cnt[k] > 0:
                    self._wait(e, k, self.cnt[k])
        if hasattr(self, "bkey2sem"):
            for sk in self.bkey2sem.values():
                self.bpool[sk[1]].append(sk)
            self.bkey2sem.clear()
        self.last_w.clear()
        self.readers.clear()

    def finish(self, eng="vector"):
        for k, h in self.sem.items():
            if k.startswith("B") and ":" in k and self.cnt[k] > 0:
                self.actions[eng].append(("w", h, self.cnt[k]))

    def finalize(self):
        nc = self.nc
        with nc.Block() as block:
            for e in ENGINES:
                acts = self.actions[e]
                if not acts:
                    continue
                def runner(eng, acts=acts):
                    for a in acts:
                        if a[0] == "w":
                            eng.wait_ge(a[1], a[2])
                        else:
                            ins = a[1](eng)
                            ins.then_inc(a[2], a[3])
                getattr(block, e)(runner)


def ceil_div(a, b):
    return (a + b - 1) // b



def emit_identity(p, ident_f32):
    p.op("gpsimd", lambda g: g.memset(ident_f32[:], 0.0), writes=["ident"])
    p.op("gpsimd", lambda g: g.affine_select(
        out=ident_f32[:], in_=ident_f32[:], compare_op=ALU.not_equal, fill=1.0,
        base=0, pattern=[[-1, 128]], channel_multiplier=1), reads=["ident"], writes=["ident"])


def transpose_f32(p, out_sb, psum, in_sb, ident, okey, ikey, out_dtype_note=None):
    p.op("tensor", lambda t: t.transpose(psum[:], in_sb[:], ident[:]),
         reads=[ikey, "ident"], writes=["psT"])
    p.op("vector", lambda v: v.tensor_copy(out_sb[:], psum[:]),
         reads=["psT"], writes=[okey])



def build_pre():
    nc = bass.Bass("TRN2", target_bir_lowering=False, debug=False)
    x = nc.dram_tensor("x", [T, D], F32, kind="ExternalInput").ap()
    xf = nc.dram_tensor("xf", [D, T], F32, kind="ExternalOutput").ap()
    p = P(nc)
    with (
        nc.sbuf_tensor("ident", [128, 128], F32) as ident,
        nc.sbuf_tensor("xt", [128, D], F32) as xt,
        nc.sbuf_tensor("xT", [128, 128], F32) as xT,
        nc.psum_tensor("ps", [128, 128], F32) as ps,
    ):
        emit_identity(p, ident)
        for it in range(T // 128):
            p.op("sync", lambda s, it=it: s.dma_start(out=xt[:], in_=x[128*it:128*(it+1), :]),
                 reads=[], writes=["xt"], dma=True)
            for dc in range(D // 128):
                p.op("tensor", lambda t, dc=dc: t.transpose(ps[:], xt[:, 128*dc:128*(dc+1)], ident[:]),
                     reads=["xt", "ident"], writes=["ps"])
                p.op("vector", lambda v: v.tensor_copy(xT[:], ps[:]), reads=["ps"], writes=["xT"])
                p.op("gpsimd", lambda g, it=it, dc=dc: g.dma_start(
                    out=xf[128*dc:128*(dc+1), 128*it:128*(it+1)], in_=xT[:]),
                    reads=["xT"], writes=[], dma=True)
        p.finish("vector")
    p.finalize()
    return nc


def build_post():
    nc = bass.Bass("TRN2", target_bir_lowering=False, debug=False)
    xf = nc.dram_tensor("xf", [D, T], F32, kind="ExternalInput").ap()
    out = nc.dram_tensor("out", [T, D], F32, kind="ExternalOutput").ap()
    p = P(nc)
    with (
        nc.sbuf_tensor("ident", [128, 128], F32) as ident,
        nc.sbuf_tensor("xt", [128, T], F32) as xt,
        nc.sbuf_tensor("xT", [128, 128], F32) as xT,
        nc.psum_tensor("ps", [128, 128], F32) as ps,
    ):
        emit_identity(p, ident)
        for dc in range(D // 128):
            p.op("sync", lambda s, dc=dc: s.dma_start(out=xt[:], in_=xf[128*dc:128*(dc+1), :]),
                 reads=[], writes=["xt"], dma=True)
            for it in range(T // 128):
                p.op("tensor", lambda t, it=it: t.transpose(ps[:], xt[:, 128*it:128*(it+1)], ident[:]),
                     reads=["xt", "ident"], writes=["ps"])
                p.op("vector", lambda v: v.tensor_copy(xT[:], ps[:]), reads=["ps"], writes=["xT"])
                p.op("gpsimd", lambda g, it=it, dc=dc: g.dma_start(
                    out=out[128*it:128*(it+1), 128*dc:128*(dc+1)], in_=xT[:]),
                    reads=["xT"], writes=[], dma=True)
        p.finish("vector")
    p.finalize()
    return nc



LN_EPS = 1e-5

def alloc_ln_ws(nc, stack, tag="", wide_x32=False):
    ws = {}
    ws["ones_col_f"] = stack.enter_context(nc.sbuf_tensor("lnw_ones" + tag, [128, 1], F32))
    ws["ones1_f"] = stack.enter_context(nc.sbuf_tensor("lnw_ones1" + tag, [1, 128], F32))
    ws["sx"] = stack.enter_context(nc.sbuf_tensor("lnw_sx" + tag, [1, T], F32))
    ws["sxx"] = stack.enter_context(nc.sbuf_tensor("lnw_sxx" + tag, [1, T], F32))
    ws["eps_col"] = stack.enter_context(nc.sbuf_tensor("lnw_eps" + tag, [1, 1], F32))
    ws["msq"] = stack.enter_context(nc.sbuf_tensor("lnw_msq" + tag, [1, 512], F32))
    ws["sq"] = stack.enter_context(nc.sbuf_tensor("lnw_sq" + tag, [128, 512], F32))
    ws["Rb"] = stack.enter_context(nc.sbuf_tensor("lnw_Rb" + tag, [128, 512], F32))
    ws["MRb"] = stack.enter_context(nc.sbuf_tensor("lnw_MRb" + tag, [128, 512], F32))
    ws["x32"] = stack.enter_context(nc.sbuf_tensor("lnw_x32" + tag, [128, (8 * 512) if wide_x32 else 512], F32))
    ws["wide"] = wide_x32
    ws["psrow"] = stack.enter_context(nc.psum_tensor("lnw_psrow" + tag, [1, 512], F32))
    ws["psb"] = stack.enter_context(nc.psum_tensor("lnw_psb" + tag, [128, 512], F32))
    return ws


def emit_ln_setup(p, ws):
    p.op("vector", lambda v: v.memset(ws["ones_col_f"][:], 1.0), writes=["ones"])
    p.op("vector", lambda v: v.memset(ws["ones1_f"][:], 1.0), reads=["ones"], writes=["ones"])
    p.op("vector", lambda v: v.memset(ws["eps_col"][:], LN_EPS), writes=["epsc"])


def emit_ln(p, ws, src, dst, gamma_beta=None, after_tc=None):
    NT = T // 512
    for tc in range(NT):
        tsl = slice(512 * tc, 512 * (tc + 1))
        for dc in range(8):
            ap, key = src(dc, tsl)
            p.op("tensor", lambda t, ap=ap: t.matmul(ws["psrow"][:], ws["ones_col_f"][:], ap,
                                                     start=True, stop=True),
                 reads=[key, "ones"], writes=["psrow"])
            if dc == 0:
                p.op("vector", lambda v, tsl=tsl: v.tensor_copy(ws["sx"][0:1, tsl], ws["psrow"][:]),
                     reads=["psrow"], writes=["sx"])
            else:
                p.op("vector", lambda v, tsl=tsl: v.tensor_add(ws["sx"][0:1, tsl], ws["sx"][0:1, tsl], ws["psrow"][:]),
                     reads=["psrow", "sx"], writes=["sx"])
            p.op("vector", lambda v, ap=ap: v.tensor_mul(ws["sq"][:], ap, ap),
                 reads=[key], writes=["lnsq"])
            p.op("tensor", lambda t: t.matmul(ws["psrow"][:], ws["ones_col_f"][:], ws["sq"][:],
                                              start=True, stop=True),
                 reads=["lnsq", "ones"], writes=["psrow"])
            if dc == 0:
                p.op("vector", lambda v, tsl=tsl: v.tensor_copy(ws["sxx"][0:1, tsl], ws["psrow"][:]),
                     reads=["psrow"], writes=["sxx"])
            else:
                p.op("vector", lambda v, tsl=tsl: v.tensor_add(ws["sxx"][0:1, tsl], ws["sxx"][0:1, tsl], ws["psrow"][:]),
                     reads=["psrow", "sxx"], writes=["sxx"])
    p.op("vector", lambda v: v.tensor_scalar_mul(ws["sx"][0:1, :], ws["sx"][0:1, :], 1.0 / D),
         reads=["sx"], writes=["sx"])
    p.op("vector", lambda v: v.tensor_scalar_mul(ws["sxx"][0:1, :], ws["sxx"][0:1, :], 1.0 / D),
         reads=["sxx"], writes=["sxx"])
    for tc in range(NT):
        tsl = slice(512 * tc, 512 * (tc + 1))
        p.op("vector", lambda v, tsl=tsl: v.tensor_mul(ws["msq"][0:1, :], ws["sx"][0:1, tsl], ws["sx"][0:1, tsl]),
             reads=["sx"], writes=["lnmsq"])
        p.op("vector", lambda v, tsl=tsl: v.tensor_sub(ws["sxx"][0:1, tsl], ws["sxx"][0:1, tsl], ws["msq"][0:1, :]),
             reads=["sxx", "lnmsq"], writes=["sxx"])
    p.op("scalar", lambda s: s.activation(out=ws["sxx"][0:1, :], in_=ws["sxx"][0:1, :],
                                          func=AF.Sqrt, bias=ws["eps_col"][0:1, 0:1], scale=1.0),
         reads=["sxx", "epsc"], writes=["sxx"])
    p.op("vector", lambda v: v.reciprocal(ws["sxx"][0:1, :], ws["sxx"][0:1, :]),
         reads=["sxx"], writes=["sxx"])
    p.op("vector", lambda v: v.tensor_mul(ws["sx"][0:1, :], ws["sx"][0:1, :], ws["sxx"][0:1, :]),
         reads=["sx", "sxx"], writes=["sx"])
    for tc in range(NT):
        tsl = slice(512 * tc, 512 * (tc + 1))
        p.op("tensor", lambda t, tsl=tsl: t.matmul(ws["psb"][:], ws["ones1_f"][:], ws["sxx"][0:1, tsl],
                                                   start=True, stop=True),
             reads=["sxx", "ones"], writes=["lnpsb"])
        p.op("vector", lambda v: v.tensor_copy(ws["Rb"][:], ws["psb"][:]), reads=["lnpsb"], writes=["lnRb"])
        p.op("tensor", lambda t, tsl=tsl: t.matmul(ws["psb"][:], ws["ones1_f"][:], ws["sx"][0:1, tsl],
                                                   start=True, stop=True),
             reads=["sx", "ones"], writes=["lnpsb"])
        p.op("vector", lambda v: v.tensor_copy(ws["MRb"][:], ws["psb"][:]), reads=["lnpsb"], writes=["lnMRb"])
        for dc in range(8):
            ap, key = src(dc, tsl)
            dap, dkey = dst(dc, tsl)
            x32 = ws["x32"][:, 512 * dc:512 * (dc + 1)] if ws["wide"] else ws["x32"][:, 0:512]
            p.op("vector", lambda v, ap=ap, x32=x32: v.tensor_mul(x32, ap, ws["Rb"][:]),
                 reads=[key, "lnRb"], writes=["lnx32"])
            p.op("vector", lambda v, x32=x32: v.tensor_sub(x32, x32, ws["MRb"][:]),
                 reads=["lnx32", "lnMRb"], writes=["lnx32"])
            if gamma_beta is not None:
                g_ap, b_ap, gkey = gamma_beta(dc)
                p.op("vector", lambda v, dap=dap, x32=x32, g_ap=g_ap, b_ap=b_ap: v.tensor_scalar(
                    dap, x32, g_ap, b_ap, ALU.mult, ALU.add),
                    reads=["lnx32", gkey], writes=[dkey])
            else:
                p.op("vector", lambda v, dap=dap, x32=x32: v.tensor_copy(dap, x32),
                     reads=["lnx32"], writes=[dkey])
        if after_tc is not None:
            after_tc(tc)



def build_l1ffn():
    nc = bass.Bass("TRN2", target_bir_lowering=False, debug=False)
    from contextlib import ExitStack
    xf = nc.dram_tensor("xf", [D, T], F32, kind="ExternalInput").ap()
    Wg = nc.dram_tensor("Wg", [D, 8], F32, kind="ExternalInput").ap()
    bg = nc.dram_tensor("bg", [8], F32, kind="ExternalInput").ap()
    W1 = nc.dram_tensor("W1", [8, D, D], F32, kind="ExternalInput").ap()
    b1 = nc.dram_tensor("b1", [8, D], F32, kind="ExternalInput").ap()
    W2 = nc.dram_tensor("W2", [8, D, D], F32, kind="ExternalInput").ap()
    b2 = nc.dram_tensor("b2", [8, D], F32, kind="ExternalInput").ap()
    xfo = nc.dram_tensor("xfo", [D, T], F32, kind="ExternalOutput").ap()
    p = P(nc)
    stack = ExitStack()
    with stack:
        sb = lambda name, shape, dt: stack.enter_context(nc.sbuf_tensor(name, shape, dt))
        psumt = lambda name, shape: stack.enter_context(nc.psum_tensor(name, [shape[0], 512], F32)).ap()[0:shape[0], 0:shape[1]]
        xn = sb("xn", [128, 8 * T], BF16)
        acc = sb("acc", [128, 8 * T], BF16)
        wgt = sb("wgt", [128, 8 * 8], BF16)
        bgr = sb("bgr", [1, 8], BF16)
        b2r = sb("b2r", [8, D], BF16)
        iota8 = sb("iota8", [128, 8], F32)
        logits = sb("logits", [128, 8], F32)
        m1 = sb("m1", [128, 1], F32)
        eq0 = sb("eq0", [128, 8], F32)
        idxc = sb("idxc", [128, 1], F32)
        maskrows = sb("maskrows", [8, T], BF16)
        mrowsT = sb("mrowsT", [8, 128], F32)
        ident = sb("ident", [128, 128], F32)
        ones1_bf = sb("ones1_bf", [1, 128], BF16)
        xbuf = sb("xbuf", [128, 2 * 512], F32)
        ps_log = psumt("ps_log", [128, 8])
        ps_t = psumt("ps_t", [8, 128])

        emit_identity(p, ident)
        p.op("vector", lambda v: v.memset(ones1_bf[:], 1.0), writes=["ones1bf"])
        p.op("gpsimd", lambda g: g.iota(iota8[:], pattern=[[1, 8]], base=0, channel_multiplier=0,
                                        allow_small_or_imprecise_dtypes=True), writes=["iota8"])
        for dc in range(8):
            p.op("gpsimd", lambda g, dc=dc: g.dma_start(out=wgt[:, 8 * dc:8 * (dc + 1)],
                                                        in_=Wg[128 * dc:128 * (dc + 1), :]),
                 writes=["wgt"], dma=True)
        p.op("gpsimd", lambda g: g.dma_start(out=bgr[:], in_=bg.rearrange("(o e) -> o e", o=1)),
             writes=["bgr"], dma=True)
        p.op("gpsimd", lambda g: g.dma_start(out=b2r[:], in_=b2[:, :]), writes=["b2r"], dma=True)

        state = {"i": 0}
        def src(dc, tsl):
            slot = state["i"] % 2
            state["i"] += 1
            key = "xbuf%d" % slot
            ap = xbuf[:, 512 * slot:512 * (slot + 1)]
            p.op("sync", lambda s, dc=dc, tsl=tsl, ap=ap: s.dma_start(
                out=ap, in_=xf[128 * dc:128 * (dc + 1), tsl]), writes=[key], dma=True)
            return ap, key

        def dst(dc, tsl):
            return xn[:, 4096 * dc + tsl.start:4096 * dc + tsl.stop], "xn"

        with ExitStack() as lnstack:
            ws = alloc_ln_ws(nc, lnstack)
            emit_ln_setup(p, ws)
            emit_ln(p, ws, src, dst)
            p.barrier_wait("vector", ["xn"])
            for e in ENGINES:
                p.barrier_wait(e, ["xn", "sx", "sxx", "lnRb", "lnMRb", "lnsq", "lnpsb", "psrow"])

        for it in range(T // 128):
            def mm_logits(t, it=it):
                for dc in range(8):
                    t.matmul(ps_log[:], xn[:, 4096 * dc + 128 * it: 4096 * dc + 128 * (it + 1)],
                             wgt[:, 8 * dc:8 * (dc + 1)], start=(dc == 0), stop=False)
                return t.matmul(ps_log[:], ones1_bf[:], bgr[:], start=False, stop=True)
            p.op("tensor", mm_logits, reads=["xn", "wgt", "bgr", "ones1bf"], writes=["ps_log"])
            p.op("vector", lambda v: v.tensor_copy(logits[:], ps_log[:]), reads=["ps_log"], writes=["logits"])
            p.op("vector", lambda v: v.tensor_reduce(m1[:], logits[:], AX.X, ALU.max), reads=["logits"], writes=["m1"])
            p.op("vector", lambda v: v.tensor_single_scalar(eq0[:], logits[:], m1[:, 0:1], ALU.is_ge),
                 reads=["logits", "m1"], writes=["eq0"])
            p.op("vector", lambda v: v.scalar_tensor_tensor(eq0[:], eq0[:], -1000.0, iota8[:], ALU.mult, ALU.add),
                 reads=["eq0", "iota8"], writes=["eq0"])
            p.op("vector", lambda v: v.tensor_reduce(idxc[:], eq0[:], AX.X, ALU.min),
                 reads=["eq0"], writes=["idxc"])
            p.op("vector", lambda v: v.tensor_scalar_add(idxc[:], idxc[:], 1000.0),
                 reads=["idxc"], writes=["idxc"])
            p.op("vector", lambda v: v.tensor_single_scalar(eq0[:], iota8[:], idxc[:, 0:1], ALU.is_equal),
                 reads=["iota8", "idxc"], writes=["eq0"])
            p.op("tensor", lambda t: t.transpose(ps_t[:], eq0[:], ident[:]), reads=["eq0", "ident"], writes=["ps_t"])
            p.op("vector", lambda v: v.tensor_copy(mrowsT[:], ps_t[:]), reads=["ps_t"], writes=["mrowsT"])
            p.op("vector", lambda v, it=it: v.tensor_copy(maskrows[:, 128 * it:128 * (it + 1)], mrowsT[:]),
                 reads=["mrowsT"], writes=["maskrows"])

        for dc in range(8):
            p.op("vector", lambda v, dc=dc: v.memset(acc[:, 4096 * dc:4096 * (dc + 1)], 0.0), writes=["acc"])

        with ExitStack() as est:
            esb = lambda name, shape, dt: est.enter_context(nc.sbuf_tensor(name, shape, dt))
            w1t = esb("w1t", [128, 8 * D], BF16)
            w2t = esb("w2t", [128, 8 * D], BF16)
            b1c = esb("b1c", [128, 8], F32)
            mrow_e = esb("mrow_e", [1, T], BF16)
            mb = esb("mb", [128, 512], BF16)
            h_e = esb("h_e", [128, 8 * 512], BF16)
            ps_h = psumt("ps_h", [128, 512])
            ps_o = psumt("ps_o", [128, 512])
            ps_m = psumt("ps_m", [128, 512])
            for e in range(8):
                p.op("gpsimd", lambda g, e=e: g.dma_start(
                    out=w1t[:].rearrange("p (dc f) -> p dc f", dc=8),
                    in_=W1[e].rearrange("(dc p) f -> p dc f", p=128)),
                    writes=["w1t"], dma=True)
                p.op("gpsimd", lambda g, e=e: g.dma_start(
                    out=w2t[:].rearrange("p (dc f) -> p dc f", dc=8),
                    in_=W2[e].rearrange("(dc p) f -> p dc f", p=128)),
                    writes=["w2t"], dma=True)
                for ic in range(8):
                    p.op("sync", lambda s, e=e, ic=ic: s.dma_start(
                        out=b1c[:, ic:ic + 1], in_=b1[e, 128 * ic:128 * (ic + 1)].rearrange("(p o) -> p o", o=1)),
                        writes=["b1c"], dma=True)
                p.op("gpsimd", lambda g, e=e: g.dma_start(out=mrow_e[:], in_=maskrows[e:e + 1, :]),
                     reads=["maskrows"], writes=["mrow_e"], dma=True)
                for tc in range(T // 512):
                    tsl = slice(512 * tc, 512 * (tc + 1))
                    p.op("tensor", lambda t, tsl=tsl: t.matmul(ps_m[:], ones1_bf[:], mrow_e[0:1, tsl],
                                                               start=True, stop=True),
                         reads=["mrow_e", "ones1bf"], writes=["ps_m"])
                    p.op("vector", lambda v: v.tensor_copy(mb[:], ps_m[:]), reads=["ps_m"], writes=["mb"])
                    for ic in range(8):
                        def mm_h(t, ic=ic, tsl=tsl):
                            for dc in range(8):
                                ins = t.matmul(ps_h[:], w1t[:, D * dc + 128 * ic: D * dc + 128 * (ic + 1)],
                                               xn[:, 4096 * dc + tsl.start:4096 * dc + tsl.stop],
                                               start=(dc == 0), stop=(dc == 7))
                            return ins
                        p.op("tensor", mm_h, reads=["w1t", "xn"], writes=["ps_h"])
                        p.op("scalar", lambda s, ic=ic: s.activation(
                            out=h_e[:, 512 * ic:512 * (ic + 1)], in_=ps_h[:], func=AF.Gelu,
                            bias=b1c[:, ic:ic + 1], scale=1.0), reads=["ps_h", "b1c"], writes=["h_e"])
                        p.op("vector", lambda v, ic=ic: v.tensor_mul(
                            h_e[:, 512 * ic:512 * (ic + 1)], h_e[:, 512 * ic:512 * (ic + 1)], mb[:]),
                            reads=["h_e", "mb"], writes=["h_e"])
                    for mc in range(8):
                        def mm_o(t, mc=mc):
                            for ic in range(8):
                                ins = t.matmul(ps_o[:], w2t[:, D * ic + 128 * mc: D * ic + 128 * (mc + 1)],
                                               h_e[:, 512 * ic:512 * (ic + 1)], start=(ic == 0), stop=(ic == 7))
                            return ins
                        p.op("tensor", mm_o, reads=["w2t", "h_e"], writes=["ps_o"])
                        p.op("vector", lambda v, mc=mc, tsl=tsl: v.tensor_add(
                            acc[:, 4096 * mc + tsl.start:4096 * mc + tsl.stop],
                            acc[:, 4096 * mc + tsl.start:4096 * mc + tsl.stop], ps_o[:]),
                            reads=["ps_o", "acc"], writes=["acc"])

            xf_out = esb("xf_out", [128, 512], F32)
            ps_b2 = psumt("ps_b2", [128, 512])
            for mc in range(8):
                for tc in range(T // 512):
                    tsl = slice(512 * tc, 512 * (tc + 1))
                    ap, key = src(mc, tsl)
                    p.op("tensor", lambda t, mc=mc, tsl=tsl: t.matmul(
                        ps_b2[:], b2r[:, 128 * mc:128 * (mc + 1)], maskrows[:, tsl],
                        start=True, stop=True), reads=["b2r", "maskrows"], writes=["ps_b2"])
                    p.op("vector", lambda v, mc=mc, tsl=tsl: v.tensor_add(
                        ps_b2[:], ps_b2[:], acc[:, 4096 * mc + tsl.start:4096 * mc + tsl.stop]),
                        reads=["ps_b2", "acc"], writes=["ps_b2"])
                    p.op("vector", lambda v, ap=ap: v.scalar_tensor_tensor(
                        xf_out[:], ps_b2[:], 0.5, ap, ALU.mult, ALU.add),
                        reads=["ps_b2", key], writes=["xf_out"])
                    p.op("sync", lambda s, mc=mc, tsl=tsl: s.dma_start(
                        out=xfo[128 * mc:128 * (mc + 1), tsl], in_=xf_out[:]),
                        reads=["xf_out"], writes=[], dma=True)
            p.finish("vector")
    p.finalize()
    return nc



def mm_group(p, psum_ap, pskey, chunks, reads):
    def fn(t, chunks=chunks):
        ins = None
        n = len(chunks)
        for i, (l, r) in enumerate(chunks):
            ins = t.matmul(psum_ap, l, r, start=(i == 0), stop=(i == n - 1))
        return ins
    return p.op("tensor", fn, reads=list(reads), writes=[pskey])


def emit_col_load(p, colt, ckey, dram_vec, n, eng="sync"):
    for ic in range(n):
        p.op(eng, lambda s, ic=ic: s.dma_start(
            out=colt[:, ic:ic + 1], in_=dram_vec[128 * ic:128 * (ic + 1)].rearrange("(p o) -> p o", o=1)),
            writes=[ckey], dma=True)



def build_l0ffn_inner():
    nc = bass.Bass("TRN2", target_bir_lowering=False, debug=False)
    from contextlib import ExitStack
    xin = nc.dram_tensor("xin", [D, T], F32, kind="ExternalInput").ap()
    xf = nc.dram_tensor("xf", [D, T], F32, kind="ExternalInput").ap()
    g3 = nc.dram_tensor("g3", [D], F32, kind="ExternalInput").ap()
    b3 = nc.dram_tensor("b3", [D], F32, kind="ExternalInput").ap()
    fc1w = nc.dram_tensor("fc1w", [D, 4 * D], F32, kind="ExternalInput").ap()
    fc1b = nc.dram_tensor("fc1b", [4 * D], F32, kind="ExternalInput").ap()
    fc2w = nc.dram_tensor("fc2w", [4 * D, D], F32, kind="ExternalInput").ap()
    fc2b = nc.dram_tensor("fc2b", [D], F32, kind="ExternalInput").ap()
    xfo = nc.dram_tensor("xfo", [D, T], F32, kind="ExternalOutput").ap()
    p = P(nc)
    stack = ExitStack()
    with stack:
        sb = lambda name, shape, dt: stack.enter_context(nc.sbuf_tensor(name, shape, dt))
        psumt = lambda name, shape: stack.enter_context(nc.psum_tensor(name, [shape[0], 512], F32)).ap()[0:shape[0], 0:shape[1]]
        xn = sb("xn", [128, 8 * T], BF16)
        xbuf = sb("xbuf", [128, 2 * 512], F32)
        xinb = sb("xinb", [128, 512], F32)
        gcol = sb("gcol", [128, 8], F32)
        bcol = sb("bcol", [128, 8], F32)
        b1col = sb("b1col", [128, 32], F32)
        b2col = sb("b2col", [128, 8], F32)
        ps_h = psumt("ps_h", [128, 512])
        ps_o = psumt("ps_o", [128, 512])

        emit_col_load(p, gcol, "gb3", g3, 8)
        emit_col_load(p, bcol, "gb3", b3, 8)
        emit_col_load(p, b1col, "b1c", fc1b, 32)
        emit_col_load(p, b2col, "b2c", fc2b, 8)

        state = {"i": 0}
        def src(dc, tsl):
            slot = state["i"] % 2
            state["i"] += 1
            key = "xbuf%d" % slot
            ap = xbuf[:, 512 * slot:512 * (slot + 1)]
            p.op("sync", lambda s, dc=dc, tsl=tsl, ap=ap: s.dma_start(
                out=ap, in_=xin[128 * dc:128 * (dc + 1), tsl]), writes=[key], dma=True)
            return ap, key

        def dst(dc, tsl):
            return xn[:, 4096 * dc + tsl.start:4096 * dc + tsl.stop], "xn"

        def gb(dc):
            return gcol[:, dc:dc + 1], bcol[:, dc:dc + 1], "gb3"

        with ExitStack() as lnstack:
            ws = alloc_ln_ws(nc, lnstack)
            emit_ln_setup(p, ws)
            emit_ln(p, ws, src, dst, gamma_beta=gb)
            for e in ENGINES:
                p.barrier_wait(e, ["xn", "sx", "sxx", "lnRb", "lnMRb", "lnsq", "lnmsq", "lnpsb", "psrow"])

        with ExitStack() as est:
            esb = lambda name, shape, dt: est.enter_context(nc.sbuf_tensor(name, shape, dt))
            h = esb("h", [128, 32 * 512], BF16)
            w1b = esb("w1b", [128, 2 * 8 * 128], BF16)
            w2r = esb("w2r", [128, 32 * 1024], BF16)
            ot = esb("ot", [128, 512], F32)
            for fc in range(32):
                p.op("gpsimd", lambda g, fc=fc: g.dma_start(
                    out=w2r[:, 1024 * fc:1024 * (fc + 1)],
                    in_=fc2w[128 * fc:128 * (fc + 1), :]), writes=["w2r"], dma=True)
            for tc in range(8):
                tsl = slice(512 * tc, 512 * (tc + 1))
                for fc in range(32):
                    slot = fc % 2
                    wkey = "w1b%d" % slot
                    p.op("gpsimd", lambda g, fc=fc, slot=slot: g.dma_start(
                        out=w1b[:, 1024 * slot:1024 * (slot + 1)].rearrange("p (dc f) -> p dc f", dc=8),
                        in_=fc1w[:, 128 * fc:128 * (fc + 1)].rearrange("(dc p) f -> p dc f", p=128)),
                        writes=[wkey], dma=True)
                    mm_group(p, ps_h[:], "ps_h",
                             [(w1b[:, 1024 * slot + 128 * dc:1024 * slot + 128 * (dc + 1)],
                               xn[:, 4096 * dc + tsl.start:4096 * dc + tsl.stop]) for dc in range(8)],
                             [wkey, "xn"])
                    p.op("scalar", lambda s, fc=fc: s.activation(
                        out=h[:, 512 * fc:512 * (fc + 1)], in_=ps_h[:], func=AF.Gelu_apprx_tanh,
                        bias=b1col[:, fc:fc + 1], scale=1.0), reads=["ps_h", "b1c"], writes=["h"])
                for mc in range(8):
                    mm_group(p, ps_o[:], "ps_o",
                             [(w2r[:, 1024 * fc + 128 * mc:1024 * fc + 128 * (mc + 1)],
                               h[:, 512 * fc:512 * (fc + 1)]) for fc in range(32)],
                             ["w2r", "h"])
                    p.op("scalar", lambda s, mc=mc: s.activation(
                        out=ps_o[:], in_=ps_o[:], func=AF.Identity, bias=b2col[:, mc:mc + 1], scale=1.0),
                        reads=["ps_o", "b2c"], writes=["ps_o"])
                    p.op("sync", lambda s, mc=mc, tsl=tsl: s.dma_start(
                        out=xinb[:], in_=xin[128 * mc:128 * (mc + 1), tsl]), writes=["xinb"], dma=True)
                    p.op("vector", lambda v: v.tensor_add(ps_o[:], ps_o[:], xinb[:]),
                         reads=["ps_o", "xinb"], writes=["ps_o"])
                    p.op("sync", lambda s, mc=mc, tsl=tsl: s.dma_start(
                        out=ot[:], in_=xf[128 * mc:128 * (mc + 1), tsl]), writes=["xft"], dma=True)
                    p.op("vector", lambda v: v.tensor_add(ot[:], ot[:], ps_o[:]),
                         reads=["xft", "ps_o"], writes=["xft"])
                    p.op("sync", lambda s, mc=mc, tsl=tsl: s.dma_start(
                        out=xfo[128 * mc:128 * (mc + 1), tsl], in_=ot[:]),
                        reads=["xft"], writes=[], dma=True)
            p.finish("vector")
    p.finalize()
    return nc




DI = 2048

def build_l0mix_a():
    nc = bass.Bass("TRN2", target_bir_lowering=False, debug=False)
    from contextlib import ExitStack
    xin = nc.dram_tensor("xin", [D, T], F32, kind="ExternalInput").ap()
    g2 = nc.dram_tensor("g2", [D], F32, kind="ExternalInput").ap()
    b2 = nc.dram_tensor("b2", [D], F32, kind="ExternalInput").ap()
    in_w = nc.dram_tensor("in_w", [D, 2 * DI], F32, kind="ExternalInput").ap()
    in_b = nc.dram_tensor("in_b", [2 * DI], F32, kind="ExternalInput").ap()
    conv_w = nc.dram_tensor("conv_w", [DI, 4], F32, kind="ExternalInput").ap()
    conv_b = nc.dram_tensor("conv_b", [DI], F32, kind="ExternalInput").ap()
    xp_w = nc.dram_tensor("xp_w", [DI, 32], F32, kind="ExternalInput").ap()
    xp_b = nc.dram_tensor("xp_b", [32], F32, kind="ExternalInput").ap()
    Bm = nc.dram_tensor("Bm", [16, T], F32, kind="ExternalOutput").ap()
    Cm = nc.dram_tensor("Cm", [16, T], F32, kind="ExternalOutput").ap()
    res_d = nc.dram_tensor("res_d", [DI, T], BF16, kind="ExternalOutput").ap()
    p = P(nc)
    stack = ExitStack()
    with stack:
        sb = lambda name, shape, dt: stack.enter_context(nc.sbuf_tensor(name, shape, dt))
        psumt = lambda name, shape: stack.enter_context(nc.psum_tensor(name, [shape[0], 512], F32)).ap()[0:shape[0], 0:shape[1]]
        xn = sb("xn", [128, 8 * T], BF16)
        xbuf = sb("xbuf", [128, 2 * 512], F32)
        gcol = sb("gcol", [128, 8], F32)
        bcol = sb("bcol", [128, 8], F32)
        inbcol = sb("inbcol", [128, 32], F32)
        cwcol = sb("cwcol", [128, 16 * 4], F32)
        cbcol = sb("cbcol", [128, 16], F32)
        xpw = sb("xpw", [128, 16 * 32], F32)
        xpbB = sb("xpbB", [16, 1], F32)
        xpbC = sb("xpbC", [16, 1], F32)
        xm = sb("xm", [128, 16 * 512], F32)
        xm2 = sb("xm2", [128, 16 * 512], F32)
        halo = sb("halo", [128, 16 * 4], F32)
        halo2 = sb("halo2", [128, 16 * 4], F32)
        xc = sb("xc", [128, 512], F32)
        resb = sb("resb", [128, 512], BF16)
        w1b = sb("w1b", [128, 2 * 8 * 128], BF16)
        BmT = sb("BmT", [16, 512], F32)
        CmT = sb("CmT", [16, 512], F32)
        ps_h = psumt("ps_h", [128, 512])
        ps_bc = psumt("ps_bc", [16, 512])

        emit_col_load(p, gcol, "gb2", g2, 8)
        emit_col_load(p, bcol, "gb2", b2, 8)
        emit_col_load(p, inbcol, "inb", in_b[:DI], 16)
        emit_col_load(p, cbcol, "cb", conv_b, 16)
        inbcol2 = sb("inbcol2", [128, 16], F32)
        emit_col_load(p, inbcol2, "inb2", in_b[DI:], 16)
        for cc in range(16):
            p.op("sync", lambda s, cc=cc: s.dma_start(
                out=cwcol[:, 4 * cc:4 * (cc + 1)],
                in_=conv_w[128 * cc:128 * (cc + 1), :]), writes=["cw"], dma=True)
        for cc in range(16):
            p.op("gpsimd", lambda g, cc=cc: g.dma_start(out=xpw[:, 32 * cc:32 * (cc + 1)],
                                                        in_=xp_w[128 * cc:128 * (cc + 1), :]),
                 writes=["xpw"], dma=True)
        p.op("sync", lambda s: s.dma_start(out=xpbB[:], in_=xp_b[0:16].rearrange("(p o) -> p o", o=1)),
             writes=["xpb"], dma=True)
        p.op("sync", lambda s: s.dma_start(out=xpbC[:], in_=xp_b[16:32].rearrange("(p o) -> p o", o=1)),
             writes=["xpb"], dma=True)

        state = {"i": 0}
        def srcf(dc, tsl):
            slot = state["i"] % 2
            state["i"] += 1
            key = "xbuf%d" % slot
            ap = xbuf[:, 512 * slot:512 * (slot + 1)]
            p.op("sync", lambda s, dc=dc, tsl=tsl, ap=ap: s.dma_start(
                out=ap, in_=xin[128 * dc:128 * (dc + 1), tsl]), writes=[key], dma=True)
            return ap, key

        def dstf(dc, tsl):
            return xn[:, 4096 * dc + tsl.start:4096 * dc + tsl.stop], "xn"

        def gb(dc):
            return gcol[:, dc:dc + 1], bcol[:, dc:dc + 1], "gb2"

        with ExitStack() as lnstack:
            ws = alloc_ln_ws(nc, lnstack)
            emit_ln_setup(p, ws)
            emit_ln(p, ws, srcf, dstf, gamma_beta=gb)
            for e in ENGINES:
                p.barrier_wait(e, ["xn", "sx", "sxx", "lnRb", "lnMRb", "lnsq", "lnmsq", "lnpsb", "psrow"])

        p.op("vector", lambda v: v.memset(halo[:], 0.0), writes=["halo"])
        for tc in range(8):
            tsl = slice(512 * tc, 512 * (tc + 1))
            for fc in range(16):
                slot = fc % 2
                wkey = "w1b%d" % slot
                p.op("gpsimd", lambda g, fc=fc, slot=slot: g.dma_start(
                    out=w1b[:, 1024 * slot:1024 * (slot + 1)].rearrange("p (dc f) -> p dc f", dc=8),
                    in_=in_w[:, 128 * fc:128 * (fc + 1)].rearrange("(dc p) f -> p dc f", p=128)),
                    writes=[wkey], dma=True)
                mm_group(p, ps_h[:], "ps_h",
                         [(w1b[:, 1024 * slot + 128 * dc:1024 * slot + 128 * (dc + 1)],
                           xn[:, 4096 * dc + tsl.start:4096 * dc + tsl.stop]) for dc in range(8)],
                         [wkey, "xn"])
                p.op("scalar", lambda s, fc=fc: s.activation(
                    out=xm[:, 512 * fc:512 * (fc + 1)], in_=ps_h[:], func=AF.Identity,
                    bias=inbcol[:, fc:fc + 1], scale=1.0), reads=["ps_h", "inb"], writes=["xm"])
            for fc in range(16):
                slot = fc % 2
                wkey = "w1b%d" % slot
                p.op("gpsimd", lambda g, fc=fc, slot=slot: g.dma_start(
                    out=w1b[:, 1024 * slot:1024 * (slot + 1)].rearrange("p (dc f) -> p dc f", dc=8),
                    in_=in_w[:, DI + 128 * fc:DI + 128 * (fc + 1)].rearrange("(dc p) f -> p dc f", p=128)),
                    writes=[wkey], dma=True)
                mm_group(p, ps_h[:], "ps_h",
                         [(w1b[:, 1024 * slot + 128 * dc:1024 * slot + 128 * (dc + 1)],
                           xn[:, 4096 * dc + tsl.start:4096 * dc + tsl.stop]) for dc in range(8)],
                         [wkey, "xn"])
                p.op("scalar", lambda s, fc=fc: s.activation(
                    out=resb[:], in_=ps_h[:], func=AF.Identity,
                    bias=inbcol2[:, fc:fc + 1], scale=1.0), reads=["ps_h", "inb2"], writes=["resb"])
                p.op("sync", lambda s, fc=fc, tsl=tsl: s.dma_start(
                    out=res_d[128 * fc:128 * (fc + 1), tsl], in_=resb[:]),
                    reads=["resb"], writes=[], dma=True)
            for cc in range(16):
                co = 512 * cc
                p.op("vector", lambda v, cc=cc, co=co: v.tensor_single_scalar(
                    xc[:], xm[:, co:co + 512], cwcol[:, 4 * cc + 3:4 * cc + 4], ALU.mult),
                    reads=["xm", "cw"], writes=["xc"])
                for j in range(3):
                    sh = 3 - j
                    p.op("vector", lambda v, cc=cc, co=co, j=j, sh=sh: v.scalar_tensor_tensor(
                        xc[:, sh:512], xm[:, co:co + 512 - sh], cwcol[:, 4 * cc + j:4 * cc + j + 1],
                        xc[:, sh:512], ALU.mult, ALU.add),
                        reads=["xm", "cw", "xc"], writes=["xc"])
                    p.op("vector", lambda v, cc=cc, j=j, sh=sh: v.scalar_tensor_tensor(
                        xc[:, 0:sh], halo[:, 4 * cc + 3 - sh:4 * cc + 3], cwcol[:, 4 * cc + j:4 * cc + j + 1],
                        xc[:, 0:sh], ALU.mult, ALU.add),
                        reads=["halo", "cw", "xc"], writes=["xc"])
                p.op("vector", lambda v, cc=cc, co=co: v.tensor_copy(
                    halo2[:, 4 * cc:4 * cc + 3], xm[:, co + 509:co + 512]), reads=["xm"], writes=["halo2"])
                p.op("scalar", lambda s, cc=cc, co=co: s.activation(
                    out=xm2[:, co:co + 512], in_=xc[:], func=AF.Silu,
                    bias=cbcol[:, cc:cc + 1], scale=1.0), reads=["xc", "cb"], writes=["xm2"])
            p.op("vector", lambda v: v.tensor_copy(halo[:], halo2[:]), reads=["halo2"], writes=["halo"])
            mm_group(p, ps_bc[:], "ps_bc",
                     [(xpw[:, 32 * cc:32 * cc + 16], xm2[:, 512 * cc:512 * (cc + 1)]) for cc in range(16)],
                     ["xpw", "xm2"])
            p.op("scalar", lambda s: s.activation(out=BmT[:], in_=ps_bc[:], func=AF.Identity,
                                                  bias=xpbB[:, 0:1], scale=1.0),
                 reads=["ps_bc", "xpb"], writes=["BmT"])
            p.op("sync", lambda s, tsl=tsl: s.dma_start(out=Bm[:, tsl], in_=BmT[:]),
                 reads=["BmT"], writes=[], dma=True)
            mm_group(p, ps_bc[:], "ps_bc",
                     [(xpw[:, 32 * cc + 16:32 * cc + 32], xm2[:, 512 * cc:512 * (cc + 1)]) for cc in range(16)],
                     ["xpw", "xm2"])
            p.op("scalar", lambda s: s.activation(out=CmT[:], in_=ps_bc[:], func=AF.Identity,
                                                  bias=xpbC[:, 0:1], scale=1.0),
                 reads=["ps_bc", "xpb"], writes=["CmT"])
            p.op("sync", lambda s, tsl=tsl: s.dma_start(out=Cm[:, tsl], in_=CmT[:]),
                 reads=["CmT"], writes=[], dma=True)
        p.finish("vector")
    p.finalize()
    return nc



def build_l0mix_b():
    nc = bass.Bass("TRN2", target_bir_lowering=False, debug=False)
    from contextlib import ExitStack
    xin = nc.dram_tensor("xin", [D, T], F32, kind="ExternalInput").ap()
    Bm = nc.dram_tensor("Bm", [16, T], F32, kind="ExternalInput").ap()
    Cm = nc.dram_tensor("Cm", [16, T], F32, kind="ExternalInput").ap()
    res_d = nc.dram_tensor("res_d", [DI, T], BF16, kind="ExternalInput").ap()
    dtv = nc.dram_tensor("dtv", [DI], F32, kind="ExternalInput").ap()
    Amat = nc.dram_tensor("Amat", [16, 16], F32, kind="ExternalInput").ap()
    out_w = nc.dram_tensor("out_w", [DI, D], F32, kind="ExternalInput").ap()
    out_b = nc.dram_tensor("out_b", [D], F32, kind="ExternalInput").ap()
    xino = nc.dram_tensor("xino", [D, T], F32, kind="ExternalOutput").ap()
    p = P(nc)
    stack = ExitStack()
    with stack:
        sb = lambda name, shape, dt: stack.enter_context(nc.sbuf_tensor(name, shape, dt))
        psumt = lambda name, shape: stack.enter_context(nc.psum_tensor(name, [shape[0], 512], F32)).ap()[0:shape[0], 0:shape[1]]
        xinb2 = sb("xinb2", [128, 512], F32)
        BmT = sb("BmT", [16, T], F32)
        CmT = sb("CmT", [16, T], F32)
        thbuf = sb("thbuf", [16, T + 1], F32)
        ubuf = sb("ubuf", [16, T + 1], F32)
        bhcbuf = sb("bhcbuf", [16, T + 1], F32)
        hsb = sb("hsb", [16, T], BF16)
        Abank = sb("Abank", [16, 16 * 16], F32)
        AT = sb("AT", [16, 16], F32)
        dtrow = sb("dtrow", [1, 16], F32)
        ecol = sb("ecol", [16, 1], F32)
        ones1_f16 = sb("ones1_f16", [1, 16], F32)
        obcol = sb("obcol", [128, 8], F32)
        ident = sb("ident", [128, 128], F32)
        resb = sb("resb", [128, 16 * 512], BF16)
        owr = sb("owr", [128, 16 * 1024], BF16)
        RTa = sb("RTa", [128, 16], F32)
        Rj = sb("Rj", [16, 8 * 128], BF16)
        onescol = sb("onescol", [128, 1], BF16)
        ot = sb("ot", [128, 512], F32)
        ps = psumt("ps", [16, 16])
        ps2 = psumt("ps2", [16, 1])
        ps_scan = psumt("ps_scan", [16, 2])
        ps_R = psumt("ps_R", [128, 128])
        ps_o = psumt("ps_o", [128, 512])

        emit_identity(p, ident)
        p.op("vector", lambda v: v.memset(ones1_f16[:], 1.0), writes=["ones116"])
        p.op("vector", lambda v: v.memset(onescol[:], 1.0), writes=["onescol"])
        p.op("sync", lambda s: s.dma_start(out=BmT[:], in_=Bm[:]), writes=["BmT"], dma=True)
        p.op("sync", lambda s: s.dma_start(out=CmT[:], in_=Cm[:]), writes=["CmT"], dma=True)
        p.op("sync", lambda s: s.dma_start(out=dtrow[:], in_=dtv[0:16].rearrange("(o e) -> o e", o=1)),
             writes=["dtrow"], dma=True)
        p.op("sync", lambda s: s.dma_start(out=AT[:], in_=Amat[:]), writes=["ATraw"], dma=True)
        emit_col_load(p, obcol, "obc", out_b, 8)
        p.op("scalar", lambda s: s.activation(out=dtrow[:], in_=dtrow[:], func=AF.Exp),
             reads=["dtrow"], writes=["dtrow"])
        p.op("tensor", lambda t: t.transpose(ps[:], AT[:], ident[0:16, 0:16]),
             reads=["ATraw", "ident"], writes=["psA"])
        p.op("vector", lambda v: v.tensor_copy(AT[:], ps[:]), reads=["psA"], writes=["AT"])
        for k in range(16):
            p.op("tensor", lambda t, k=k: t.matmul(ps2[:], ones1_f16[:], dtrow[0:1, k:k + 1],
                                                   start=True, stop=True),
                 reads=["dtrow", "ones116"], writes=["ps2"])
            p.op("vector", lambda v: v.tensor_copy(ecol[:], ps2[:]), reads=["ps2"], writes=["ecol"])
            p.op("scalar", lambda s, k=k: s.activation(out=Abank[:, 16 * k:16 * (k + 1)], in_=AT[:],
                                                       func=AF.Exp, scale=ecol[:, 0:1]),
                 reads=["AT", "ecol"], writes=["Abank"])
        p.op("vector", lambda v: v.memset(thbuf[:, 0:1], 0.0), writes=["thbuf"])
        p.op("vector", lambda v: v.memset(bhcbuf[:, 0:1], 0.0), writes=["bhcbuf"])
        for t in range(1, T + 1):
            k = (t - 1) % 16
            p.op("vector", lambda v, t=t: v.scalar_tensor_tensor(
                ubuf[:, t:t + 1], bhcbuf[:, t - 1:t], BmT[:, t - 1:t], CmT[:, t - 1:t],
                ALU.mult, ALU.add), reads=["bhcbuf", "BmT", "CmT"], writes=["ubuf"])
            p.op("vector", lambda v, t=t: v.scalar_tensor_tensor(
                bhcbuf[:, t:t + 1], thbuf[:, t - 1:t], BmT[:, t - 1:t], ubuf[:, t:t + 1],
                ALU.mult, ALU.add), reads=["thbuf", "BmT", "ubuf"], writes=["bhcbuf"])
            def mmscan(t_eng, t=t, k=k):
                t_eng.matmul(ps_scan[:, 0:1], Abank[:, 16 * k:16 * (k + 1)], thbuf[:, t - 1:t],
                             start=True, stop=False)
                return t_eng.matmul(ps_scan[:, 0:1], Abank[:, 16 * k:16 * (k + 1)], bhcbuf[:, t - 1:t],
                                    start=False, stop=True)
            p.op("tensor", mmscan, reads=["Abank", "thbuf", "bhcbuf"], writes=["ps_scan"])
            p.op("scalar", lambda s, t=t: s.activation(out=thbuf[:, t:t + 1], in_=ps_scan[:, 0:1],
                                                       func=AF.Tanh),
                 reads=["ps_scan"], writes=["thbuf"])
        p.op("vector", lambda v: v.tensor_add(hsb[:], thbuf[:, 1:T + 1], bhcbuf[:, 1:T + 1]),
             reads=["thbuf", "bhcbuf"], writes=["hsb"])
        p.op("gpsimd", lambda g: g.dma_start(
            out=owr[:].rearrange("p (cc f) -> p cc f", cc=16),
            in_=out_w.rearrange("(cc p) f -> p cc f", p=128)), writes=["owr"], dma=True)
        for mc in range(8):
            for j in range(16):
                p.op("tensor", lambda t, mc=mc, j=j: t.matmul(
                    ps_R[:, 0:1], owr[:, 1024 * j + 128 * mc:1024 * j + 128 * (mc + 1)], onescol[:],
                    start=True, stop=True), reads=["owr", "onescol"], writes=["ps_R"])
                p.op("vector", lambda v, j=j: v.tensor_copy(RTa[:, j:j + 1], ps_R[:, 0:1]),
                     reads=["ps_R"], writes=["RTa"])
            p.op("tensor", lambda t: t.transpose(ps_R[0:16, 0:128], RTa[:], ident[:]),
                 reads=["RTa", "ident"], writes=["ps_R"])
            p.op("vector", lambda v, mc=mc: v.tensor_copy(Rj[0:16, 128 * mc:128 * (mc + 1)],
                                                          ps_R[0:16, 0:128]),
                 reads=["ps_R"], writes=["Rj"])
        for tc in range(8):
            tsl = slice(512 * tc, 512 * (tc + 1))
            p.op("sync", lambda s, tsl=tsl: s.dma_start(
                out=resb[:].rearrange("p (cc f) -> p cc f", cc=16),
                in_=res_d[:, tsl].rearrange("(cc p) f -> p cc f", p=128)),
                writes=["resb"], dma=True)
            for mc in range(8):
                def mmo(t, mc=mc, tsl=tsl):
                    for cc in range(16):
                        t.matmul(ps_o[:], owr[:, 1024 * cc + 128 * mc:1024 * cc + 128 * (mc + 1)],
                                 resb[:, 512 * cc:512 * (cc + 1)], start=(cc == 0), stop=False)
                    return t.matmul(ps_o[:], Rj[0:16, 128 * mc:128 * (mc + 1)], hsb[0:16, tsl],
                                    start=False, stop=True)
                p.op("tensor", mmo, reads=["owr", "resb", "Rj", "hsb"], writes=["ps_o"])
                p.op("scalar", lambda s, mc=mc: s.activation(out=ps_o[:], in_=ps_o[:], func=AF.Identity,
                                                             bias=obcol[:, mc:mc + 1], scale=1.0),
                     reads=["ps_o", "obc"], writes=["ps_o"])
                p.op("sync", lambda s, mc=mc, tsl=tsl: s.dma_start(
                    out=xinb2[:], in_=xin[128 * mc:128 * (mc + 1), tsl]), writes=["xinb2"], dma=True)
                p.op("vector", lambda v: v.tensor_add(ot[:], ps_o[:], xinb2[:]),
                    reads=["ps_o", "xinb2"], writes=["ot"])
                p.op("sync", lambda s, mc=mc, tsl=tsl: s.dma_start(
                    out=xino[128 * mc:128 * (mc + 1), tsl], in_=ot[:]),
                    reads=["ot"], writes=[], dma=True)
        p.finish("vector")
    p.finalize()
    return nc



def build_l0qkv():
    nc = bass.Bass("TRN2", target_bir_lowering=False, debug=False)
    from contextlib import ExitStack
    xf = nc.dram_tensor("xf", [D, T], F32, kind="ExternalInput").ap()
    g1 = nc.dram_tensor("g1", [D], F32, kind="ExternalInput").ap()
    b1 = nc.dram_tensor("b1", [D], F32, kind="ExternalInput").ap()
    Wq = nc.dram_tensor("Wq", [D, D], F32, kind="ExternalInput").ap()
    bq = nc.dram_tensor("bq", [D], F32, kind="ExternalInput").ap()
    Wk = nc.dram_tensor("Wk", [D, D], F32, kind="ExternalInput").ap()
    bk = nc.dram_tensor("bk", [D], F32, kind="ExternalInput").ap()
    Wv = nc.dram_tensor("Wv", [D, D], F32, kind="ExternalInput").ap()
    bv = nc.dram_tensor("bv", [D], F32, kind="ExternalInput").ap()
    xin0 = nc.dram_tensor("xin0", [D, T], F32, kind="ExternalOutput").ap()
    Qd = nc.dram_tensor("Qd", [D, T], BF16, kind="ExternalOutput").ap()
    Kd = nc.dram_tensor("Kd", [D, T], BF16, kind="ExternalOutput").ap()
    Vd = nc.dram_tensor("Vd", [T, D], BF16, kind="ExternalOutput").ap()
    p = P(nc)
    stack = ExitStack()
    with stack:
        sb = lambda name, shape, dt: stack.enter_context(nc.sbuf_tensor(name, shape, dt))
        psumt = lambda name, shape: stack.enter_context(nc.psum_tensor(name, [shape[0], 512], F32)).ap()[0:shape[0], 0:shape[1]]
        xn = sb("xn", [128, 8 * T], BF16)
        hh = sb("hh", [128, 8 * T], BF16)
        xbuf = sb("xbuf", [128, 2 * 512], F32)
        gcol = sb("gcol", [128, 8], F32)
        bcol = sb("bcol", [128, 8], F32)
        qcol = sb("qcol", [128, 8], F32)
        kcol = sb("kcol", [128, 8], F32)
        bvrow = sb("bvrow", [1, D], BF16)
        wblk = sb("wblk", [128, 2 * 8 * 128], BF16)
        wvblk = sb("wvblk", [128, 8 * 512], BF16)
        ones1_bf = sb("ones1_bf", [1, 128], BF16)
        ot = sb("ot", [128, 512], BF16)
        ps_q = psumt("ps_q", [128, 512])
        ps_v = psumt("ps_v", [128, 512])

        p.op("vector", lambda v: v.memset(ones1_bf[:], 1.0), writes=["ones1bf"])
        emit_col_load(p, gcol, "g1c", g1, 8)
        emit_col_load(p, bcol, "b1c", b1, 8)
        emit_col_load(p, qcol, "bqc", bq, 8)
        emit_col_load(p, kcol, "bkc", bk, 8)
        p.op("gpsimd", lambda g: g.dma_start(out=bvrow[:], in_=bv.rearrange("(o e) -> o e", o=1)),
             writes=["bvrow"], dma=True)

        state = {"i": 0}
        def srcf(dc, tsl):
            slot = state["i"] % 2
            state["i"] += 1
            key = "xbuf%d" % slot
            ap = xbuf[:, 512 * slot:512 * (slot + 1)]
            p.op("sync", lambda s, dc=dc, tsl=tsl, ap=ap: s.dma_start(
                out=ap, in_=xf[128 * dc:128 * (dc + 1), tsl]), writes=[key], dma=True)
            return ap, key

        def dstf(dc, tsl):
            return xn[:, 4096 * dc + tsl.start:4096 * dc + tsl.stop], "xn"

        with ExitStack() as lnstack:
            ws = alloc_ln_ws(nc, lnstack, wide_x32=True)
            emit_ln_setup(p, ws)
            def store_xin0(tc):
                tsl = slice(512 * tc, 512 * (tc + 1))
                for dc in range(8):
                    p.op("sync", lambda s, dc=dc, tsl=tsl: s.dma_start(
                        out=xin0[128 * dc:128 * (dc + 1), tsl],
                        in_=ws["x32"][:, 512 * dc:512 * (dc + 1)]),
                        reads=["lnx32"], writes=[], dma=True)
            emit_ln(p, ws, srcf, dstf, after_tc=store_xin0)
            for e in ENGINES:
                p.barrier_wait(e, ["xn", "sx", "sxx", "lnRb", "lnMRb", "lnsq", "lnmsq", "lnpsb", "psrow"])
        for dc in range(8):
            p.op("vector", lambda v, dc=dc: v.tensor_scalar(
                hh[:, 4096 * dc:4096 * (dc + 1)], xn[:, 4096 * dc:4096 * (dc + 1)],
                gcol[:, dc:dc + 1], bcol[:, dc:dc + 1], ALU.mult, ALU.add),
                reads=["xn", "g1c", "b1c"], writes=["hh"])
        for W, bc, bkey, outd, name in ((Wq, qcol, "bqc", Qd, "q"), (Wk, kcol, "bkc", Kd, "k")):
            for oc in range(8):
                slot = oc % 2
                wkey = "wblk%d" % slot
                p.op("gpsimd", lambda g, W=W, oc=oc, slot=slot: g.dma_start(
                    out=wblk[:, 1024 * slot:1024 * (slot + 1)].rearrange("p (dc f) -> p dc f", dc=8),
                    in_=W[:, 128 * oc:128 * (oc + 1)].rearrange("(dc p) f -> p dc f", p=128)),
                    writes=[wkey], dma=True)
                for tc in range(8):
                    tsl = slice(512 * tc, 512 * (tc + 1))
                    mm_group(p, ps_q[:], "ps_q",
                             [(wblk[:, 1024 * slot + 128 * dc:1024 * slot + 128 * (dc + 1)],
                               hh[:, 4096 * dc + tsl.start:4096 * dc + tsl.stop]) for dc in range(8)],
                             [wkey, "hh"])
                    p.op("scalar", lambda s, bc=bc, oc=oc: s.activation(
                        out=ot[:], in_=ps_q[:], func=AF.Identity, bias=bc[:, oc:oc + 1], scale=1.0),
                        reads=["ps_q", bkey], writes=["ot"])
                    p.op("sync", lambda s, outd=outd, oc=oc, tsl=tsl: s.dma_start(
                        out=outd[128 * oc:128 * (oc + 1), tsl], in_=ot[:]),
                        reads=["ot"], writes=[], dma=True)
        for vc in range(2):
            p.op("gpsimd", lambda g, vc=vc: g.dma_start(
                out=wvblk[:].rearrange("p (dc f) -> p dc f", dc=8),
                in_=Wv[:, 512 * vc:512 * (vc + 1)].rearrange("(dc p) f -> p dc f", p=128)),
                writes=["wvblk"], dma=True)
            for tt in range(32):
                def mmv(t, vc=vc, tt=tt):
                    for dc in range(8):
                        t.matmul(ps_v[:], hh[:, 4096 * dc + 128 * tt:4096 * dc + 128 * (tt + 1)],
                                 wvblk[:, 512 * dc:512 * (dc + 1)], start=(dc == 0), stop=False)
                    return t.matmul(ps_v[:], ones1_bf[:], bvrow[0:1, 512 * vc:512 * (vc + 1)],
                                    start=False, stop=True)
                p.op("tensor", mmv, reads=["hh", "wvblk", "ones1bf", "bvrow"], writes=["ps_v"])
                p.op("vector", lambda v: v.tensor_copy(ot[:], ps_v[:]), reads=["ps_v"], writes=["ot"])
                p.op("sync", lambda s, vc=vc, tt=tt: s.dma_start(
                    out=Vd[128 * tt:128 * (tt + 1), 512 * vc:512 * (vc + 1)], in_=ot[:]),
                    reads=["ot"], writes=[], dma=True)
        p.finish("vector")
    p.finalize()
    return nc



def build_l0attn():
    nc = bass.Bass("TRN2", target_bir_lowering=False, debug=False)
    from contextlib import ExitStack
    xin0 = nc.dram_tensor("xin0", [D, T], F32, kind="ExternalInput").ap()
    Qd = nc.dram_tensor("Qd", [D, T], BF16, kind="ExternalInput").ap()
    Kd = nc.dram_tensor("Kd", [D, T], BF16, kind="ExternalInput").ap()
    Vd = nc.dram_tensor("Vd", [T, D], BF16, kind="ExternalInput").ap()
    Wo = nc.dram_tensor("Wo", [D, D], F32, kind="ExternalInput").ap()
    bo = nc.dram_tensor("bo", [D], F32, kind="ExternalInput").ap()
    xin1 = nc.dram_tensor("xin1", [D, T], F32, kind="ExternalOutput").ap()
    p = P(nc)
    stack = ExitStack()
    NH = 16
    with stack:
        sb = lambda name, shape, dt: stack.enter_context(nc.sbuf_tensor(name, shape, dt))
        psumt = lambda name, shape: stack.enter_context(nc.psum_tensor(name, [shape[0], 512], F32)).ap()[0:shape[0], 0:shape[1]]
        attnT = sb("attnT", [128, 8 * T], BF16)
        xinb = sb("xinb", [128, 512], F32)
        Qh = [sb("Qh%d" % i, [64, T], BF16) for i in range(2)]
        Kh = [sb("Kh%d" % i, [64, T], BF16) for i in range(2)]
        Vh = [sb("Vh%d" % i, [128, 32 * 64], BF16) for i in range(2)]
        scores = [sb("scores%d" % i, [128, T], F32) for i in range(2)]
        nb = [sb("nb%d" % i, [128, 1], F32) for i in range(2)]
        rsc = [sb("rsc%d" % i, [128, 32], F32) for i in range(2)]
        Et = [sb("Et%d" % i, [128, 512], F32) for i in range(2)]
        Eb = [sb("Eb%d" % i, [128, 512], BF16) for i in range(2)]
        ETb = [sb("ETb%d" % i, [128, 4 * 128], BF16) for i in range(2)]
        masks = sb("masks", [128, 4 * 512], F32)
        rcol = sb("rcol", [128, 1], F32)
        atile = sb("atile", [128, 64], F32)
        ident = sb("ident", [128, 128], F32)
        identb = sb("identb", [128, 128], BF16)
        obcol = sb("obcol", [128, 8], F32)
        woblk = sb("woblk", [128, 2 * 8 * 128], BF16)
        ot = sb("ot", [128, 512], F32)
        ps_s = [psumt("ps_s%d" % i, [128, 512]) for i in range(2)]
        ps_T = psumt("ps_T", [128, 128], )
        ps_av = [psumt("ps_av%d" % i, [128, 64]) for i in range(2)]
        ps_o = psumt("ps_o", [128, 512])

        emit_identity(p, ident)
        p.op("scalar", lambda s: s.activation(out=identb[:], in_=ident[:], func=AF.Identity),
             reads=["ident"], writes=["identb"])
        emit_col_load(p, obcol, "obc", bo, 8)
        for v_ in range(4):
            p.op("gpsimd", lambda g, v_=v_: g.memset(masks[:, 512 * v_:512 * (v_ + 1)], 0.0),
                 writes=["masks"])
            p.op("gpsimd", lambda g, v_=v_: g.affine_select(
                out=masks[:, 512 * v_:512 * (v_ + 1)], in_=masks[:, 512 * v_:512 * (v_ + 1)],
                compare_op=ALU.is_ge, fill=-1e30, base=128 * v_,
                pattern=[[-1, 512]], channel_multiplier=1),
                reads=["masks"], writes=["masks"])

        def load_head(h):
            hp = h % 2
            p.op("sync", lambda s, h=h, hp=hp: s.dma_start(out=Qh[hp][:], in_=Qd[64 * h:64 * (h + 1), :]),
                 writes=["Qh%d" % hp], dma=True)
            p.op("sync", lambda s, h=h, hp=hp: s.dma_start(out=Kh[hp][:], in_=Kd[64 * h:64 * (h + 1), :]),
                 writes=["Kh%d" % hp], dma=True)
            p.op("sync", lambda s, h=h, hp=hp: s.dma_start(
                out=Vh[hp][:].rearrange("p (a v) -> p a v", a=32),
                in_=Vd[:, 64 * h:64 * (h + 1)].rearrange("(a p) v -> p a v", p=128)),
                writes=["Vh%d" % hp], dma=True)

        def stageA(h, qt):
            hp, qp = h % 2, qt % 2
            nkc = (128 * qt) // 512 + 1
            kd = nkc - 1
            voff = qt % 4
            for kc in range(nkc):
                cp = kc % 2
                p.op("tensor", lambda t, hp=hp, qt=qt, kc=kc, cp=cp: t.matmul(
                    ps_s[cp][:], Qh[hp][:, 128 * qt:128 * (qt + 1)], Kh[hp][:, 512 * kc:512 * (kc + 1)],
                    start=True, stop=True), reads=["Qh%d" % hp, "Kh%d" % hp], writes=["ps_s%d" % cp])
                if kc == kd:
                    p.op("vector", lambda v, cp=cp, voff=voff: v.tensor_add(
                        ps_s[cp][:], ps_s[cp][:], masks[:, 512 * voff:512 * (voff + 1)]),
                        reads=["ps_s%d" % cp, "masks"], writes=["ps_s%d" % cp])
                p.op("scalar", lambda s, qp=qp, kc=kc, cp=cp: s.activation(
                    out=scores[qp][:, 512 * kc:512 * (kc + 1)], in_=ps_s[cp][:], func=AF.Identity),
                    reads=["ps_s%d" % cp], writes=["scores%d" % qp])
            p.op("vector", lambda v, qp=qp, nkc=nkc: v.tensor_reduce(
                nb[qp][:], scores[qp][:, 0:512 * nkc], AX.X, ALU.max),
                reads=["scores%d" % qp], writes=["nb%d" % qp])
            p.op("vector", lambda v, qp=qp: v.tensor_scalar_mul(nb[qp][:], nb[qp][:], -0.125),
                 reads=["nb%d" % qp], writes=["nb%d" % qp])

        def stageB(h, qt):
            hp, qp = h % 2, qt % 2
            nkc = (128 * qt) // 512 + 1
            kd = nkc - 1
            for kc in range(nkc):
                cp = kc % 2
                p.op("scalar", lambda s, qp=qp, kc=kc, cp=cp: s.activation(
                    out=Et[cp][:], in_=scores[qp][:, 512 * kc:512 * (kc + 1)], func=AF.Exp,
                    bias=nb[qp][:, 0:1], scale=0.125, accum_out=rsc[qp][:, kc:kc + 1]),
                    reads=["scores%d" % qp, "nb%d" % qp], writes=["Et%d" % cp, "rsc%d" % qp])
                nsub = 4 if kc < kd else ((128 * (qt + 1) - 512 * kd + 127) // 128)
                for sj in range(nsub):
                    p.op("tensor", lambda t, cp=cp, sj=sj: t.transpose(
                        ps_T[:], Et[cp][:, 128 * sj:128 * (sj + 1)], ident[:]),
                        reads=["Et%d" % cp, "ident"], writes=["ps_T"])
                    p.op("vector", lambda v, cp=cp, sj=sj: v.tensor_copy(
                        ETb[cp][:, 128 * sj:128 * (sj + 1)], ps_T[:]),
                        reads=["ps_T"], writes=["ETb%d" % cp])
                def mmpv(t, hp=hp, qp=qp, cp=cp, kc=kc, nsub=nsub, first=(kc == 0), last=(kc == nkc - 1)):
                    ins = None
                    for sj in range(nsub):
                        ins = t.matmul(ps_av[qp][:], ETb[cp][:, 128 * sj:128 * (sj + 1)],
                                       Vh[hp][:, (4 * kc + sj) * 64:(4 * kc + sj + 1) * 64],
                                       start=(first and sj == 0), stop=(last and sj == nsub - 1))
                    return ins
                p.op("tensor", mmpv, reads=["ETb%d" % cp, "Vh%d" % hp], writes=["ps_av%d" % qp])
            p.op("vector", lambda v, qp=qp, nkc=nkc: v.tensor_reduce(rcol[:], rsc[qp][:, 0:nkc], AX.X, ALU.add),
                 reads=["rsc%d" % qp], writes=["rcol"])
            p.op("vector", lambda v: v.reciprocal(rcol[:], rcol[:]), reads=["rcol"], writes=["rcol"])
            p.op("scalar", lambda s, qp=qp: s.activation(out=atile[:], in_=ps_av[qp][:], func=AF.Identity,
                                                         scale=rcol[:, 0:1]),
                 reads=["ps_av%d" % qp, "rcol"], writes=["atile"])
            p.op("tensor", lambda t: t.transpose(ps_T[0:64, 0:128], atile[:], ident[:]),
                 reads=["atile", "ident"], writes=["ps_T"])
            ro = 64 * (h % 2)
            cbase = 4096 * (h // 2) + 128 * qt
            if ro == 0:
                p.op("vector", lambda v, cbase=cbase: v.tensor_copy(
                    attnT[0:64, cbase:cbase + 128], ps_T[0:64, 0:128]),
                    reads=["ps_T"], writes=["attnT"])
            else:
                p.op("vector", lambda v, cbase=cbase: v.tensor_copy(
                    attnT[64:128, cbase:cbase + 128], ps_T[0:64, 0:128]),
                    reads=["ps_T"], writes=["attnT"])

        import os
        _mode = os.environ.get("ATTN_MODE", "skew")
        load_head(0)
        prev = None
        for h in range(NH):
            if _mode != "skew" and h + 1 < NH:
                load_head(h + 1)
            for qt in range(32):
                stageA(h, qt)
                if _mode == "noskew":
                    stageB(h, qt)
                else:
                    if prev is not None:
                        stageB(*prev)
                    prev = (h, qt)
                if _mode == "skew" and qt == 1 and h + 1 < NH:
                    load_head(h + 1)
            if _mode == "flush" and prev is not None:
                stageB(*prev)
                prev = None
        if _mode == "skew" and prev is not None:
            stageB(*prev)

        for mc in range(8):
            slot = mc % 2
            wkey = "woblk%d" % slot
            p.op("gpsimd", lambda g, mc=mc, slot=slot: g.dma_start(
                out=woblk[:, 1024 * slot:1024 * (slot + 1)].rearrange("p (dc f) -> p dc f", dc=8),
                in_=Wo[:, 128 * mc:128 * (mc + 1)].rearrange("(dc p) f -> p dc f", p=128)),
                writes=[wkey], dma=True)
            for tc in range(8):
                tsl = slice(512 * tc, 512 * (tc + 1))
                mm_group(p, ps_o[:], "ps_o",
                         [(woblk[:, 1024 * slot + 128 * dc:1024 * slot + 128 * (dc + 1)],
                           attnT[:, 4096 * dc + tsl.start:4096 * dc + tsl.stop]) for dc in range(8)],
                         [wkey, "attnT"])
                p.op("scalar", lambda s, mc=mc: s.activation(out=ps_o[:], in_=ps_o[:], func=AF.Identity,
                                                             bias=obcol[:, mc:mc + 1], scale=1.0),
                     reads=["ps_o", "obc"], writes=["ps_o"])
                p.op("sync", lambda s, mc=mc, tsl=tsl: s.dma_start(
                    out=xinb[:], in_=xin0[128 * mc:128 * (mc + 1), tsl]), writes=["xinb"], dma=True)
                p.op("vector", lambda v: v.tensor_add(ot[:], ps_o[:], xinb[:]),
                    reads=["ps_o", "xinb"], writes=["ot"])
                p.op("sync", lambda s, mc=mc, tsl=tsl: s.dma_start(
                    out=xin1[128 * mc:128 * (mc + 1), tsl], in_=ot[:]),
                    reads=["ot"], writes=[], dma=True)
        p.finish("vector")
    p.finalize()
    return nc



def emit_weightT_block(p, nc, wT_dst, dkey, W_dram, r0, c0, tmpf, psT, ident, nr=128, ncol=128):
    p.op("sync", lambda s: s.dma_start(out=tmpf[0:nr, 0:ncol], in_=W_dram[r0:r0 + nr, c0:c0 + ncol]),
         writes=["wtmpf"], dma=True)
    p.op("tensor", lambda t: t.transpose(psT[0:ncol, 0:nr], tmpf[0:nr, 0:ncol], ident[:]),
         reads=["wtmpf", "ident"], writes=["psWT"])
    p.op("vector", lambda v: v.tensor_copy(wT_dst, psT[0:ncol, 0:nr]), reads=["psWT"], writes=[dkey])


M1_SKIP_DT = False
M1_SKIP_PRET = False

def build_l1mix_a():
    nc = bass.Bass("TRN2", target_bir_lowering=False, debug=False)
    from contextlib import ExitStack
    xf = nc.dram_tensor("xf", [D, T], F32, kind="ExternalInput").ap()
    W_in = nc.dram_tensor("W_in", [2 * DI, D], F32, kind="ExternalInput").ap()
    conv_w = nc.dram_tensor("conv_w", [DI, 4], F32, kind="ExternalInput").ap()
    conv_b = nc.dram_tensor("conv_b", [DI], F32, kind="ExternalInput").ap()
    W_x = nc.dram_tensor("W_x", [32, DI], F32, kind="ExternalInput").ap()
    W_dt = nc.dram_tensor("W_dt", [DI, DI], F32, kind="ExternalInput").ap()
    b_dt = nc.dram_tensor("b_dt", [DI], F32, kind="ExternalInput").ap()
    u_d = nc.dram_tensor("u_d", [DI, T], BF16, kind="ExternalOutput").ap()
    z_d = nc.dram_tensor("z_d", [DI, T], BF16, kind="ExternalOutput").ap()
    dt_d = nc.dram_tensor("dt_d", [DI, T], F32, kind="ExternalOutput").ap()
    Bm_d = nc.dram_tensor("Bm_d", [16, T], F32, kind="ExternalOutput").ap()
    Cm_d = nc.dram_tensor("Cm_d", [16, T], F32, kind="ExternalOutput").ap()
    Win_bf = nc.dram_tensor("Win_bf", [2 * DI, D], BF16, kind="Internal").ap()
    Wdt_bf = nc.dram_tensor("Wdt_bf", [DI, DI], BF16, kind="Internal").ap()
    p = P(nc)
    stack = ExitStack()
    with stack:
        sb = lambda name, shape, dt: stack.enter_context(nc.sbuf_tensor(name, shape, dt))
        psumt = lambda name, shape: stack.enter_context(nc.psum_tensor(name, [shape[0], 512], F32)).ap()[0:shape[0], 0:shape[1]]
        xn = sb("xn", [128, 8 * T], BF16)
        utc = sb("utc", [128, 16 * 512], BF16)
        xbuf = sb("xbuf", [128, 2 * 512], F32)
        cwcol = sb("cwcol", [128, 16 * 4], F32)
        cbcol = sb("cbcol", [128, 16], F32)
        bdtcol = sb("bdtcol", [128, 16], F32)
        onescol1 = sb("onescol1", [128, 1], F32)
        wpan = sb("wpan", [128, 2 * 8 * 512], BF16)
        wdtp = sb("wdtp", [128, 2 * 16 * 512], BF16)
        wxTb = sb("wxTb", [128, 16 * 32], BF16)
        wxtmp = sb("wxtmp", [32, DI // 2], F32)
        xm = sb("xm", [128, 512], F32)
        xc = sb("xc", [128, 512], F32)
        halo = sb("halo", [128, 16 * 4], F32)
        halo2 = sb("halo2", [128, 16 * 4], F32)
        sp1 = sb("sp1", [128, 512], F32)
        sp2 = sb("sp2", [128, 512], F32)
        dtt = sb("dtt", [128, 512], F32)
        ztile = sb("ztile", [128, 512], BF16)
        bcT = sb("bcT", [16, 512], F32)
        ident = sb("ident", [128, 128], F32)
        ps_h = psumt("ps_h", [128, 512])
        ps_bc = psumt("ps_bc", [16, 512])
        ps_T = psumt("ps_T", [128, 128])

        emit_identity(p, ident)
        p.op("vector", lambda v: v.memset(onescol1[:], 1.0), writes=["ones11"])
        for cc in range(16):
            p.op("sync", lambda s, cc=cc: s.dma_start(
                out=cwcol[:, 4 * cc:4 * (cc + 1)], in_=conv_w[128 * cc:128 * (cc + 1), :]),
                writes=["cw"], dma=True)
        emit_col_load(p, cbcol, "cb", conv_b, 16)
        emit_col_load(p, bdtcol, "bdt", b_dt, 16)
        for hh in range(2):
            p.op("sync", lambda s, hh=hh: s.dma_start(out=wxtmp[:], in_=W_x[:, 1024 * hh:1024 * (hh + 1)]),
                 writes=["wxtmp"], dma=True)
            for c8 in range(8):
                cc = 8 * hh + c8
                p.op("tensor", lambda t, c8=c8: t.transpose(ps_T[:, 0:32], wxtmp[:, 128 * c8:128 * (c8 + 1)],
                                                            ident[0:32, 0:32]),
                     reads=["wxtmp", "ident"], writes=["psWT"])
                p.op("vector", lambda v, cc=cc: v.tensor_copy(wxTb[:, 32 * cc:32 * (cc + 1)], ps_T[:, 0:32]),
                     reads=["psWT"], writes=["wxTb"])
        for rb in range(4):
            p.op("gpsimd", lambda g, rb=rb: g.dma_start(
                out=Win_bf[1024 * rb:1024 * (rb + 1), :], in_=W_in[1024 * rb:1024 * (rb + 1), :]),
                writes=["Win_bf"], dma=True)
            p.op("gpsimd", lambda g, rb=rb: g.dma_start(
                out=Wdt_bf[512 * rb:512 * (rb + 1), :], in_=W_dt[512 * rb:512 * (rb + 1), :]),
                writes=["Wdt_bf"], dma=True)

        state = {"i": 0}
        def srcf(dc, tsl):
            slot = state["i"] % 2
            state["i"] += 1
            key = "xbuf%d" % slot
            ap = xbuf[:, 512 * slot:512 * (slot + 1)]
            p.op("sync", lambda s, dc=dc, tsl=tsl, ap=ap: s.dma_start(
                out=ap, in_=xf[128 * dc:128 * (dc + 1), tsl]), writes=[key], dma=True)
            return ap, key

        def dstf(dc, tsl):
            return xn[:, 4096 * dc + tsl.start:4096 * dc + tsl.stop], "xn"

        with ExitStack() as lnstack:
            ws = alloc_ln_ws(nc, lnstack)
            emit_ln_setup(p, ws)
            emit_ln(p, ws, srcf, dstf)
            for e in ENGINES:
                p.barrier_wait(e, ["xn", "sx", "sxx", "lnRb", "lnMRb", "lnsq", "lnmsq", "lnpsb", "psrow"])

        p.op("vector", lambda v: v.memset(halo[:], 0.0), writes=["halo"])
        for tc in range(8):
            tsl = slice(512 * tc, 512 * (tc + 1))
            for fc in range(32):
                fcg, sub = fc // 4, fc % 4
                slot = fcg % 2
                wkey = "wpan%d" % slot
                if sub == 0:
                    for dc in range(8):
                        p.op("sync", lambda s, fcg=fcg, dc=dc, slot=slot: s.dma_start(
                            out=wpan[:, 4096 * slot + 512 * dc:4096 * slot + 512 * (dc + 1)],
                            in_=Win_bf[512 * fcg:512 * (fcg + 1), 128 * dc:128 * (dc + 1)],
                            transpose=True),
                            reads=["Win_bf"], writes=[wkey], dma=True)
                mm_group(p, ps_h[:], "ps_h",
                         [(wpan[:, 4096 * slot + 512 * dc + 128 * sub:4096 * slot + 512 * dc + 128 * (sub + 1)],
                           xn[:, 4096 * dc + tsl.start:4096 * dc + tsl.stop]) for dc in range(8)],
                         [wkey, "xn"])
                if fc >= 16:
                    cc = fc - 16
                    p.op("vector", lambda v: v.tensor_copy(ztile[:], ps_h[:]), reads=["ps_h"], writes=["ztile"])
                    p.op("sync", lambda s, cc=cc, tsl=tsl: s.dma_start(
                        out=z_d[128 * cc:128 * (cc + 1), tsl], in_=ztile[:]),
                        reads=["ztile"], writes=[], dma=True)
                else:
                    cc = fc
                    p.op("vector", lambda v: v.tensor_copy(xm[:], ps_h[:]), reads=["ps_h"], writes=["xm"])
                    p.op("vector", lambda v, cc=cc: v.tensor_single_scalar(
                        xc[:], xm[:], cwcol[:, 4 * cc + 3:4 * cc + 4], ALU.mult),
                        reads=["xm", "cw"], writes=["xc"])
                    for j in range(3):
                        sh = 3 - j
                        p.op("vector", lambda v, cc=cc, j=j, sh=sh: v.scalar_tensor_tensor(
                            xc[:, sh:512], xm[:, 0:512 - sh], cwcol[:, 4 * cc + j:4 * cc + j + 1],
                            xc[:, sh:512], ALU.mult, ALU.add),
                            reads=["xm", "cw", "xc"], writes=["xc"])
                        p.op("vector", lambda v, cc=cc, j=j, sh=sh: v.scalar_tensor_tensor(
                            xc[:, 0:sh], halo[:, 4 * cc + 3 - sh:4 * cc + 3], cwcol[:, 4 * cc + j:4 * cc + j + 1],
                            xc[:, 0:sh], ALU.mult, ALU.add),
                            reads=["halo", "cw", "xc"], writes=["xc"])
                    p.op("vector", lambda v, cc=cc: v.tensor_copy(
                        halo2[:, 4 * cc:4 * cc + 3], xm[:, 509:512]), reads=["xm"], writes=["halo2"])
                    p.op("scalar", lambda s, cc=cc: s.activation(
                        out=utc[:, 512 * cc:512 * (cc + 1)], in_=xc[:], func=AF.Silu,
                        bias=cbcol[:, cc:cc + 1], scale=1.0), reads=["xc", "cb"], writes=["utc"])
                    p.op("sync", lambda s, cc=cc, tsl=tsl: s.dma_start(
                        out=u_d[128 * cc:128 * (cc + 1), tsl], in_=utc[:, 512 * cc:512 * (cc + 1)]),
                        reads=["utc"], writes=[], dma=True)
            p.op("vector", lambda v: v.tensor_copy(halo[:], halo2[:]), reads=["halo2"], writes=["halo"])
            mm_group(p, ps_bc[:], "ps_bc",
                     [(wxTb[:, 32 * cc:32 * cc + 16], utc[:, 512 * cc:512 * (cc + 1)]) for cc in range(16)],
                     ["wxTb", "utc"])
            p.op("vector", lambda v: v.tensor_copy(bcT[:], ps_bc[:]), reads=["ps_bc"], writes=["bcT"])
            p.op("sync", lambda s, tsl=tsl: s.dma_start(out=Bm_d[:, tsl], in_=bcT[:]),
                 reads=["bcT"], writes=[], dma=True)
            mm_group(p, ps_bc[:], "ps_bc",
                     [(wxTb[:, 32 * cc + 16:32 * cc + 32], utc[:, 512 * cc:512 * (cc + 1)]) for cc in range(16)],
                     ["wxTb", "utc"])
            p.op("vector", lambda v: v.tensor_copy(bcT[:], ps_bc[:]), reads=["ps_bc"], writes=["bcT"])
            p.op("sync", lambda s, tsl=tsl: s.dma_start(out=Cm_d[:, tsl], in_=bcT[:]),
                 reads=["bcT"], writes=[], dma=True)
            for c2 in range(16):
                c2g, sub2 = c2 // 4, c2 % 4
                slot = c2g % 2
                wkey = "wdtp%d" % slot
                if sub2 == 0:
                    for cc in range(16):
                        p.op("sync", lambda s, c2g=c2g, cc=cc, slot=slot: s.dma_start(
                            out=wdtp[:, 8192 * slot + 512 * cc:8192 * slot + 512 * (cc + 1)],
                            in_=Wdt_bf[512 * c2g:512 * (c2g + 1), 128 * cc:128 * (cc + 1)],
                            transpose=True),
                            reads=["Wdt_bf"], writes=[wkey], dma=True)
                mm_group(p, ps_h[:], "ps_h",
                         [(wdtp[:, 8192 * slot + 512 * cc + 128 * sub2:8192 * slot + 512 * cc + 128 * (sub2 + 1)],
                           utc[:, 512 * cc:512 * (cc + 1)]) for cc in range(16)],
                         [wkey, "utc"])
                p.op("scalar", lambda s, c2=c2: s.activation(out=sp1[:], in_=ps_h[:], func=AF.Abs,
                                                             bias=bdtcol[:, c2:c2 + 1], scale=1.0),
                     reads=["ps_h", "bdt"], writes=["sp1"])
                p.op("scalar", lambda s: s.activation(out=sp2[:], in_=sp1[:], func=AF.Exp, scale=-1.0),
                     reads=["sp1"], writes=["sp2"])
                p.op("scalar", lambda s: s.activation(out=sp2[:], in_=sp2[:], func=AF.Ln,
                                                      bias=onescol1[:, 0:1], scale=1.0),
                     reads=["sp2", "ones11"], writes=["sp2"])
                p.op("vector", lambda v, c2=c2: v.tensor_scalar(dtt[:], ps_h[:], bdtcol[:, c2:c2 + 1],
                                                                0.0, ALU.add, ALU.max),
                     reads=["ps_h", "bdt"], writes=["dtt"])
                p.op("vector", lambda v: v.tensor_add(dtt[:], dtt[:], sp2[:]),
                     reads=["dtt", "sp2"], writes=["dtt"])
                p.op("sync", lambda s, c2=c2, tsl=tsl: s.dma_start(
                    out=dt_d[128 * c2:128 * (c2 + 1), tsl], in_=dtt[:]),
                    reads=["dtt"], writes=[], dma=True)
        p.finish("vector")
    p.finalize()
    return nc



def build_l1mix_b():
    nc = bass.Bass("TRN2", target_bir_lowering=False, debug=False)
    from contextlib import ExitStack
    xf = nc.dram_tensor("xf", [D, T], F32, kind="ExternalInput").ap()
    u_d = nc.dram_tensor("u_d", [DI, T], BF16, kind="ExternalInput").ap()
    z_d = nc.dram_tensor("z_d", [DI, T], BF16, kind="ExternalInput").ap()
    dt_d = nc.dram_tensor("dt_d", [DI, T], F32, kind="ExternalInput").ap()
    Bm_d = nc.dram_tensor("Bm_d", [16, T], F32, kind="ExternalInput").ap()
    Cm_d = nc.dram_tensor("Cm_d", [16, T], F32, kind="ExternalInput").ap()
    A_log = nc.dram_tensor("A_log", [DI, 16], F32, kind="ExternalInput").ap()
    Dp = nc.dram_tensor("Dp", [DI], F32, kind="ExternalInput").ap()
    W_out = nc.dram_tensor("W_out", [D, DI], F32, kind="ExternalInput").ap()
    xfo = nc.dram_tensor("xfo", [D, T], F32, kind="ExternalOutput").ap()
    p = P(nc)
    stack = ExitStack()
    TCH = 1024
    with stack:
        sb = lambda name, shape, dt: stack.enter_context(nc.sbuf_tensor(name, shape, dt))
        psumt = lambda name, shape: stack.enter_context(nc.psum_tensor(name, [shape[0], 512], F32)).ap()[0:shape[0], 0:shape[1]]
        Bb = sb("Bb", [128, 16 * TCH], BF16)
        Cb = sb("Cb", [128, 16 * TCH], BF16)
        ygb = sb("ygb", [128, 16 * TCH], BF16)
        woT = sb("woT", [128, 16 * 8 * 128], BF16)
        Aneg = sb("Aneg", [128, 16 * 16], F32)
        Dcol = sb("Dcol", [128, 16], F32)
        s_state = sb("s_state", [128, 16 * 16], F32)
        y_acc = sb("y_acc", [128, TCH], F32)
        dtc = sb("dtc", [128, TCH], F32)
        uc = sb("uc", [128, TCH], BF16)
        zc = sb("zc", [128, TCH], BF16)
        dtu = sb("dtu", [128, TCH], BF16)
        dA = sb("dA", [128, TCH], BF16)
        dBu = sb("dBu", [128, TCH], BF16)
        s_sc = sb("s_sc", [128, TCH], BF16)
        rowf = sb("rowf", [1, TCH], F32)
        wtmpf = sb("wtmpf", [128, 128], F32)
        ones1_f = sb("ones1_f", [1, 128], F32)
        ident = sb("ident", [128, 128], F32)
        ot = sb("ot", [128, 512], F32)
        ps_b = psumt("ps_b", [128, 512])
        ps_T = psumt("ps_T", [128, 128])
        ps_del = [psumt("ps_del%d" % i, [128, 512]) for i in range(4)]

        emit_identity(p, ident)
        p.op("vector", lambda v: v.memset(ones1_f[:], 1.0), writes=["ones1f"])
        emit_col_load(p, Dcol, "Dc", Dp, 16)
        for cc in range(16):
            p.op("sync", lambda s, cc=cc: s.dma_start(out=Aneg[:, 16 * cc:16 * (cc + 1)],
                                                      in_=A_log[128 * cc:128 * (cc + 1), :]),
                 writes=["Aneg"], dma=True)
        p.op("scalar", lambda s: s.activation(out=Aneg[:], in_=Aneg[:], func=AF.Exp, scale=1.0),
             reads=["Aneg"], writes=["Aneg"])
        p.op("vector", lambda v: v.tensor_scalar_mul(Aneg[:], Aneg[:], -1.0), reads=["Aneg"], writes=["Aneg"])
        Wout_bf = nc.dram_tensor("Wout_bf", [D, DI], BF16, kind="Internal").ap()
        for rb in range(2):
            p.op("gpsimd", lambda g, rb=rb: g.dma_start(
                out=Wout_bf[512 * rb:512 * (rb + 1), :], in_=W_out[512 * rb:512 * (rb + 1), :]),
                writes=["Wout_bf"], dma=True)
        for cc in range(16):
            for mg in range(2):
                p.op("sync", lambda s, cc=cc, mg=mg: s.dma_start(
                    out=woT[:, 1024 * cc + 512 * mg:1024 * cc + 512 * (mg + 1)],
                    in_=Wout_bf[512 * mg:512 * (mg + 1), 128 * cc:128 * (cc + 1)], transpose=True),
                    reads=["Wout_bf"], writes=["woT"], dma=True)
        p.op("vector", lambda v: v.memset(s_state[:], 0.0), writes=["s_state"])

        for tci in range(T // TCH):
            tsl = slice(TCH * tci, TCH * (tci + 1))
            for n in range(16):
                for half in range(2):
                    hs = slice(512 * half, 512 * (half + 1))
                    gsl = slice(tsl.start + 512 * half, tsl.start + 512 * (half + 1))
                    p.op("sync", lambda s, n=n, gsl=gsl, hs=hs: s.dma_start(
                        out=rowf[0:1, hs], in_=Bm_d[n:n + 1, gsl]), writes=["rowf"], dma=True)
                    p.op("tensor", lambda t, hs=hs: t.matmul(ps_b[:], ones1_f[:], rowf[0:1, hs],
                                                             start=True, stop=True),
                         reads=["rowf", "ones1f"], writes=["ps_b"])
                    p.op("vector", lambda v, n=n, hs=hs: v.tensor_copy(Bb[:, TCH * n + hs.start:TCH * n + hs.stop], ps_b[:]),
                         reads=["ps_b"], writes=["Bb"])
                    p.op("sync", lambda s, n=n, gsl=gsl, hs=hs: s.dma_start(
                        out=rowf[0:1, hs], in_=Cm_d[n:n + 1, gsl]), writes=["rowf"], dma=True)
                    p.op("tensor", lambda t, hs=hs: t.matmul(ps_b[:], ones1_f[:], rowf[0:1, hs],
                                                             start=True, stop=True),
                         reads=["rowf", "ones1f"], writes=["ps_b"])
                    p.op("vector", lambda v, n=n, hs=hs: v.tensor_copy(Cb[:, TCH * n + hs.start:TCH * n + hs.stop], ps_b[:]),
                         reads=["ps_b"], writes=["Cb"])
            for cc in range(16):
                p.op("sync", lambda s, cc=cc, tsl=tsl: s.dma_start(
                    out=dtc[:], in_=dt_d[128 * cc:128 * (cc + 1), tsl]), writes=["dtc"], dma=True)
                p.op("sync", lambda s, cc=cc, tsl=tsl: s.dma_start(
                    out=uc[:], in_=u_d[128 * cc:128 * (cc + 1), tsl]), writes=["uc"], dma=True)
                p.op("sync", lambda s, cc=cc, tsl=tsl: s.dma_start(
                    out=zc[:], in_=z_d[128 * cc:128 * (cc + 1), tsl]), writes=["zc"], dma=True)
                p.op("vector", lambda v: v.tensor_mul(dtu[:], dtc[:], uc[:]), reads=["dtc", "uc"], writes=["dtu"])
                p.op("vector", lambda v, cc=cc: v.tensor_single_scalar(y_acc[:], uc[:], Dcol[:, cc:cc + 1],
                                                                       ALU.mult),
                     reads=["uc", "Dc"], writes=["y_acc"])
                for n in range(16):
                    p.op("scalar", lambda s, cc=cc, n=n: s.activation(
                        out=dA[:], in_=dtc[:], func=AF.Exp, scale=Aneg[:, 16 * cc + n:16 * cc + n + 1]),
                        reads=["dtc", "Aneg"], writes=["dA"])
                    p.op("vector", lambda v, n=n: v.tensor_mul(dBu[:], dtu[:], Bb[:, TCH * n:TCH * (n + 1)]),
                         reads=["dtu", "Bb"], writes=["dBu"])
                    p.op("vector", lambda v, cc=cc, n=n: v.tensor_tensor_scan(
                        out=s_sc[:], data0=dA[:], data1=dBu[:],
                        initial=s_state[:, 16 * cc + n:16 * cc + n + 1],
                        op0=ALU.mult, op1=ALU.add),
                        reads=["dA", "dBu", "s_state"], writes=["s_sc"])
                    p.op("vector", lambda v, cc=cc, n=n: v.tensor_copy(
                        s_state[:, 16 * cc + n:16 * cc + n + 1], s_sc[:, TCH - 1:TCH]),
                        reads=["s_sc"], writes=["s_state"])
                    p.op("vector", lambda v, n=n: v.tensor_mul(dBu[:], s_sc[:], Cb[:, TCH * n:TCH * (n + 1)]),
                         reads=["s_sc", "Cb"], writes=["dBu"])
                    p.op("vector", lambda v: v.tensor_add(y_acc[:], y_acc[:], dBu[:]),
                         reads=["y_acc", "dBu"], writes=["y_acc"])
                p.op("scalar", lambda s: s.activation(out=zc[:], in_=zc[:], func=AF.Silu),
                     reads=["zc"], writes=["zc"])
                p.op("vector", lambda v, cc=cc: v.tensor_mul(ygb[:, TCH * cc:TCH * (cc + 1)], y_acc[:], zc[:]),
                     reads=["y_acc", "zc"], writes=["ygb"])
            for tv in range(2):
                vs = slice(512 * tv, 512 * (tv + 1))
                for mc in range(8):
                    pd = ps_del[mc % 4]
                    mm_group(p, pd[:], "ps_del%d" % (mc % 4),
                             [(woT[:, 1024 * cc + 128 * mc:1024 * cc + 128 * (mc + 1)],
                               ygb[:, TCH * cc + vs.start:TCH * cc + vs.stop]) for cc in range(16)],
                             ["woT", "ygb"])
                    gsl = slice(tsl.start + vs.start, tsl.start + vs.stop)
                    p.op("sync", lambda s, mc=mc, gsl=gsl: s.dma_start(
                        out=ot[:], in_=xf[128 * mc:128 * (mc + 1), gsl]), writes=["xft"], dma=True)
                    p.op("vector", lambda v, pd=pd: v.tensor_add(ot[:], ot[:], pd[:]),
                         reads=["xft", "ps_del%d" % (mc % 4)], writes=["xft"])
                    p.op("sync", lambda s, mc=mc, gsl=gsl: s.dma_start(
                        out=xfo[128 * mc:128 * (mc + 1), gsl], in_=ot[:]),
                        reads=["xft"], writes=[], dma=True)
        p.finish("vector")
    p.finalize()
    return nc




def build_moe_gate():
    nc = bass.Bass("TRN2", target_bir_lowering=False, debug=False)
    from contextlib import ExitStack
    xf = nc.dram_tensor("xf", [D, T], F32, kind="ExternalInput").ap()
    w_gate = nc.dram_tensor("w_gate", [D, 8], F32, kind="ExternalInput").ap()
    xn_tm = nc.dram_tensor("xn_tm", [T, D], BF16, kind="ExternalOutput").ap()
    e1_d = nc.dram_tensor("e1_d", [128, 32], F32, kind="ExternalOutput").ap()
    e2_d = nc.dram_tensor("e2_d", [128, 32], F32, kind="ExternalOutput").ap()
    w1_d = nc.dram_tensor("w1_d", [128, 32], F32, kind="ExternalOutput").ap()
    w2_d = nc.dram_tensor("w2_d", [128, 32], F32, kind="ExternalOutput").ap()
    pl1_d = nc.dram_tensor("pl1_d", [128, 32], F32, kind="ExternalOutput").ap()
    pl2_d = nc.dram_tensor("pl2_d", [128, 32], F32, kind="ExternalOutput").ap()
    tot_d = nc.dram_tensor("tot_d", [1, 8], F32, kind="ExternalOutput").ap()
    p = P(nc)
    stack = ExitStack()
    with stack:
        sb = lambda name, shape, dt: stack.enter_context(nc.sbuf_tensor(name, shape, dt))
        psumt = lambda name, shape: stack.enter_context(nc.psum_tensor(name, [shape[0], 512], F32)).ap()[0:shape[0], 0:shape[1]]
        xn = sb("xn", [128, 8 * T], BF16)
        xbuf = sb("xbuf", [128, 2 * 512], F32)
        wgt = sb("wgt", [128, 8 * 8], F32)
        iota8 = sb("iota8", [128, 8], F32)
        logits_all = sb("logits_all", [128, 32 * 8], F32)
        logits = sb("logits", [128, 8], F32)
        max8 = sb("max8", [128, 8], F32)
        nm1 = sb("nm1", [128, 1], F32)
        sume = sb("sume", [128, 1], F32)
        etmp = sb("etmp", [128, 8], F32)
        l2 = sb("l2", [128, 8], F32)
        eq0 = sb("eq0", [128, 8], F32)
        col1 = sb("col1", [128, 1], F32)
        ohsum = sb("ohsum", [128, 8], BF16)
        e1c = sb("e1c", [128, 32], F32)
        e2c = sb("e2c", [128, 32], F32)
        w1c = sb("w1c", [128, 32], F32)
        w2c = sb("w2c", [128, 32], F32)
        pl1c = sb("pl1c", [128, 32], F32)
        pl2c = sb("pl2c", [128, 32], F32)
        oh1t = sb("oh1t", [128, 8], F32)
        oh2t = sb("oh2t", [128, 8], F32)
        cumt = sb("cumt", [128, 8], F32)
        Ustrict = sb("Ustrict", [128, 128], BF16)
        U32s = sb("U32s", [32, 32], BF16)
        onesc = sb("onesc", [128, 1], BF16)
        ones1fr = sb("ones1fr", [1, 128], F32)
        totrows = sb("totrows", [32, 8], F32)
        totrowsb = sb("totrowsb", [32, 8], BF16)
        prefr = sb("prefr", [32, 8], F32)
        prow = sb("prow", [1, 8], F32)
        pbc = sb("pbc", [128, 8], F32)
        tmpf = sb("tmpf", [128, 128], F32)
        tmpb = sb("tmpb", [128, 128], BF16)
        ident = sb("ident", [128, 128], F32)
        ps8 = psumt("ps8", [128, 8])
        psb = psumt("psb_g", [128, 128])
        ps1 = psumt("ps1", [1, 8])

        emit_identity(p, ident)
        p.op("vector", lambda v: v.memset(onesc[:], 1.0), writes=["onesc"])
        p.op("vector", lambda v: v.memset(ones1fr[:], 1.0), writes=["ones1fr"])
        p.op("gpsimd", lambda g: g.iota(iota8[:], pattern=[[1, 8]], base=0, channel_multiplier=0,
                                        allow_small_or_imprecise_dtypes=True), writes=["iota8"])
        p.op("gpsimd", lambda g: g.memset(tmpf[:], 0.0), writes=["tmpf"])
        p.op("gpsimd", lambda g: g.affine_select(out=tmpf[:], in_=tmpf[:], compare_op=ALU.is_ge,
                                                 fill=1.0, base=0, pattern=[[-1, 128]],
                                                 channel_multiplier=1),
             reads=["tmpf"], writes=["tmpf"])
        p.op("scalar", lambda s: s.activation(out=Ustrict[:], in_=tmpf[:], func=AF.Identity),
             reads=["tmpf"], writes=["Ustrict"])
        p.op("gpsimd", lambda g: g.memset(tmpf[:, 0:32], 0.0), writes=["tmpf"])
        p.op("gpsimd", lambda g: g.affine_select(out=tmpf[0:32, 0:32], in_=tmpf[0:32, 0:32],
                                                 compare_op=ALU.is_ge, fill=1.0, base=0,
                                                 pattern=[[-1, 32]], channel_multiplier=1),
             reads=["tmpf"], writes=["tmpf"])
        p.op("scalar", lambda s: s.activation(out=U32s[:], in_=tmpf[0:32, 0:32], func=AF.Identity),
             reads=["tmpf"], writes=["U32s"])
        for dc in range(8):
            p.op("sync", lambda s, dc=dc: s.dma_start(out=wgt[:, 8 * dc:8 * (dc + 1)],
                                                      in_=w_gate[128 * dc:128 * (dc + 1), :]),
                 writes=["wgt"], dma=True)

        state = {"i": 0}
        def srcf(dc, tsl):
            slot = state["i"] % 2
            state["i"] += 1
            key = "xbuf%d" % slot
            ap = xbuf[:, 512 * slot:512 * (slot + 1)]
            p.op("sync", lambda s, dc=dc, tsl=tsl, ap=ap: s.dma_start(
                out=ap, in_=xf[128 * dc:128 * (dc + 1), tsl]), writes=[key], dma=True)
            return ap, key

        def dstf(dc, tsl):
            return xn[:, 4096 * dc + tsl.start:4096 * dc + tsl.stop], "xn"

        with ExitStack() as lnstack:
            ws = alloc_ln_ws(nc, lnstack, wide_x32=True)
            emit_ln_setup(p, ws)
            def gate_logits(tc):
                for sub in range(4):
                    it = 4 * tc + sub
                    def mmg(t, sub=sub):
                        ins = None
                        for dc in range(8):
                            ins = t.matmul(ps8[:], ws["x32"][:, 512 * dc + 128 * sub:512 * dc + 128 * (sub + 1)],
                                           wgt[:, 8 * dc:8 * (dc + 1)], start=(dc == 0), stop=(dc == 7))
                        return ins
                    p.op("tensor", mmg, reads=["lnx32", "wgt"], writes=["ps8"])
                    p.op("vector", lambda v, it=it: v.tensor_copy(logits_all[:, 8 * it:8 * (it + 1)], ps8[:]),
                         reads=["ps8"], writes=["logits_all"])
            emit_ln(p, ws, srcf, dstf, after_tc=gate_logits)
            for e in ENGINES:
                p.barrier_wait(e, ["xn", "sx", "sxx", "lnRb", "lnMRb", "lnsq", "lnmsq", "lnpsb", "psrow"])

        for it in range(32):
            for dc in range(8):
                p.op("scalar", lambda s, it=it, dc=dc: s.activation(
                    out=tmpf[:], in_=xn[:, 4096 * dc + 128 * it:4096 * dc + 128 * (it + 1)],
                    func=AF.Identity), reads=["xn"], writes=["tmpf"])
                p.op("tensor", lambda t: t.transpose(psb[:], tmpf[:], ident[:]),
                     reads=["tmpf", "ident"], writes=["psb_g"])
                p.op("vector", lambda v: v.tensor_copy(tmpb[:], psb[:]), reads=["psb_g"], writes=["tmpb"])
                p.op("gpsimd", lambda g, it=it, dc=dc: g.dma_start(
                    out=xn_tm[128 * it:128 * (it + 1), 128 * dc:128 * (dc + 1)], in_=tmpb[:]),
                    reads=["tmpb"], writes=[], dma=True)

        for it in range(32):
            p.op("vector", lambda v, it=it: v.tensor_copy(logits[:], logits_all[:, 8 * it:8 * (it + 1)]),
                 reads=["logits_all"], writes=["logits"])
            p.op("vector", lambda v: v.max(max8[:], logits[:]), reads=["logits"], writes=["max8"])
            p.op("vector", lambda v: v.tensor_scalar_mul(nm1[:], max8[:, 0:1], -1.0),
                 reads=["max8"], writes=["nm1"])
            p.op("scalar", lambda s: s.activation(out=etmp[:], in_=logits[:], func=AF.Exp,
                                                  bias=nm1[:, 0:1], scale=1.0, accum_out=sume[:]),
                 reads=["logits", "nm1"], writes=["etmp", "sume"])
            p.op("vector", lambda v: v.reciprocal(col1[:], sume[:]), reads=["sume"], writes=["col1"])
            p.op("vector", lambda v, it=it: v.tensor_copy(w1c[:, it:it + 1], col1[:]),
                 reads=["col1"], writes=["w1c"])
            p.op("scalar", lambda s: s.activation(out=sume[:], in_=max8[:, 1:2], func=AF.Exp,
                                                  bias=nm1[:, 0:1], scale=1.0),
                 reads=["max8", "nm1"], writes=["sume"])
            p.op("vector", lambda v, it=it: v.tensor_mul(w2c[:, it:it + 1], sume[:], col1[:]),
                 reads=["sume", "col1"], writes=["w2c"])
            p.op("vector", lambda v: v.tensor_single_scalar(eq0[:], logits[:], max8[:, 0:1], ALU.is_ge),
                 reads=["logits", "max8"], writes=["eq0"])
            p.op("vector", lambda v: v.scalar_tensor_tensor(eq0[:], eq0[:], -1000.0, iota8[:],
                                                            ALU.mult, ALU.add),
                 reads=["eq0", "iota8"], writes=["eq0"])
            p.op("vector", lambda v: v.tensor_reduce(col1[:], eq0[:], AX.X, ALU.min),
                 reads=["eq0"], writes=["col1"])
            p.op("vector", lambda v, it=it: v.tensor_scalar_add(e1c[:, it:it + 1], col1[:], 1000.0),
                 reads=["col1"], writes=["e1c"])
            p.op("vector", lambda v: v.tensor_copy(etmp[:], max8[:]), reads=["max8"], writes=["etmp"])
            p.op("vector", lambda v: v.memset(etmp[:, 1:8], -1e30), reads=["etmp"], writes=["etmp"])
            p.op("vector", lambda v: v.match_replace(out=l2[:], in_to_replace=etmp[:],
                                                     in_values=logits[:], imm_value=-1e30),
                 reads=["etmp", "logits"], writes=["l2"])
            p.op("vector", lambda v: v.tensor_single_scalar(eq0[:], l2[:], max8[:, 1:2], ALU.is_ge),
                 reads=["l2", "max8"], writes=["eq0"])
            p.op("vector", lambda v: v.scalar_tensor_tensor(eq0[:], eq0[:], -1000.0, iota8[:],
                                                            ALU.mult, ALU.add),
                 reads=["eq0", "iota8"], writes=["eq0"])
            p.op("vector", lambda v: v.tensor_reduce(col1[:], eq0[:], AX.X, ALU.min),
                 reads=["eq0"], writes=["col1"])
            p.op("vector", lambda v, it=it: v.tensor_scalar_add(e2c[:, it:it + 1], col1[:], 1000.0),
                 reads=["col1"], writes=["e2c"])
            p.op("vector", lambda v, it=it: v.tensor_single_scalar(oh1t[:], iota8[:], e1c[:, it:it + 1],
                                                                   ALU.is_equal),
                 reads=["iota8", "e1c"], writes=["oh1t"])
            p.op("vector", lambda v, it=it: v.tensor_single_scalar(oh2t[:], iota8[:], e2c[:, it:it + 1],
                                                                   ALU.is_equal),
                 reads=["iota8", "e2c"], writes=["oh2t"])
            p.op("vector", lambda v: v.tensor_add(etmp[:], oh1t[:], oh2t[:]),
                 reads=["oh1t", "oh2t"], writes=["etmp"])
            p.op("vector", lambda v: v.tensor_copy(ohsum[:], etmp[:]), reads=["etmp"], writes=["ohsum"])
            p.op("tensor", lambda t: t.matmul(ps8[:], Ustrict[:], ohsum[:], start=True, stop=True),
                 reads=["Ustrict", "ohsum"], writes=["ps8"])
            p.op("vector", lambda v: v.tensor_copy(cumt[:], ps8[:]), reads=["ps8"], writes=["cumt"])
            p.op("tensor", lambda t: t.matmul(ps1[:], onesc[:], ohsum[:], start=True, stop=True),
                 reads=["onesc", "ohsum"], writes=["ps1"])
            p.op("vector", lambda v: v.tensor_copy(prow[:], ps1[:]), reads=["ps1"], writes=["prow"])
            p.op("gpsimd", lambda g, it=it: g.dma_start(out=totrows[it:it + 1, :], in_=prow[0:1, :]),
                 reads=["prow"], writes=["totrows"], dma=True)
            p.op("vector", lambda v: v.tensor_mul(oh1t[:], oh1t[:], cumt[:]), reads=["oh1t", "cumt"], writes=["oh1t"])
            p.op("vector", lambda v, it=it: v.tensor_reduce(pl1c[:, it:it + 1], oh1t[:], AX.X, ALU.add),
                 reads=["oh1t"], writes=["pl1c"])
            p.op("vector", lambda v: v.tensor_mul(oh2t[:], oh2t[:], cumt[:]), reads=["oh2t", "cumt"], writes=["oh2t"])
            p.op("vector", lambda v, it=it: v.tensor_reduce(pl2c[:, it:it + 1], oh2t[:], AX.X, ALU.add),
                 reads=["oh2t"], writes=["pl2c"])
        p.op("vector", lambda v: v.tensor_copy(totrowsb[:], totrows[:]), reads=["totrows"], writes=["totrowsb"])
        p.op("tensor", lambda t: t.matmul(ps8[0:32, 0:8], U32s[:], totrowsb[:], start=True, stop=True),
             reads=["U32s", "totrowsb"], writes=["ps8"])
        p.op("vector", lambda v: v.tensor_copy(prefr[:], ps8[0:32, 0:8]), reads=["ps8"], writes=["prefr"])
        p.op("tensor", lambda t: t.matmul(ps1[:], onesc[0:32, 0:1], totrowsb[:], start=True, stop=True),
             reads=["onesc", "totrowsb"], writes=["ps1"])
        p.op("vector", lambda v: v.tensor_copy(prow[:], ps1[:]), reads=["ps1"], writes=["prow"])
        p.op("sync", lambda s: s.dma_start(out=tot_d[:], in_=prow[0:1, :]), reads=["prow"], writes=[], dma=True)
        for it in range(32):
            p.op("gpsimd", lambda g, it=it: g.dma_start(out=prow[0:1, :], in_=prefr[it:it + 1, :]),
                 reads=["prefr"], writes=["prow"], dma=True)
            p.op("tensor", lambda t: t.matmul(ps8[:], ones1fr[:], prow[0:1, :],
                                              start=True, stop=True),
                 reads=["prow", "ones1fr"], writes=["ps8"])
            p.op("vector", lambda v: v.tensor_copy(pbc[:], ps8[:]), reads=["ps8"], writes=["pbc"])
            for (ec, plc, ekey, plkey) in ((e1c, pl1c, "e1c", "pl1c"), (e2c, pl2c, "e2c", "pl2c")):
                p.op("vector", lambda v, it=it, ec=ec: v.tensor_single_scalar(
                    oh1t[:], iota8[:], ec[:, it:it + 1], ALU.is_equal),
                    reads=["iota8", ekey], writes=["oh1t"])
                p.op("vector", lambda v: v.tensor_mul(oh1t[:], oh1t[:], pbc[:]),
                     reads=["oh1t", "pbc"], writes=["oh1t"])
                p.op("vector", lambda v: v.tensor_reduce(col1[:], oh1t[:], AX.X, ALU.add),
                     reads=["oh1t"], writes=["col1"])
                p.op("vector", lambda v, it=it, plc=plc: v.tensor_add(
                    plc[:, it:it + 1], plc[:, it:it + 1], col1[:]),
                    reads=["col1", plkey], writes=[plkey])
        for (dram, tile, key) in ((e1_d, e1c, "e1c"), (e2_d, e2c, "e2c"), (w1_d, w1c, "w1c"),
                                  (w2_d, w2c, "w2c"), (pl1_d, pl1c, "pl1c"), (pl2_d, pl2c, "pl2c")):
            p.op("sync", lambda s, dram=dram, tile=tile: s.dma_start(out=dram[:], in_=tile[:]),
                 reads=[key], writes=[], dma=True)
        p.finish("vector")
    p.finalize()
    return nc




BIGIDX = float(1 << 20)

def build_moe_route():
    nc = bass.Bass("TRN2", target_bir_lowering=False, debug=False)
    from contextlib import ExitStack
    all_e1 = nc.dram_tensor("all_e1", [8, 128, 32], F32, kind="ExternalInput").ap()
    all_e2 = nc.dram_tensor("all_e2", [8, 128, 32], F32, kind="ExternalInput").ap()
    all_pl1 = nc.dram_tensor("all_pl1", [8, 128, 32], F32, kind="ExternalInput").ap()
    all_pl2 = nc.dram_tensor("all_pl2", [8, 128, 32], F32, kind="ExternalInput").ap()
    tot_all = nc.dram_tensor("tot_all", [8, 8], F32, kind="ExternalInput").ap()
    own_w1 = nc.dram_tensor("own_w1", [128, 32], F32, kind="ExternalInput").ap()
    own_w2 = nc.dram_tensor("own_w2", [128, 32], F32, kind="ExternalInput").ap()
    bsel = nc.dram_tensor("bsel", [8, 1], F32, kind="ExternalInput").ap()
    eid = nc.dram_tensor("eid", [1, 1], F32, kind="ExternalInput").ap()
    slot_tbl = nc.dram_tensor("slot_tbl", [4224, 1], I32, kind="ExternalOutput").ap()
    vw1_d = nc.dram_tensor("vw1_d", [128, 32], F32, kind="ExternalOutput").ap()
    vw2_d = nc.dram_tensor("vw2_d", [128, 32], F32, kind="ExternalOutput").ap()
    gidx1_d = nc.dram_tensor("gidx1_d", [128, 32], I32, kind="ExternalOutput").ap()
    gidx2_d = nc.dram_tensor("gidx2_d", [128, 32], I32, kind="ExternalOutput").ap()
    p = P(nc)
    pfx = ""
    stack = ExitStack()
    with stack:
        sb = lambda name, shape, dt: stack.enter_context(nc.sbuf_tensor(name, shape, dt))
        psumt = lambda name, shape: stack.enter_context(nc.psum_tensor(name, [shape[0], 512], F32)).ap()[0:shape[0], 0:shape[1]]
        e1b = sb("e1b", [128, 32], F32)
        e2b = sb("e2b", [128, 32], F32)
        pl1b = sb("pl1b", [128, 32], F32)
        pl2b = sb("pl2b", [128, 32], F32)
        w1b_ = sb("w1b_", [128, 32], F32)
        w2b_ = sb("w2b_", [128, 32], F32)
        totb = sb("totb", [8, 8], F32)
        totbb = sb("totbb", [8, 8], F32)
        U8s = sb("U8s", [8, 8], F32)
        base = sb("base", [8, 8], F32)
        bselc = sb("bselc", [8, 1], F32)
        eidc = sb("eidc", [128, 1], F32)
        eidr = sb("eidr", [1, 1], F32)
        brow = sb("brow", [1, 8], F32)
        bbc = sb("bbc", [128, 8], F32)
        obrow = sb("obrow", [1, 8], F32)
        obbc = sb("obbc", [128, 8], F32)
        iota8 = sb("iota8", [128, 8], F32)
        iotac = sb("iotac", [128, 1], F32)
        oh = sb("oh", [128, 8], F32)
        gp1 = sb("gp1", [128, 1], F32)
        gp2 = sb("gp2", [128, 1], F32)
        colA = sb("colA", [128, 1], F32)
        colB = sb("colB", [128, 1], F32)
        bigc = sb("bigc", [128, 1], F32)
        dumpc = sb("dumpc", [128, 1], F32)
        maskb = sb("maskb", [128, 32], F32)
        idxf0 = sb("idxf0", [128, 32], F32)
        idxf1 = sb("idxf1", [128, 32], F32)
        idxw0 = sb("idxw0", [128, 32], I32)
        idxw1 = sb("idxw1", [128, 32], I32)
        valf = sb("valf", [128, 32], F32)
        dumpb = sb("dumpb", [128, 32], F32)
        prefc = sb("prefc", [128, 33], F32)
        mt = [sb("mt%d" % k, [128, 33], F32) for k in range(4)]
        mti = sb("mti", [128, 33], I32)
        tmpf = sb("tmpf", [128, 32], F32)
        vw1sb = sb("vw1sb", [128, 32], F32)
        vw2sb = sb("vw2sb", [128, 32], F32)
        gx1sb = sb("gx1sb", [128, 32], I32)
        gx2sb = sb("gx2sb", [128, 32], I32)
        ones1fr = sb("ones1fr", [1, 128], F32)
        ps8 = psumt("ps8", [128, 8])
        ps1 = psumt("ps1", [1, 8])
        ps88 = psumt("ps88", [8, 8])

        p.op("vector", lambda v: v.memset(ones1fr[:], 1.0), writes=["ones1fr"])
        p.op("vector", lambda v: v.memset(bigc[:], BIGIDX), writes=["bigc"])
        p.op("gpsimd", lambda g: g.iota(iota8[:], pattern=[[1, 8]], base=0, channel_multiplier=0,
                                        allow_small_or_imprecise_dtypes=True), writes=["iota8"])
        p.op("sync", lambda s: s.dma_start(out=totb[:], in_=tot_all[:]), writes=["totb"], dma=True)
        p.op("sync", lambda s: s.dma_start(out=bselc[:], in_=bsel[:]), writes=["bselc"], dma=True)
        p.op("sync", lambda s: s.dma_start(out=eidr[:], in_=eid[:]), writes=["eidr"], dma=True)
        p.op("sync", lambda s: s.dma_start(out=w1b_[:], in_=own_w1[:]), writes=["w1b_"], dma=True)
        p.op("sync", lambda s: s.dma_start(out=w2b_[:], in_=own_w2[:]), writes=["w2b_"], dma=True)
        p.op("gpsimd", lambda g: g.memset(tmpf[:, 0:8], 0.0), writes=["tmpf"])
        p.op("gpsimd", lambda g: g.affine_select(out=tmpf[0:8, 0:8], in_=tmpf[0:8, 0:8],
                                                 compare_op=ALU.is_ge, fill=1.0, base=0,
                                                 pattern=[[-1, 8]], channel_multiplier=1),
             reads=["tmpf"], writes=["tmpf"])
        p.op("vector", lambda v: v.tensor_copy(U8s[:], tmpf[0:8, 0:8]), reads=["tmpf"], writes=["U8s"])
        p.op("vector", lambda v: v.tensor_copy(totbb[:], totb[:]), reads=["totb"], writes=["totbb"])
        p.op("tensor", lambda t: t.matmul(ps88[:], U8s[:], totbb[:], start=True, stop=True),
             reads=["U8s", "totbb"], writes=["ps88"])
        p.op("vector", lambda v: v.tensor_copy(base[:], ps88[:]), reads=["ps88"], writes=["base"])
        p.op("vector", lambda v: v.tensor_copy(tmpf[0:8, 0:8], base[:]), reads=["base"], writes=["tmpf"])
        p.op("vector", lambda v: v.tensor_copy(totbb[:], tmpf[0:8, 0:8]), reads=["tmpf"], writes=["totbb"])
        p.op("tensor", lambda t: t.matmul(ps1[:], bselc[:], totbb[:], start=True, stop=True),
             reads=["bselc", "totbb"], writes=["ps1"])
        p.op("vector", lambda v: v.tensor_copy(obrow[:], ps1[:]), reads=["ps1"], writes=["obrow"])
        p.op("tensor", lambda t: t.matmul(ps8[:], ones1fr[:], obrow[0:1, :], start=True, stop=True),
             reads=["obrow", "ones1fr"], writes=["ps8"])
        p.op("vector", lambda v: v.tensor_copy(obbc[:], ps8[:]), reads=["ps8"], writes=["obbc"])
        p.op("tensor", lambda t: t.matmul(ps8[:, 0:1], ones1fr[:], eidr[0:1, 0:1], start=True, stop=True),
             reads=["eidr", "ones1fr"], writes=["ps8"])
        p.op("vector", lambda v: v.tensor_copy(eidc[:], ps8[:, 0:1]), reads=["ps8"], writes=["eidc"])

        tbl4 = [nc.dram_tensor(pfx + "tbl4_%d" % k, [4224, 1], F32, kind="Internal").ap()
                for k in range(4)]
        p.op("vector", lambda v: v.memset(prefc[:], float(NCORE * T)), writes=["prefc"])
        for k in range(4):
            p.op("sync", lambda s, k=k: s.dma_start(
                out=tbl4[k].rearrange("(p a) c -> p (a c)", a=33), in_=prefc[:]),
                reads=["prefc"], writes=["tblP%d" % k], dma=True)
        p.op("gpsimd", lambda g: g.iota(dumpb[:], pattern=[[0, 32]], base=4096, channel_multiplier=1,
                                        allow_small_or_imprecise_dtypes=True), writes=["dumpb"])

        for c in range(8):
            p.op("gpsimd", lambda g, c=c: g.dma_start(out=brow[0:1, :], in_=base[c:c + 1, :]),
                 reads=["base"], writes=["brow"], dma=True)
            p.op("tensor", lambda t: t.matmul(ps8[:], ones1fr[:], brow[0:1, :], start=True, stop=True),
                 reads=["brow", "ones1fr"], writes=["ps8"])
            p.op("vector", lambda v: v.tensor_copy(bbc[:], ps8[:]), reads=["ps8"], writes=["bbc"])
            p.op("sync", lambda s, c=c: s.dma_start(out=e1b[:], in_=all_e1[c]), writes=["e1b"], dma=True)
            p.op("sync", lambda s, c=c: s.dma_start(out=e2b[:], in_=all_e2[c]), writes=["e2b"], dma=True)
            p.op("sync", lambda s, c=c: s.dma_start(out=pl1b[:], in_=all_pl1[c]), writes=["pl1b"], dma=True)
            p.op("sync", lambda s, c=c: s.dma_start(out=pl2b[:], in_=all_pl2[c]), writes=["pl2b"], dma=True)
            p.op("gpsimd", lambda g, c=c: g.iota(valf[:], pattern=[[128, 32]], base=4096 * c,
                                                 channel_multiplier=1,
                                                 allow_small_or_imprecise_dtypes=True), writes=["valf"])
            for (eb, plb, ekey, plkey, rank) in ((e1b, pl1b, "e1b", "pl1b", 0), (e2b, pl2b, "e2b", "pl2b", 1)):
                idxf = idxf0 if rank == 0 else idxf1
                idxw = idxw0 if rank == 0 else idxw1
                fk = "idxf%d" % rank
                wk = "idxw%d" % rank
                p.op("vector", lambda v, plb=plb, idxf=idxf: v.tensor_copy(idxf[:], plb[:]),
                     reads=[plkey], writes=[fk])
                for e in range(8):
                    p.op("vector", lambda v, e=e, eb=eb: v.tensor_single_scalar(
                        maskb[:], eb[:], float(e), ALU.is_equal),
                        reads=[ekey], writes=["maskb"])
                    p.op("vector", lambda v, e=e, idxf=idxf: v.scalar_tensor_tensor(
                        idxf[:], maskb[:], bbc[:, e:e + 1], idxf[:], ALU.mult, ALU.add),
                        reads=["maskb", "bbc", fk], writes=[fk])
                p.op("vector", lambda v, eb=eb: v.tensor_single_scalar(
                    maskb[:], eb[:], eidc[:, 0:1], ALU.is_equal),
                    reads=[ekey, "eidc"], writes=["maskb"])
                p.op("vector", lambda v, idxf=idxf: v.tensor_scalar_add(idxf[:], idxf[:], -BIGIDX),
                     reads=[fk], writes=[fk])
                p.op("vector", lambda v, idxf=idxf: v.tensor_mul(idxf[:], idxf[:], maskb[:]),
                     reads=[fk, "maskb"], writes=[fk])
                p.op("vector", lambda v, idxf=idxf: v.tensor_scalar_add(idxf[:], idxf[:], BIGIDX),
                     reads=[fk], writes=[fk])
                p.op("vector", lambda v, idxf=idxf: v.tensor_tensor(idxf[:], idxf[:], dumpb[:], ALU.min),
                     reads=[fk, "dumpb"], writes=[fk])
                p.op("vector", lambda v, idxf=idxf, idxw=idxw: v.tensor_copy(idxw[:], idxf[:]),
                     reads=[fk], writes=[wk])
                for j in range(32):
                    k4 = (2 * j + rank) % 4
                    p.op("gpsimd", lambda g, j=j, k4=k4, idxw=idxw: g.indirect_dma_start(
                        out=tbl4[k4], out_offset=IndirectOffsetOnAxis(idxw[:, j:j + 1], 0),
                        in_=valf[:, j:j + 1], in_offset=None),
                        reads=[wk, "valf"], writes=["tblP%d" % k4], dma=True)
        for k in range(4):
            p.op("sync", lambda s, k=k: s.dma_start(
                out=mt[k][:], in_=tbl4[k].rearrange("(p a) c -> p (a c)", a=33)),
                reads=["tblP%d" % k], writes=["mt%d" % k], dma=True)
        p.op("vector", lambda v: v.tensor_tensor(mt[0][:], mt[0][:], mt[1][:], ALU.min),
             reads=["mt0", "mt1"], writes=["mt0"])
        p.op("vector", lambda v: v.tensor_tensor(mt[2][:], mt[2][:], mt[3][:], ALU.min),
             reads=["mt2", "mt3"], writes=["mt2"])
        p.op("vector", lambda v: v.tensor_tensor(mt[0][:], mt[0][:], mt[2][:], ALU.min),
             reads=["mt0", "mt2"], writes=["mt0"])
        p.op("vector", lambda v: v.tensor_copy(mti[:], mt[0][:]), reads=["mt0"], writes=["mti"])
        p.op("sync", lambda s: s.dma_start(
            out=slot_tbl.rearrange("(p a) c -> p (a c)", a=33), in_=mti[:]),
            reads=["mti"], writes=["slot_tbl"], dma=True)
        own_e1 = nc.dram_tensor("own_e1", [128, 32], F32, kind="ExternalInput").ap()
        own_e2 = nc.dram_tensor("own_e2", [128, 32], F32, kind="ExternalInput").ap()
        own_pl1 = nc.dram_tensor("own_pl1", [128, 32], F32, kind="ExternalInput").ap()
        own_pl2 = nc.dram_tensor("own_pl2", [128, 32], F32, kind="ExternalInput").ap()
        p.op("sync", lambda s: s.dma_start(out=e1b[:], in_=own_e1[:]), writes=["e1b"], dma=True)
        p.op("sync", lambda s: s.dma_start(out=e2b[:], in_=own_e2[:]), writes=["e2b"], dma=True)
        p.op("sync", lambda s: s.dma_start(out=pl1b[:], in_=own_pl1[:]), writes=["pl1b"], dma=True)
        p.op("sync", lambda s: s.dma_start(out=pl2b[:], in_=own_pl2[:]), writes=["pl2b"], dma=True)
        for it in range(32):
            for (eb, plb, wb, ekey, plkey, wkey, vwsb, gxsb, vwk, gxk) in (
                    (e1b, pl1b, w1b_, "e1b", "pl1b", "w1b_", vw1sb, gx1sb, "vw1sb", "gx1sb"),
                    (e2b, pl2b, w2b_, "e2b", "pl2b", "w2b_", vw2sb, gx2sb, "vw2sb", "gx2sb")):
                p.op("vector", lambda v, it=it, eb=eb: v.tensor_single_scalar(
                    oh[:], iota8[:], eb[:, it:it + 1], ALU.is_equal),
                    reads=["iota8", ekey], writes=["oh"])
                p.op("vector", lambda v: v.tensor_mul(oh[:], oh[:], obbc[:]),
                     reads=["oh", "obbc"], writes=["oh"])
                p.op("vector", lambda v: v.tensor_reduce(colA[:], oh[:], AX.X, ALU.add),
                     reads=["oh"], writes=["colA"])
                p.op("vector", lambda v, it=it, plb=plb: v.tensor_add(colA[:], colA[:], plb[:, it:it + 1]),
                     reads=["colA", plkey], writes=["colA"])
                p.op("vector", lambda v: v.tensor_single_scalar(colB[:], colA[:], 4096.0, ALU.is_lt),
                     reads=["colA"], writes=["colB"])
                p.op("vector", lambda v, it=it, wb=wb, vwsb=vwsb: v.tensor_mul(
                    vwsb[:, it:it + 1], wb[:, it:it + 1], colB[:]),
                    reads=[wkey, "colB"], writes=[vwk])
                p.op("vector", lambda v: v.tensor_mul(colA[:], colA[:], colB[:]),
                     reads=["colA", "colB"], writes=["colA"])
                p.op("vector", lambda v, it=it, eb=eb: v.scalar_tensor_tensor(
                    colA[:], eb[:, it:it + 1], 4096.0, colA[:], ALU.mult, ALU.add),
                    reads=[ekey, "colA"], writes=["colA"])
                p.op("vector", lambda v, it=it, gxsb=gxsb: v.tensor_copy(gxsb[:, it:it + 1], colA[:]),
                     reads=["colA"], writes=[gxk])
        p.op("sync", lambda s: s.dma_start(out=vw1_d[:], in_=vw1sb[:]), reads=["vw1sb"], writes=[], dma=True)
        p.op("sync", lambda s: s.dma_start(out=vw2_d[:], in_=vw2sb[:]), reads=["vw2sb"], writes=[], dma=True)
        p.op("sync", lambda s: s.dma_start(out=gidx1_d[:], in_=gx1sb[:]), reads=["gx1sb"], writes=[], dma=True)
        p.op("sync", lambda s: s.dma_start(out=gidx2_d[:], in_=gx2sb[:]), reads=["gx2sb"], writes=[], dma=True)
        p.finish("vector")
    p.finalize()
    return nc



def build_moe_expert():
    nc = bass.Bass("TRN2", target_bir_lowering=False, debug=False)
    from contextlib import ExitStack
    xn_full = nc.dram_tensor("xn_full", [NCORE * T + 128, D], BF16, kind="ExternalInput").ap()
    slot_tbl = nc.dram_tensor("slot_tbl", [4224, 1], I32, kind="ExternalInput").ap()
    c_fc = nc.dram_tensor("c_fc", [D, 3072], F32, kind="ExternalInput").ap()
    c_proj = nc.dram_tensor("c_proj", [1536, D], F32, kind="ExternalInput").ap()
    exp_out = nc.dram_tensor("exp_out", [4096, D], BF16, kind="ExternalOutput").ap()
    p = P(nc)
    stack = ExitStack()
    with stack:
        sb = lambda name, shape, dt: stack.enter_context(nc.sbuf_tensor(name, shape, dt))
        psumt = lambda name, shape: stack.enter_context(nc.psum_tensor(name, [shape[0], 512], F32)).ap()[0:shape[0], 0:shape[1]]
        XT = sb("XT", [128, 8 * 4096], BF16)
        sidx = sb("sidx", [128, 1], I32)
        Xrow = sb("Xrow", [128, D], BF16)
        tmpf = sb("tmpf", [128, 128], F32)
        ident = sb("ident", [128, 128], F32)
        h = sb("h", [128, 24 * 512], BF16)
        h2 = sb("h2", [128, 12 * 512], BF16)
        sil = sb("sil", [128, 512], BF16)
        wfc = sb("wfc", [128, 8 * 3072], BF16)
        wpj = sb("wpj", [128, 12 * 1024], BF16)
        otile = sb("otile", [128, 512], BF16)
        ps_h = psumt("ps_h", [128, 512])
        ps_o = psumt("ps_o", [128, 512])
        ps_T = psumt("ps_T", [128, 128])

        emit_identity(p, ident)
        p.op("gpsimd", lambda g: g.dma_start(
            out=wfc[:].rearrange("p (dc f) -> p dc f", dc=8),
            in_=c_fc.rearrange("(dc p) f -> p dc f", p=128)), writes=["wfc"], dma=True)
        p.op("gpsimd", lambda g: g.dma_start(
            out=wpj[:].rearrange("p (fc f) -> p fc f", fc=12),
            in_=c_proj.rearrange("(fc p) f -> p fc f", p=128)), writes=["wpj"], dma=True)
        for it in range(32):
            p.op("sync", lambda s, it=it: s.dma_start(out=sidx[:], in_=slot_tbl[128 * it:128 * (it + 1), :]),
                 writes=["sidx"], dma=True)
            p.op("gpsimd", lambda g: g.indirect_dma_start(
                out=Xrow[:], out_offset=None, in_=xn_full,
                in_offset=IndirectOffsetOnAxis(sidx[:, 0:1], 0)),
                reads=["sidx"], writes=["Xrow"], dma=True)
            for dc in range(8):
                p.op("scalar", lambda s, dc=dc: s.activation(out=tmpf[:], in_=Xrow[:, 128 * dc:128 * (dc + 1)],
                                                             func=AF.Identity), reads=["Xrow"], writes=["tmpf"])
                p.op("tensor", lambda t: t.transpose(ps_T[:], tmpf[:], ident[:]),
                     reads=["tmpf", "ident"], writes=["ps_T"])
                p.op("vector", lambda v, it=it, dc=dc: v.tensor_copy(
                    XT[:, 4096 * dc + 128 * it:4096 * dc + 128 * (it + 1)], ps_T[:]),
                    reads=["ps_T"], writes=["XT"])
        for tc in range(8):
            tsl = slice(512 * tc, 512 * (tc + 1))
            for fc in range(24):
                mm_group(p, ps_h[:], "ps_h",
                         [(wfc[:, 3072 * dc + 128 * fc:3072 * dc + 128 * (fc + 1)],
                           XT[:, 4096 * dc + tsl.start:4096 * dc + tsl.stop]) for dc in range(8)],
                         ["wfc", "XT"])
                p.op("vector", lambda v, fc=fc: v.tensor_copy(h[:, 512 * fc:512 * (fc + 1)], ps_h[:]),
                     reads=["ps_h"], writes=["h"])
            for c_ in range(12):
                p.op("scalar", lambda s, c_=c_: s.activation(out=sil[:], in_=h[:, 512 * c_:512 * (c_ + 1)],
                                                             func=AF.Silu), reads=["h"], writes=["sil"])
                p.op("vector", lambda v, c_=c_: v.tensor_mul(h2[:, 512 * c_:512 * (c_ + 1)], sil[:],
                                                             h[:, 512 * (c_ + 12):512 * (c_ + 13)]),
                     reads=["sil", "h"], writes=["h2"])
            for st_ in range(4):
                for oc in range(2):
                    def mmpj(t, st_=st_, oc=oc):
                        for fc in range(12):
                            ins = t.matmul(ps_o[:], h2[:, 512 * fc + 128 * st_:512 * fc + 128 * (st_ + 1)],
                                           wpj[:, 1024 * fc + 512 * oc:1024 * fc + 512 * (oc + 1)],
                                           start=(fc == 0), stop=(fc == 11))
                        return ins
                    p.op("tensor", mmpj, reads=["wpj", "h2"], writes=["ps_o"])
                    p.op("vector", lambda v: v.tensor_copy(otile[:], ps_o[:]), reads=["ps_o"], writes=["otile"])
                    p.op("sync", lambda s, tc=tc, st_=st_, oc=oc: s.dma_start(
                        out=exp_out[512 * tc + 128 * st_:512 * tc + 128 * (st_ + 1), 512 * oc:512 * (oc + 1)],
                        in_=otile[:]), reads=["otile"], writes=[], dma=True)
        p.finish("vector")
    p.finalize()
    return nc



def build_moe_combine():
    nc = bass.Bass("TRN2", target_bir_lowering=False, debug=False)
    from contextlib import ExitStack
    xf = nc.dram_tensor("xf", [D, T], F32, kind="ExternalInput").ap()
    eo_full = nc.dram_tensor("eo_full", [8 * 4096, D], BF16, kind="ExternalInput").ap()
    gidx1_d = nc.dram_tensor("gidx1_d", [128, 32], I32, kind="ExternalInput").ap()
    gidx2_d = nc.dram_tensor("gidx2_d", [128, 32], I32, kind="ExternalInput").ap()
    vw1_d = nc.dram_tensor("vw1_d", [128, 32], F32, kind="ExternalInput").ap()
    vw2_d = nc.dram_tensor("vw2_d", [128, 32], F32, kind="ExternalInput").ap()
    xfo = nc.dram_tensor("xfo", [D, T], F32, kind="ExternalOutput").ap()
    p = P(nc)
    stack = ExitStack()
    with stack:
        sb = lambda name, shape, dt: stack.enter_context(nc.sbuf_tensor(name, shape, dt))
        psumt = lambda name, shape: stack.enter_context(nc.psum_tensor(name, [shape[0], 512], F32)).ap()[0:shape[0], 0:shape[1]]
        xf_sb = sb("xf_sb", [128, 8 * T], F32)
        g1 = sb("g1", [128, 32], I32)
        g2 = sb("g2", [128, 32], I32)
        v1 = sb("v1", [128, 32], F32)
        v2 = sb("v2", [128, 32], F32)
        r1 = sb("r1", [128, D], BF16)
        r2 = sb("r2", [128, D], BF16)
        ctile = sb("ctile", [128, D], F32)
        ident = sb("ident", [128, 128], F32)
        ps_T = psumt("ps_T", [128, 128])

        emit_identity(p, ident)
        for dc in range(8):
            p.op("sync", lambda s, dc=dc: s.dma_start(out=xf_sb[:, 4096 * dc:4096 * (dc + 1)],
                                                      in_=xf[128 * dc:128 * (dc + 1), :]),
                 writes=["xf"], dma=True)
        p.op("sync", lambda s: s.dma_start(out=g1[:], in_=gidx1_d[:]), writes=["g1"], dma=True)
        p.op("sync", lambda s: s.dma_start(out=g2[:], in_=gidx2_d[:]), writes=["g2"], dma=True)
        p.op("sync", lambda s: s.dma_start(out=v1[:], in_=vw1_d[:]), writes=["v1"], dma=True)
        p.op("sync", lambda s: s.dma_start(out=v2[:], in_=vw2_d[:]), writes=["v2"], dma=True)
        for it in range(32):
            p.op("gpsimd", lambda g, it=it: g.indirect_dma_start(
                out=r1[:], out_offset=None, in_=eo_full,
                in_offset=IndirectOffsetOnAxis(g1[:, it:it + 1], 0)),
                reads=["g1"], writes=["r1"], dma=True)
            p.op("gpsimd", lambda g, it=it: g.indirect_dma_start(
                out=r2[:], out_offset=None, in_=eo_full,
                in_offset=IndirectOffsetOnAxis(g2[:, it:it + 1], 0)),
                reads=["g2"], writes=["r2"], dma=True)
            p.op("vector", lambda v, it=it: v.tensor_single_scalar(ctile[:], r1[:], v1[:, it:it + 1], ALU.mult),
                 reads=["r1", "v1"], writes=["ctile"])
            p.op("vector", lambda v, it=it: v.scalar_tensor_tensor(ctile[:], r2[:], v2[:, it:it + 1],
                                                                   ctile[:], ALU.mult, ALU.add),
                 reads=["r2", "v2", "ctile"], writes=["ctile"])
            for dc in range(8):
                p.op("tensor", lambda t, dc=dc: t.transpose(ps_T[:], ctile[:, 128 * dc:128 * (dc + 1)], ident[:]),
                     reads=["ctile", "ident"], writes=["ps_T"])
                p.op("vector", lambda v, it=it, dc=dc: v.scalar_tensor_tensor(
                    xf_sb[:, 4096 * dc + 128 * it:4096 * dc + 128 * (it + 1)], ps_T[:], 0.5,
                    xf_sb[:, 4096 * dc + 128 * it:4096 * dc + 128 * (it + 1)], ALU.mult, ALU.add),
                    reads=["ps_T", "xf"], writes=["xf"])
        for dc in range(8):
            p.op("sync", lambda s, dc=dc: s.dma_start(out=xfo[128 * dc:128 * (dc + 1), :],
                                                      in_=xf_sb[:, 4096 * dc:4096 * (dc + 1)]),
                 reads=["xf"], writes=[], dma=True)
        p.finish("vector")
    p.finalize()
    return nc


def emit_moe_gate(p, nc, pfx, xf, w_gate, xn_tm, e1_d, e2_d, w1_d, w2_d, pl1_d, pl2_d, tot_d):
    stack = ExitStack()
    with stack:
        sb = lambda name, shape, dt: stack.enter_context(nc.sbuf_tensor(pfx + name, shape, dt))
        psumt = lambda name, shape: stack.enter_context(nc.psum_tensor(pfx + name, [shape[0], 512], F32)).ap()[0:shape[0], 0:shape[1]]
        xn = sb("xn", [128, 8 * T], BF16)
        xbuf = sb("xbuf", [128, 2 * 512], F32)
        wgt = sb("wgt", [128, 8 * 8], F32)
        iota8 = sb("iota8", [128, 8], F32)
        logits_all = sb("logits_all", [128, 32 * 8], F32)
        logits = sb("logits", [128, 8], F32)
        max8 = sb("max8", [128, 8], F32)
        nm1 = sb("nm1", [128, 1], F32)
        sume = sb("sume", [128, 1], F32)
        etmp = sb("etmp", [128, 8], F32)
        l2 = sb("l2", [128, 8], F32)
        eq0 = sb("eq0", [128, 8], F32)
        col1 = sb("col1", [128, 1], F32)
        ohsum = sb("ohsum", [128, 8], BF16)
        e1c = sb("e1c", [128, 32], F32)
        e2c = sb("e2c", [128, 32], F32)
        w1c = sb("w1c", [128, 32], F32)
        w2c = sb("w2c", [128, 32], F32)
        pl1c = sb("pl1c", [128, 32], F32)
        pl2c = sb("pl2c", [128, 32], F32)
        oh1t = sb("oh1t", [128, 8], F32)
        oh2t = sb("oh2t", [128, 8], F32)
        cumt = sb("cumt", [128, 8], F32)
        Ustrict = sb("Ustrict", [128, 128], BF16)
        U32s = sb("U32s", [32, 32], BF16)
        onesc = sb("onesc", [128, 1], BF16)
        ones1fr = sb("ones1fr", [1, 128], F32)
        totrows = sb("totrows", [32, 8], F32)
        totrowsb = sb("totrowsb", [32, 8], BF16)
        prefr = sb("prefr", [32, 8], F32)
        prow = sb("prow", [1, 8], F32)
        pbc = sb("pbc", [128, 8], F32)
        tmpf = sb("tmpf", [128, 128], F32)
        tmpb = sb("tmpb", [128, 128], BF16)
        ident = sb("ident", [128, 128], F32)
        ps8 = psumt("ps8", [128, 8])
        psb = psumt("psb_g", [128, 128])
        ps1 = psumt("ps1", [1, 8])

        emit_identity(p, ident)
        p.op("vector", lambda v: v.memset(onesc[:], 1.0), writes=["onesc"])
        p.op("vector", lambda v: v.memset(ones1fr[:], 1.0), writes=["ones1fr"])
        p.op("gpsimd", lambda g: g.iota(iota8[:], pattern=[[1, 8]], base=0, channel_multiplier=0,
                                        allow_small_or_imprecise_dtypes=True), writes=["iota8"])
        p.op("gpsimd", lambda g: g.memset(tmpf[:], 0.0), writes=["tmpf"])
        p.op("gpsimd", lambda g: g.affine_select(out=tmpf[:], in_=tmpf[:], compare_op=ALU.is_ge,
                                                 fill=1.0, base=0, pattern=[[-1, 128]],
                                                 channel_multiplier=1),
             reads=["tmpf"], writes=["tmpf"])
        p.op("scalar", lambda s: s.activation(out=Ustrict[:], in_=tmpf[:], func=AF.Identity),
             reads=["tmpf"], writes=["Ustrict"])
        p.op("gpsimd", lambda g: g.memset(tmpf[:, 0:32], 0.0), writes=["tmpf"])
        p.op("gpsimd", lambda g: g.affine_select(out=tmpf[0:32, 0:32], in_=tmpf[0:32, 0:32],
                                                 compare_op=ALU.is_ge, fill=1.0, base=0,
                                                 pattern=[[-1, 32]], channel_multiplier=1),
             reads=["tmpf"], writes=["tmpf"])
        p.op("scalar", lambda s: s.activation(out=U32s[:], in_=tmpf[0:32, 0:32], func=AF.Identity),
             reads=["tmpf"], writes=["U32s"])
        for dc in range(8):
            p.op("sync", lambda s, dc=dc: s.dma_start(out=wgt[:, 8 * dc:8 * (dc + 1)],
                                                      in_=w_gate[128 * dc:128 * (dc + 1), :]),
                 writes=["wgt"], dma=True)

        state = {"i": 0}
        def srcf(dc, tsl):
            slot = state["i"] % 2
            state["i"] += 1
            key = "xbuf%d" % slot
            ap = xbuf[:, 512 * slot:512 * (slot + 1)]
            p.op("sync", lambda s, dc=dc, tsl=tsl, ap=ap: s.dma_start(
                out=ap, in_=xf[128 * dc:128 * (dc + 1), tsl]), writes=[key], dma=True)
            return ap, key

        def dstf(dc, tsl):
            return xn[:, 4096 * dc + tsl.start:4096 * dc + tsl.stop], "xn"

        with ExitStack() as lnstack:
            ws = alloc_ln_ws(nc, lnstack, tag=pfx, wide_x32=True)
            emit_ln_setup(p, ws)
            def gate_logits(tc):
                for sub in range(4):
                    it = 4 * tc + sub
                    def mmg(t, sub=sub):
                        ins = None
                        for dc in range(8):
                            ins = t.matmul(ps8[:], ws["x32"][:, 512 * dc + 128 * sub:512 * dc + 128 * (sub + 1)],
                                           wgt[:, 8 * dc:8 * (dc + 1)], start=(dc == 0), stop=(dc == 7))
                        return ins
                    p.op("tensor", mmg, reads=["lnx32", "wgt"], writes=["ps8"])
                    p.op("vector", lambda v, it=it: v.tensor_copy(logits_all[:, 8 * it:8 * (it + 1)], ps8[:]),
                         reads=["ps8"], writes=["logits_all"])
            emit_ln(p, ws, srcf, dstf, after_tc=gate_logits)
            for e in ENGINES:
                p.barrier_wait(e, ["xn", "sx", "sxx", "lnRb", "lnMRb", "lnsq", "lnmsq", "lnpsb", "psrow"])

        for it in range(32):
            for dc in range(8):
                p.op("scalar", lambda s, it=it, dc=dc: s.activation(
                    out=tmpf[:], in_=xn[:, 4096 * dc + 128 * it:4096 * dc + 128 * (it + 1)],
                    func=AF.Identity), reads=["xn"], writes=["tmpf"])
                p.op("tensor", lambda t: t.transpose(psb[:], tmpf[:], ident[:]),
                     reads=["tmpf", "ident"], writes=["psb_g"])
                p.op("vector", lambda v: v.tensor_copy(tmpb[:], psb[:]), reads=["psb_g"], writes=["tmpb"])
                p.op("gpsimd", lambda g, it=it, dc=dc: g.dma_start(
                    out=xn_tm[128 * it:128 * (it + 1), 128 * dc:128 * (dc + 1)], in_=tmpb[:]),
                    reads=["tmpb"], writes=[], dma=True)

        for it in range(32):
            p.op("vector", lambda v, it=it: v.tensor_copy(logits[:], logits_all[:, 8 * it:8 * (it + 1)]),
                 reads=["logits_all"], writes=["logits"])
            p.op("vector", lambda v: v.max(max8[:], logits[:]), reads=["logits"], writes=["max8"])
            p.op("vector", lambda v: v.tensor_scalar_mul(nm1[:], max8[:, 0:1], -1.0),
                 reads=["max8"], writes=["nm1"])
            p.op("scalar", lambda s: s.activation(out=etmp[:], in_=logits[:], func=AF.Exp,
                                                  bias=nm1[:, 0:1], scale=1.0, accum_out=sume[:]),
                 reads=["logits", "nm1"], writes=["etmp", "sume"])
            p.op("vector", lambda v: v.reciprocal(col1[:], sume[:]), reads=["sume"], writes=["col1"])
            p.op("vector", lambda v, it=it: v.tensor_copy(w1c[:, it:it + 1], col1[:]),
                 reads=["col1"], writes=["w1c"])
            p.op("scalar", lambda s: s.activation(out=sume[:], in_=max8[:, 1:2], func=AF.Exp,
                                                  bias=nm1[:, 0:1], scale=1.0),
                 reads=["max8", "nm1"], writes=["sume"])
            p.op("vector", lambda v, it=it: v.tensor_mul(w2c[:, it:it + 1], sume[:], col1[:]),
                 reads=["sume", "col1"], writes=["w2c"])
            p.op("vector", lambda v: v.tensor_single_scalar(eq0[:], logits[:], max8[:, 0:1], ALU.is_ge),
                 reads=["logits", "max8"], writes=["eq0"])
            p.op("vector", lambda v: v.scalar_tensor_tensor(eq0[:], eq0[:], -1000.0, iota8[:],
                                                            ALU.mult, ALU.add),
                 reads=["eq0", "iota8"], writes=["eq0"])
            p.op("vector", lambda v: v.tensor_reduce(col1[:], eq0[:], AX.X, ALU.min),
                 reads=["eq0"], writes=["col1"])
            p.op("vector", lambda v, it=it: v.tensor_scalar_add(e1c[:, it:it + 1], col1[:], 1000.0),
                 reads=["col1"], writes=["e1c"])
            p.op("vector", lambda v: v.tensor_copy(etmp[:], max8[:]), reads=["max8"], writes=["etmp"])
            p.op("vector", lambda v: v.memset(etmp[:, 1:8], -1e30), reads=["etmp"], writes=["etmp"])
            p.op("vector", lambda v: v.match_replace(out=l2[:], in_to_replace=etmp[:],
                                                     in_values=logits[:], imm_value=-1e30),
                 reads=["etmp", "logits"], writes=["l2"])
            p.op("vector", lambda v: v.tensor_single_scalar(eq0[:], l2[:], max8[:, 1:2], ALU.is_ge),
                 reads=["l2", "max8"], writes=["eq0"])
            p.op("vector", lambda v: v.scalar_tensor_tensor(eq0[:], eq0[:], -1000.0, iota8[:],
                                                            ALU.mult, ALU.add),
                 reads=["eq0", "iota8"], writes=["eq0"])
            p.op("vector", lambda v: v.tensor_reduce(col1[:], eq0[:], AX.X, ALU.min),
                 reads=["eq0"], writes=["col1"])
            p.op("vector", lambda v, it=it: v.tensor_scalar_add(e2c[:, it:it + 1], col1[:], 1000.0),
                 reads=["col1"], writes=["e2c"])
            p.op("vector", lambda v, it=it: v.tensor_single_scalar(oh1t[:], iota8[:], e1c[:, it:it + 1],
                                                                   ALU.is_equal),
                 reads=["iota8", "e1c"], writes=["oh1t"])
            p.op("vector", lambda v, it=it: v.tensor_single_scalar(oh2t[:], iota8[:], e2c[:, it:it + 1],
                                                                   ALU.is_equal),
                 reads=["iota8", "e2c"], writes=["oh2t"])
            p.op("vector", lambda v: v.tensor_add(etmp[:], oh1t[:], oh2t[:]),
                 reads=["oh1t", "oh2t"], writes=["etmp"])
            p.op("vector", lambda v: v.tensor_copy(ohsum[:], etmp[:]), reads=["etmp"], writes=["ohsum"])
            p.op("tensor", lambda t: t.matmul(ps8[:], Ustrict[:], ohsum[:], start=True, stop=True),
                 reads=["Ustrict", "ohsum"], writes=["ps8"])
            p.op("vector", lambda v: v.tensor_copy(cumt[:], ps8[:]), reads=["ps8"], writes=["cumt"])
            p.op("tensor", lambda t: t.matmul(ps1[:], onesc[:], ohsum[:], start=True, stop=True),
                 reads=["onesc", "ohsum"], writes=["ps1"])
            p.op("vector", lambda v: v.tensor_copy(prow[:], ps1[:]), reads=["ps1"], writes=["prow"])
            p.op("gpsimd", lambda g, it=it: g.dma_start(out=totrows[it:it + 1, :], in_=prow[0:1, :]),
                 reads=["prow"], writes=["totrows"], dma=True)
            p.op("vector", lambda v: v.tensor_mul(oh1t[:], oh1t[:], cumt[:]), reads=["oh1t", "cumt"], writes=["oh1t"])
            p.op("vector", lambda v, it=it: v.tensor_reduce(pl1c[:, it:it + 1], oh1t[:], AX.X, ALU.add),
                 reads=["oh1t"], writes=["pl1c"])
            p.op("vector", lambda v: v.tensor_mul(oh2t[:], oh2t[:], cumt[:]), reads=["oh2t", "cumt"], writes=["oh2t"])
            p.op("vector", lambda v, it=it: v.tensor_reduce(pl2c[:, it:it + 1], oh2t[:], AX.X, ALU.add),
                 reads=["oh2t"], writes=["pl2c"])
        p.op("vector", lambda v: v.tensor_copy(totrowsb[:], totrows[:]), reads=["totrows"], writes=["totrowsb"])
        p.op("tensor", lambda t: t.matmul(ps8[0:32, 0:8], U32s[:], totrowsb[:], start=True, stop=True),
             reads=["U32s", "totrowsb"], writes=["ps8"])
        p.op("vector", lambda v: v.tensor_copy(prefr[:], ps8[0:32, 0:8]), reads=["ps8"], writes=["prefr"])
        p.op("tensor", lambda t: t.matmul(ps1[:], onesc[0:32, 0:1], totrowsb[:], start=True, stop=True),
             reads=["onesc", "totrowsb"], writes=["ps1"])
        p.op("vector", lambda v: v.tensor_copy(prow[:], ps1[:]), reads=["ps1"], writes=["prow"])
        p.op("sync", lambda s: s.dma_start(out=tot_d[:], in_=prow[0:1, :]), reads=["prow"], writes=[], dma=True)
        for it in range(32):
            p.op("gpsimd", lambda g, it=it: g.dma_start(out=prow[0:1, :], in_=prefr[it:it + 1, :]),
                 reads=["prefr"], writes=["prow"], dma=True)
            p.op("tensor", lambda t: t.matmul(ps8[:], ones1fr[:], prow[0:1, :],
                                              start=True, stop=True),
                 reads=["prow", "ones1fr"], writes=["ps8"])
            p.op("vector", lambda v: v.tensor_copy(pbc[:], ps8[:]), reads=["ps8"], writes=["pbc"])
            for (ec, plc, ekey, plkey) in ((e1c, pl1c, "e1c", "pl1c"), (e2c, pl2c, "e2c", "pl2c")):
                p.op("vector", lambda v, it=it, ec=ec: v.tensor_single_scalar(
                    oh1t[:], iota8[:], ec[:, it:it + 1], ALU.is_equal),
                    reads=["iota8", ekey], writes=["oh1t"])
                p.op("vector", lambda v: v.tensor_mul(oh1t[:], oh1t[:], pbc[:]),
                     reads=["oh1t", "pbc"], writes=["oh1t"])
                p.op("vector", lambda v: v.tensor_reduce(col1[:], oh1t[:], AX.X, ALU.add),
                     reads=["oh1t"], writes=["col1"])
                p.op("vector", lambda v, it=it, plc=plc: v.tensor_add(
                    plc[:, it:it + 1], plc[:, it:it + 1], col1[:]),
                    reads=["col1", plkey], writes=[plkey])
        for (dram, tile, key) in ((e1_d, e1c, "e1c"), (e2_d, e2c, "e2c"), (w1_d, w1c, "w1c"),
                                  (w2_d, w2c, "w2c"), (pl1_d, pl1c, "pl1c"), (pl2_d, pl2c, "pl2c")):
            p.op("sync", lambda s, dram=dram, tile=tile: s.dma_start(out=dram[:], in_=tile[:]),
                 reads=[key], writes=[], dma=True)
        p.finish("vector")

def emit_moe_route(p, nc, pfx, all_e1, all_e2, all_pl1, all_pl2, tot_all, own_w1, own_w2, bsel, eid, slot_tbl, vw1_d, vw2_d, gidx1_d, gidx2_d, own_e1, own_e2, own_pl1, own_pl2):
    stack = ExitStack()
    with stack:
        sb = lambda name, shape, dt: stack.enter_context(nc.sbuf_tensor(pfx + name, shape, dt))
        psumt = lambda name, shape: stack.enter_context(nc.psum_tensor(pfx + name, [shape[0], 512], F32)).ap()[0:shape[0], 0:shape[1]]
        e1b = sb("e1b", [128, 32], F32)
        e2b = sb("e2b", [128, 32], F32)
        pl1b = sb("pl1b", [128, 32], F32)
        pl2b = sb("pl2b", [128, 32], F32)
        w1b_ = sb("w1b_", [128, 32], F32)
        w2b_ = sb("w2b_", [128, 32], F32)
        totb = sb("totb", [8, 8], F32)
        totbb = sb("totbb", [8, 8], F32)
        U8s = sb("U8s", [8, 8], F32)
        base = sb("base", [8, 8], F32)
        bselc = sb("bselc", [8, 1], F32)
        eidc = sb("eidc", [128, 1], F32)
        eidr = sb("eidr", [1, 1], F32)
        brow = sb("brow", [1, 8], F32)
        bbc = sb("bbc", [128, 8], F32)
        obrow = sb("obrow", [1, 8], F32)
        obbc = sb("obbc", [128, 8], F32)
        iota8 = sb("iota8", [128, 8], F32)
        iotac = sb("iotac", [128, 1], F32)
        oh = sb("oh", [128, 8], F32)
        gp1 = sb("gp1", [128, 1], F32)
        gp2 = sb("gp2", [128, 1], F32)
        colA = sb("colA", [128, 1], F32)
        colB = sb("colB", [128, 1], F32)
        bigc = sb("bigc", [128, 1], F32)
        dumpc = sb("dumpc", [128, 1], F32)
        maskb = sb("maskb", [128, 32], F32)
        idxf0 = sb("idxf0", [128, 32], F32)
        idxf1 = sb("idxf1", [128, 32], F32)
        idxw0 = sb("idxw0", [128, 32], I32)
        idxw1 = sb("idxw1", [128, 32], I32)
        valf = sb("valf", [128, 32], F32)
        dumpb = sb("dumpb", [128, 32], F32)
        prefc = sb("prefc", [128, 33], F32)
        mt = [sb("mt%d" % k, [128, 33], F32) for k in range(4)]
        mti = sb("mti", [128, 33], I32)
        tmpf = sb("tmpf", [128, 32], F32)
        vw1sb = sb("vw1sb", [128, 32], F32)
        vw2sb = sb("vw2sb", [128, 32], F32)
        gx1sb = sb("gx1sb", [128, 32], I32)
        gx2sb = sb("gx2sb", [128, 32], I32)
        ones1fr = sb("ones1fr", [1, 128], F32)
        ps8 = psumt("ps8", [128, 8])
        ps1 = psumt("ps1", [1, 8])
        ps88 = psumt("ps88", [8, 8])

        p.op("vector", lambda v: v.memset(ones1fr[:], 1.0), writes=["ones1fr"])
        p.op("vector", lambda v: v.memset(bigc[:], BIGIDX), writes=["bigc"])
        p.op("gpsimd", lambda g: g.iota(iota8[:], pattern=[[1, 8]], base=0, channel_multiplier=0,
                                        allow_small_or_imprecise_dtypes=True), writes=["iota8"])
        p.op("sync", lambda s: s.dma_start(out=totb[:], in_=tot_all[:]), writes=["totb"], dma=True)
        p.op("sync", lambda s: s.dma_start(out=bselc[:], in_=bsel[:]), writes=["bselc"], dma=True)
        p.op("sync", lambda s: s.dma_start(out=eidr[:], in_=eid[:]), writes=["eidr"], dma=True)
        p.op("sync", lambda s: s.dma_start(out=w1b_[:], in_=own_w1[:]), writes=["w1b_"], dma=True)
        p.op("sync", lambda s: s.dma_start(out=w2b_[:], in_=own_w2[:]), writes=["w2b_"], dma=True)
        p.op("gpsimd", lambda g: g.memset(tmpf[:, 0:8], 0.0), writes=["tmpf"])
        p.op("gpsimd", lambda g: g.affine_select(out=tmpf[0:8, 0:8], in_=tmpf[0:8, 0:8],
                                                 compare_op=ALU.is_ge, fill=1.0, base=0,
                                                 pattern=[[-1, 8]], channel_multiplier=1),
             reads=["tmpf"], writes=["tmpf"])
        p.op("vector", lambda v: v.tensor_copy(U8s[:], tmpf[0:8, 0:8]), reads=["tmpf"], writes=["U8s"])
        p.op("vector", lambda v: v.tensor_copy(totbb[:], totb[:]), reads=["totb"], writes=["totbb"])
        p.op("tensor", lambda t: t.matmul(ps88[:], U8s[:], totbb[:], start=True, stop=True),
             reads=["U8s", "totbb"], writes=["ps88"])
        p.op("vector", lambda v: v.tensor_copy(base[:], ps88[:]), reads=["ps88"], writes=["base"])
        p.op("vector", lambda v: v.tensor_copy(tmpf[0:8, 0:8], base[:]), reads=["base"], writes=["tmpf"])
        p.op("vector", lambda v: v.tensor_copy(totbb[:], tmpf[0:8, 0:8]), reads=["tmpf"], writes=["totbb"])
        p.op("tensor", lambda t: t.matmul(ps1[:], bselc[:], totbb[:], start=True, stop=True),
             reads=["bselc", "totbb"], writes=["ps1"])
        p.op("vector", lambda v: v.tensor_copy(obrow[:], ps1[:]), reads=["ps1"], writes=["obrow"])
        p.op("tensor", lambda t: t.matmul(ps8[:], ones1fr[:], obrow[0:1, :], start=True, stop=True),
             reads=["obrow", "ones1fr"], writes=["ps8"])
        p.op("vector", lambda v: v.tensor_copy(obbc[:], ps8[:]), reads=["ps8"], writes=["obbc"])
        p.op("tensor", lambda t: t.matmul(ps8[:, 0:1], ones1fr[:], eidr[0:1, 0:1], start=True, stop=True),
             reads=["eidr", "ones1fr"], writes=["ps8"])
        p.op("vector", lambda v: v.tensor_copy(eidc[:], ps8[:, 0:1]), reads=["ps8"], writes=["eidc"])

        tbl4 = [nc.dram_tensor(pfx + "tbl4_%d" % k, [4224, 1], F32, kind="Internal").ap()
                for k in range(4)]
        p.op("vector", lambda v: v.memset(prefc[:], float(NCORE * T)), writes=["prefc"])
        for k in range(4):
            p.op("sync", lambda s, k=k: s.dma_start(
                out=tbl4[k].rearrange("(p a) c -> p (a c)", a=33), in_=prefc[:]),
                reads=["prefc"], writes=["tblP%d" % k], dma=True)
        p.op("gpsimd", lambda g: g.iota(dumpb[:], pattern=[[0, 32]], base=4096, channel_multiplier=1,
                                        allow_small_or_imprecise_dtypes=True), writes=["dumpb"])

        for c in range(8):
            p.op("gpsimd", lambda g, c=c: g.dma_start(out=brow[0:1, :], in_=base[c:c + 1, :]),
                 reads=["base"], writes=["brow"], dma=True)
            p.op("tensor", lambda t: t.matmul(ps8[:], ones1fr[:], brow[0:1, :], start=True, stop=True),
                 reads=["brow", "ones1fr"], writes=["ps8"])
            p.op("vector", lambda v: v.tensor_copy(bbc[:], ps8[:]), reads=["ps8"], writes=["bbc"])
            p.op("sync", lambda s, c=c: s.dma_start(out=e1b[:], in_=all_e1[c]), writes=["e1b"], dma=True)
            p.op("sync", lambda s, c=c: s.dma_start(out=e2b[:], in_=all_e2[c]), writes=["e2b"], dma=True)
            p.op("sync", lambda s, c=c: s.dma_start(out=pl1b[:], in_=all_pl1[c]), writes=["pl1b"], dma=True)
            p.op("sync", lambda s, c=c: s.dma_start(out=pl2b[:], in_=all_pl2[c]), writes=["pl2b"], dma=True)
            p.op("gpsimd", lambda g, c=c: g.iota(valf[:], pattern=[[128, 32]], base=4096 * c,
                                                 channel_multiplier=1,
                                                 allow_small_or_imprecise_dtypes=True), writes=["valf"])
            for (eb, plb, ekey, plkey, rank) in ((e1b, pl1b, "e1b", "pl1b", 0), (e2b, pl2b, "e2b", "pl2b", 1)):
                idxf = idxf0 if rank == 0 else idxf1
                idxw = idxw0 if rank == 0 else idxw1
                fk = "idxf%d" % rank
                wk = "idxw%d" % rank
                p.op("vector", lambda v, plb=plb, idxf=idxf: v.tensor_copy(idxf[:], plb[:]),
                     reads=[plkey], writes=[fk])
                for e in range(8):
                    p.op("vector", lambda v, e=e, eb=eb: v.tensor_single_scalar(
                        maskb[:], eb[:], float(e), ALU.is_equal),
                        reads=[ekey], writes=["maskb"])
                    p.op("vector", lambda v, e=e, idxf=idxf: v.scalar_tensor_tensor(
                        idxf[:], maskb[:], bbc[:, e:e + 1], idxf[:], ALU.mult, ALU.add),
                        reads=["maskb", "bbc", fk], writes=[fk])
                p.op("vector", lambda v, eb=eb: v.tensor_single_scalar(
                    maskb[:], eb[:], eidc[:, 0:1], ALU.is_equal),
                    reads=[ekey, "eidc"], writes=["maskb"])
                p.op("vector", lambda v, idxf=idxf: v.tensor_scalar_add(idxf[:], idxf[:], -BIGIDX),
                     reads=[fk], writes=[fk])
                p.op("vector", lambda v, idxf=idxf: v.tensor_mul(idxf[:], idxf[:], maskb[:]),
                     reads=[fk, "maskb"], writes=[fk])
                p.op("vector", lambda v, idxf=idxf: v.tensor_scalar_add(idxf[:], idxf[:], BIGIDX),
                     reads=[fk], writes=[fk])
                p.op("vector", lambda v, idxf=idxf: v.tensor_tensor(idxf[:], idxf[:], dumpb[:], ALU.min),
                     reads=[fk, "dumpb"], writes=[fk])
                p.op("vector", lambda v, idxf=idxf, idxw=idxw: v.tensor_copy(idxw[:], idxf[:]),
                     reads=[fk], writes=[wk])
                for j in range(32):
                    k4 = (2 * j + rank) % 4
                    p.op("gpsimd", lambda g, j=j, k4=k4, idxw=idxw: g.indirect_dma_start(
                        out=tbl4[k4], out_offset=IndirectOffsetOnAxis(idxw[:, j:j + 1], 0),
                        in_=valf[:, j:j + 1], in_offset=None),
                        reads=[wk, "valf"], writes=["tblP%d" % k4], dma=True)
        for k in range(4):
            p.op("sync", lambda s, k=k: s.dma_start(
                out=mt[k][:], in_=tbl4[k].rearrange("(p a) c -> p (a c)", a=33)),
                reads=["tblP%d" % k], writes=["mt%d" % k], dma=True)
        p.op("vector", lambda v: v.tensor_tensor(mt[0][:], mt[0][:], mt[1][:], ALU.min),
             reads=["mt0", "mt1"], writes=["mt0"])
        p.op("vector", lambda v: v.tensor_tensor(mt[2][:], mt[2][:], mt[3][:], ALU.min),
             reads=["mt2", "mt3"], writes=["mt2"])
        p.op("vector", lambda v: v.tensor_tensor(mt[0][:], mt[0][:], mt[2][:], ALU.min),
             reads=["mt0", "mt2"], writes=["mt0"])
        p.op("vector", lambda v: v.tensor_copy(mti[:], mt[0][:]), reads=["mt0"], writes=["mti"])
        p.op("sync", lambda s: s.dma_start(
            out=slot_tbl.rearrange("(p a) c -> p (a c)", a=33), in_=mti[:]),
            reads=["mti"], writes=["slot_tbl"], dma=True)
        p.op("sync", lambda s: s.dma_start(out=e1b[:], in_=own_e1[:]), writes=["e1b"], dma=True)
        p.op("sync", lambda s: s.dma_start(out=e2b[:], in_=own_e2[:]), writes=["e2b"], dma=True)
        p.op("sync", lambda s: s.dma_start(out=pl1b[:], in_=own_pl1[:]), writes=["pl1b"], dma=True)
        p.op("sync", lambda s: s.dma_start(out=pl2b[:], in_=own_pl2[:]), writes=["pl2b"], dma=True)
        for it in range(32):
            for (eb, plb, wb, ekey, plkey, wkey, vwsb, gxsb, vwk, gxk) in (
                    (e1b, pl1b, w1b_, "e1b", "pl1b", "w1b_", vw1sb, gx1sb, "vw1sb", "gx1sb"),
                    (e2b, pl2b, w2b_, "e2b", "pl2b", "w2b_", vw2sb, gx2sb, "vw2sb", "gx2sb")):
                p.op("vector", lambda v, it=it, eb=eb: v.tensor_single_scalar(
                    oh[:], iota8[:], eb[:, it:it + 1], ALU.is_equal),
                    reads=["iota8", ekey], writes=["oh"])
                p.op("vector", lambda v: v.tensor_mul(oh[:], oh[:], obbc[:]),
                     reads=["oh", "obbc"], writes=["oh"])
                p.op("vector", lambda v: v.tensor_reduce(colA[:], oh[:], AX.X, ALU.add),
                     reads=["oh"], writes=["colA"])
                p.op("vector", lambda v, it=it, plb=plb: v.tensor_add(colA[:], colA[:], plb[:, it:it + 1]),
                     reads=["colA", plkey], writes=["colA"])
                p.op("vector", lambda v: v.tensor_single_scalar(colB[:], colA[:], 4096.0, ALU.is_lt),
                     reads=["colA"], writes=["colB"])
                p.op("vector", lambda v, it=it, wb=wb, vwsb=vwsb: v.tensor_mul(
                    vwsb[:, it:it + 1], wb[:, it:it + 1], colB[:]),
                    reads=[wkey, "colB"], writes=[vwk])
                p.op("vector", lambda v: v.tensor_mul(colA[:], colA[:], colB[:]),
                     reads=["colA", "colB"], writes=["colA"])
                p.op("vector", lambda v, it=it, eb=eb: v.scalar_tensor_tensor(
                    colA[:], eb[:, it:it + 1], 4096.0, colA[:], ALU.mult, ALU.add),
                    reads=[ekey, "colA"], writes=["colA"])
                p.op("vector", lambda v, it=it, gxsb=gxsb: v.tensor_copy(gxsb[:, it:it + 1], colA[:]),
                     reads=["colA"], writes=[gxk])
        p.op("sync", lambda s: s.dma_start(out=vw1_d[:], in_=vw1sb[:]), reads=["vw1sb"], writes=[], dma=True)
        p.op("sync", lambda s: s.dma_start(out=vw2_d[:], in_=vw2sb[:]), reads=["vw2sb"], writes=[], dma=True)
        p.op("sync", lambda s: s.dma_start(out=gidx1_d[:], in_=gx1sb[:]), reads=["gx1sb"], writes=[], dma=True)
        p.op("sync", lambda s: s.dma_start(out=gidx2_d[:], in_=gx2sb[:]), reads=["gx2sb"], writes=[], dma=True)
        p.finish("vector")

def emit_moe_expert(p, nc, pfx, xn_full, slot_tbl, c_fc, c_proj, exp_out):
    stack = ExitStack()
    with stack:
        sb = lambda name, shape, dt: stack.enter_context(nc.sbuf_tensor(pfx + name, shape, dt))
        psumt = lambda name, shape: stack.enter_context(nc.psum_tensor(pfx + name, [shape[0], 512], F32)).ap()[0:shape[0], 0:shape[1]]
        XT = sb("XT", [128, 8 * 4096], BF16)
        sidx = sb("sidx", [128, 1], I32)
        Xrow = sb("Xrow", [128, D], BF16)
        tmpf = sb("tmpf", [128, 128], F32)
        ident = sb("ident", [128, 128], F32)
        h = sb("h", [128, 24 * 512], BF16)
        h2 = sb("h2", [128, 12 * 512], BF16)
        sil = sb("sil", [128, 512], BF16)
        wfc = sb("wfc", [128, 8 * 3072], BF16)
        wpj = sb("wpj", [128, 12 * 1024], BF16)
        otile = sb("otile", [128, 512], BF16)
        ps_h = psumt("ps_h", [128, 512])
        ps_o = psumt("ps_o", [128, 512])
        ps_T = psumt("ps_T", [128, 128])

        emit_identity(p, ident)
        p.op("gpsimd", lambda g: g.dma_start(
            out=wfc[:].rearrange("p (dc f) -> p dc f", dc=8),
            in_=c_fc.rearrange("(dc p) f -> p dc f", p=128)), writes=["wfc"], dma=True)
        p.op("gpsimd", lambda g: g.dma_start(
            out=wpj[:].rearrange("p (fc f) -> p fc f", fc=12),
            in_=c_proj.rearrange("(fc p) f -> p fc f", p=128)), writes=["wpj"], dma=True)
        for it in range(32):
            p.op("sync", lambda s, it=it: s.dma_start(out=sidx[:], in_=slot_tbl[128 * it:128 * (it + 1), :]),
                 writes=["sidx"], dma=True)
            p.op("gpsimd", lambda g: g.indirect_dma_start(
                out=Xrow[:], out_offset=None, in_=xn_full,
                in_offset=IndirectOffsetOnAxis(sidx[:, 0:1], 0)),
                reads=["sidx"], writes=["Xrow"], dma=True)
            for dc in range(8):
                p.op("scalar", lambda s, dc=dc: s.activation(out=tmpf[:], in_=Xrow[:, 128 * dc:128 * (dc + 1)],
                                                             func=AF.Identity), reads=["Xrow"], writes=["tmpf"])
                p.op("tensor", lambda t: t.transpose(ps_T[:], tmpf[:], ident[:]),
                     reads=["tmpf", "ident"], writes=["ps_T"])
                p.op("vector", lambda v, it=it, dc=dc: v.tensor_copy(
                    XT[:, 4096 * dc + 128 * it:4096 * dc + 128 * (it + 1)], ps_T[:]),
                    reads=["ps_T"], writes=["XT"])
        for tc in range(8):
            tsl = slice(512 * tc, 512 * (tc + 1))
            for fc in range(24):
                mm_group(p, ps_h[:], "ps_h",
                         [(wfc[:, 3072 * dc + 128 * fc:3072 * dc + 128 * (fc + 1)],
                           XT[:, 4096 * dc + tsl.start:4096 * dc + tsl.stop]) for dc in range(8)],
                         ["wfc", "XT"])
                p.op("vector", lambda v, fc=fc: v.tensor_copy(h[:, 512 * fc:512 * (fc + 1)], ps_h[:]),
                     reads=["ps_h"], writes=["h"])
            for c_ in range(12):
                p.op("scalar", lambda s, c_=c_: s.activation(out=sil[:], in_=h[:, 512 * c_:512 * (c_ + 1)],
                                                             func=AF.Silu), reads=["h"], writes=["sil"])
                p.op("vector", lambda v, c_=c_: v.tensor_mul(h2[:, 512 * c_:512 * (c_ + 1)], sil[:],
                                                             h[:, 512 * (c_ + 12):512 * (c_ + 13)]),
                     reads=["sil", "h"], writes=["h2"])
            for st_ in range(4):
                for oc in range(2):
                    def mmpj(t, st_=st_, oc=oc):
                        for fc in range(12):
                            ins = t.matmul(ps_o[:], h2[:, 512 * fc + 128 * st_:512 * fc + 128 * (st_ + 1)],
                                           wpj[:, 1024 * fc + 512 * oc:1024 * fc + 512 * (oc + 1)],
                                           start=(fc == 0), stop=(fc == 11))
                        return ins
                    p.op("tensor", mmpj, reads=["wpj", "h2"], writes=["ps_o"])
                    p.op("vector", lambda v: v.tensor_copy(otile[:], ps_o[:]), reads=["ps_o"], writes=["otile"])
                    p.op("sync", lambda s, tc=tc, st_=st_, oc=oc: s.dma_start(
                        out=exp_out[512 * tc + 128 * st_:512 * tc + 128 * (st_ + 1), 512 * oc:512 * (oc + 1)],
                        in_=otile[:]), reads=["otile"], writes=[], dma=True)
        p.finish("vector")

def emit_moe_combine(p, nc, pfx, xf, eo_full, gidx1_d, gidx2_d, vw1_d, vw2_d, xfo):
    stack = ExitStack()
    with stack:
        sb = lambda name, shape, dt: stack.enter_context(nc.sbuf_tensor(pfx + name, shape, dt))
        psumt = lambda name, shape: stack.enter_context(nc.psum_tensor(pfx + name, [shape[0], 512], F32)).ap()[0:shape[0], 0:shape[1]]
        xf_sb = sb("xf_sb", [128, 8 * T], F32)
        g1 = sb("g1", [128, 32], I32)
        g2 = sb("g2", [128, 32], I32)
        v1 = sb("v1", [128, 32], F32)
        v2 = sb("v2", [128, 32], F32)
        r1 = sb("r1", [128, D], BF16)
        r2 = sb("r2", [128, D], BF16)
        ctile = sb("ctile", [128, D], F32)
        ident = sb("ident", [128, 128], F32)
        ps_T = psumt("ps_T", [128, 128])

        emit_identity(p, ident)
        for dc in range(8):
            p.op("sync", lambda s, dc=dc: s.dma_start(out=xf_sb[:, 4096 * dc:4096 * (dc + 1)],
                                                      in_=xf[128 * dc:128 * (dc + 1), :]),
                 writes=["xf"], dma=True)
        p.op("sync", lambda s: s.dma_start(out=g1[:], in_=gidx1_d[:]), writes=["g1"], dma=True)
        p.op("sync", lambda s: s.dma_start(out=g2[:], in_=gidx2_d[:]), writes=["g2"], dma=True)
        p.op("sync", lambda s: s.dma_start(out=v1[:], in_=vw1_d[:]), writes=["v1"], dma=True)
        p.op("sync", lambda s: s.dma_start(out=v2[:], in_=vw2_d[:]), writes=["v2"], dma=True)
        for it in range(32):
            p.op("gpsimd", lambda g, it=it: g.indirect_dma_start(
                out=r1[:], out_offset=None, in_=eo_full,
                in_offset=IndirectOffsetOnAxis(g1[:, it:it + 1], 0)),
                reads=["g1"], writes=["r1"], dma=True)
            p.op("gpsimd", lambda g, it=it: g.indirect_dma_start(
                out=r2[:], out_offset=None, in_=eo_full,
                in_offset=IndirectOffsetOnAxis(g2[:, it:it + 1], 0)),
                reads=["g2"], writes=["r2"], dma=True)
            p.op("vector", lambda v, it=it: v.tensor_single_scalar(ctile[:], r1[:], v1[:, it:it + 1], ALU.mult),
                 reads=["r1", "v1"], writes=["ctile"])
            p.op("vector", lambda v, it=it: v.scalar_tensor_tensor(ctile[:], r2[:], v2[:, it:it + 1],
                                                                   ctile[:], ALU.mult, ALU.add),
                 reads=["r2", "v2", "ctile"], writes=["ctile"])
            for dc in range(8):
                p.op("tensor", lambda t, dc=dc: t.transpose(ps_T[:], ctile[:, 128 * dc:128 * (dc + 1)], ident[:]),
                     reads=["ctile", "ident"], writes=["ps_T"])
                p.op("vector", lambda v, it=it, dc=dc: v.scalar_tensor_tensor(
                    xf_sb[:, 4096 * dc + 128 * it:4096 * dc + 128 * (it + 1)], ps_T[:], 0.5,
                    xf_sb[:, 4096 * dc + 128 * it:4096 * dc + 128 * (it + 1)], ALU.mult, ALU.add),
                    reads=["ps_T", "xf"], writes=["xf"])
        for dc in range(8):
            p.op("sync", lambda s, dc=dc: s.dma_start(out=xfo[128 * dc:128 * (dc + 1), :],
                                                      in_=xf_sb[:, 4096 * dc:4096 * (dc + 1)]),
                 reads=["xf"], writes=[], dma=True)
        p.finish("vector")

def emit_l0qkv(p, nc, pfx, xf, g1, b1, Wq, bq, Wk, bk, Wv, bv, xin0, Qd, Kd, Vd):
    stack = ExitStack()
    with stack:
        sb = lambda name, shape, dt: stack.enter_context(nc.sbuf_tensor(pfx + name, shape, dt))
        psumt = lambda name, shape: stack.enter_context(nc.psum_tensor(pfx + name, [shape[0], 512], F32)).ap()[0:shape[0], 0:shape[1]]
        xn = sb("xn", [128, 8 * T], BF16)
        hh = sb("hh", [128, 8 * T], BF16)
        xbuf = sb("xbuf", [128, 2 * 512], F32)
        gcol = sb("gcol", [128, 8], F32)
        bcol = sb("bcol", [128, 8], F32)
        qcol = sb("qcol", [128, 8], F32)
        kcol = sb("kcol", [128, 8], F32)
        bvrow = sb("bvrow", [1, D], BF16)
        wblk = sb("wblk", [128, 2 * 8 * 128], BF16)
        wvblk = sb("wvblk", [128, 8 * 512], BF16)
        ones1_bf = sb("ones1_bf", [1, 128], BF16)
        ot = sb("ot", [128, 512], BF16)
        ps_q = psumt("ps_q", [128, 512])
        ps_v = psumt("ps_v", [128, 512])

        p.op("vector", lambda v: v.memset(ones1_bf[:], 1.0), writes=["ones1bf"])
        emit_col_load(p, gcol, "g1c", g1, 8)
        emit_col_load(p, bcol, "b1c", b1, 8)
        emit_col_load(p, qcol, "bqc", bq, 8)
        emit_col_load(p, kcol, "bkc", bk, 8)
        p.op("gpsimd", lambda g: g.dma_start(out=bvrow[:], in_=bv.rearrange("(o e) -> o e", o=1)),
             writes=["bvrow"], dma=True)

        state = {"i": 0}
        def srcf(dc, tsl):
            slot = state["i"] % 2
            state["i"] += 1
            key = "xbuf%d" % slot
            ap = xbuf[:, 512 * slot:512 * (slot + 1)]
            p.op("sync", lambda s, dc=dc, tsl=tsl, ap=ap: s.dma_start(
                out=ap, in_=xf[128 * dc:128 * (dc + 1), tsl]), writes=[key], dma=True)
            return ap, key

        def dstf(dc, tsl):
            return xn[:, 4096 * dc + tsl.start:4096 * dc + tsl.stop], "xn"

        with ExitStack() as lnstack:
            ws = alloc_ln_ws(nc, lnstack, tag=pfx, wide_x32=True)
            emit_ln_setup(p, ws)
            def store_xin0(tc):
                tsl = slice(512 * tc, 512 * (tc + 1))
                for dc in range(8):
                    p.op("sync", lambda s, dc=dc, tsl=tsl: s.dma_start(
                        out=xin0[128 * dc:128 * (dc + 1), tsl],
                        in_=ws["x32"][:, 512 * dc:512 * (dc + 1)]),
                        reads=["lnx32"], writes=[], dma=True)
            emit_ln(p, ws, srcf, dstf, after_tc=store_xin0)
            for e in ENGINES:
                p.barrier_wait(e, ["xn", "sx", "sxx", "lnRb", "lnMRb", "lnsq", "lnmsq", "lnpsb", "psrow"])
        for dc in range(8):
            p.op("vector", lambda v, dc=dc: v.tensor_scalar(
                hh[:, 4096 * dc:4096 * (dc + 1)], xn[:, 4096 * dc:4096 * (dc + 1)],
                gcol[:, dc:dc + 1], bcol[:, dc:dc + 1], ALU.mult, ALU.add),
                reads=["xn", "g1c", "b1c"], writes=["hh"])
        for W, bc, bkey, outd, name in ((Wq, qcol, "bqc", Qd, "q"), (Wk, kcol, "bkc", Kd, "k")):
            for oc in range(8):
                slot = oc % 2
                wkey = "wblk%d" % slot
                p.op("gpsimd", lambda g, W=W, oc=oc, slot=slot: g.dma_start(
                    out=wblk[:, 1024 * slot:1024 * (slot + 1)].rearrange("p (dc f) -> p dc f", dc=8),
                    in_=W[:, 128 * oc:128 * (oc + 1)].rearrange("(dc p) f -> p dc f", p=128)),
                    writes=[wkey], dma=True)
                for tc in range(8):
                    tsl = slice(512 * tc, 512 * (tc + 1))
                    mm_group(p, ps_q[:], "ps_q",
                             [(wblk[:, 1024 * slot + 128 * dc:1024 * slot + 128 * (dc + 1)],
                               hh[:, 4096 * dc + tsl.start:4096 * dc + tsl.stop]) for dc in range(8)],
                             [wkey, "hh"])
                    p.op("scalar", lambda s, bc=bc, oc=oc: s.activation(
                        out=ot[:], in_=ps_q[:], func=AF.Identity, bias=bc[:, oc:oc + 1], scale=1.0),
                        reads=["ps_q", bkey], writes=["ot"])
                    p.op("sync", lambda s, outd=outd, oc=oc, tsl=tsl: s.dma_start(
                        out=outd[128 * oc:128 * (oc + 1), tsl], in_=ot[:]),
                        reads=["ot"], writes=[], dma=True)
        for vc in range(2):
            p.op("gpsimd", lambda g, vc=vc: g.dma_start(
                out=wvblk[:].rearrange("p (dc f) -> p dc f", dc=8),
                in_=Wv[:, 512 * vc:512 * (vc + 1)].rearrange("(dc p) f -> p dc f", p=128)),
                writes=["wvblk"], dma=True)
            for tt in range(32):
                def mmv(t, vc=vc, tt=tt):
                    for dc in range(8):
                        t.matmul(ps_v[:], hh[:, 4096 * dc + 128 * tt:4096 * dc + 128 * (tt + 1)],
                                 wvblk[:, 512 * dc:512 * (dc + 1)], start=(dc == 0), stop=False)
                    return t.matmul(ps_v[:], ones1_bf[:], bvrow[0:1, 512 * vc:512 * (vc + 1)],
                                    start=False, stop=True)
                p.op("tensor", mmv, reads=["hh", "wvblk", "ones1bf", "bvrow"], writes=["ps_v"])
                p.op("vector", lambda v: v.tensor_copy(ot[:], ps_v[:]), reads=["ps_v"], writes=["ot"])
                p.op("sync", lambda s, vc=vc, tt=tt: s.dma_start(
                    out=Vd[128 * tt:128 * (tt + 1), 512 * vc:512 * (vc + 1)], in_=ot[:]),
                    reads=["ot"], writes=[], dma=True)
        p.finish("vector")

def emit_l0attn(p, nc, pfx, xin0, Qd, Kd, Vd, Wo, bo, xin1):
    stack = ExitStack()
    NH = 16
    with stack:
        sb = lambda name, shape, dt: stack.enter_context(nc.sbuf_tensor(pfx + name, shape, dt))
        psumt = lambda name, shape: stack.enter_context(nc.psum_tensor(pfx + name, [shape[0], 512], F32)).ap()[0:shape[0], 0:shape[1]]
        attnT = sb("attnT", [128, 8 * T], BF16)
        xinb = sb("xinb", [128, 512], F32)
        Qh = [sb("Qh%d" % i, [64, T], BF16) for i in range(2)]
        Kh = [sb("Kh%d" % i, [64, T], BF16) for i in range(2)]
        Vh = [sb("Vh%d" % i, [128, 32 * 64], BF16) for i in range(2)]
        scores = [sb("scores%d" % i, [128, T], F32) for i in range(2)]
        nb = [sb("nb%d" % i, [128, 1], F32) for i in range(2)]
        rsc = [sb("rsc%d" % i, [128, 32], F32) for i in range(2)]
        Et = [sb("Et%d" % i, [128, 512], F32) for i in range(2)]
        Eb = [sb("Eb%d" % i, [128, 512], BF16) for i in range(2)]
        ETb = [sb("ETb%d" % i, [128, 4 * 128], BF16) for i in range(2)]
        masks = sb("masks", [128, 4 * 512], F32)
        rcol = sb("rcol", [128, 1], F32)
        atile = sb("atile", [128, 64], F32)
        ident = sb("ident", [128, 128], F32)
        identb = sb("identb", [128, 128], BF16)
        obcol = sb("obcol", [128, 8], F32)
        woblk = sb("woblk", [128, 2 * 8 * 128], BF16)
        ot = sb("ot", [128, 512], F32)
        ps_s = [psumt("ps_s%d" % i, [128, 512]) for i in range(2)]
        ps_T = psumt("ps_T", [128, 128], )
        ps_av = [psumt("ps_av%d" % i, [128, 64]) for i in range(2)]
        ps_o = psumt("ps_o", [128, 512])

        emit_identity(p, ident)
        p.op("scalar", lambda s: s.activation(out=identb[:], in_=ident[:], func=AF.Identity),
             reads=["ident"], writes=["identb"])
        emit_col_load(p, obcol, "obc", bo, 8)
        for v_ in range(4):
            p.op("gpsimd", lambda g, v_=v_: g.memset(masks[:, 512 * v_:512 * (v_ + 1)], 0.0),
                 writes=["masks"])
            p.op("gpsimd", lambda g, v_=v_: g.affine_select(
                out=masks[:, 512 * v_:512 * (v_ + 1)], in_=masks[:, 512 * v_:512 * (v_ + 1)],
                compare_op=ALU.is_ge, fill=-1e30, base=128 * v_,
                pattern=[[-1, 512]], channel_multiplier=1),
                reads=["masks"], writes=["masks"])

        def load_head(h):
            hp = h % 2
            p.op("sync", lambda s, h=h, hp=hp: s.dma_start(out=Qh[hp][:], in_=Qd[64 * h:64 * (h + 1), :]),
                 writes=["Qh%d" % hp], dma=True)
            p.op("sync", lambda s, h=h, hp=hp: s.dma_start(out=Kh[hp][:], in_=Kd[64 * h:64 * (h + 1), :]),
                 writes=["Kh%d" % hp], dma=True)
            p.op("sync", lambda s, h=h, hp=hp: s.dma_start(
                out=Vh[hp][:].rearrange("p (a v) -> p a v", a=32),
                in_=Vd[:, 64 * h:64 * (h + 1)].rearrange("(a p) v -> p a v", p=128)),
                writes=["Vh%d" % hp], dma=True)

        def stageA(h, qt):
            hp, qp = h % 2, qt % 2
            nkc = (128 * qt) // 512 + 1
            kd = nkc - 1
            voff = qt % 4
            for kc in range(nkc):
                cp = kc % 2
                p.op("tensor", lambda t, hp=hp, qt=qt, kc=kc, cp=cp: t.matmul(
                    ps_s[cp][:], Qh[hp][:, 128 * qt:128 * (qt + 1)], Kh[hp][:, 512 * kc:512 * (kc + 1)],
                    start=True, stop=True), reads=["Qh%d" % hp, "Kh%d" % hp], writes=["ps_s%d" % cp])
                if kc == kd:
                    p.op("vector", lambda v, cp=cp, voff=voff: v.tensor_add(
                        ps_s[cp][:], ps_s[cp][:], masks[:, 512 * voff:512 * (voff + 1)]),
                        reads=["ps_s%d" % cp, "masks"], writes=["ps_s%d" % cp])
                p.op("scalar", lambda s, qp=qp, kc=kc, cp=cp: s.activation(
                    out=scores[qp][:, 512 * kc:512 * (kc + 1)], in_=ps_s[cp][:], func=AF.Identity),
                    reads=["ps_s%d" % cp], writes=["scores%d" % qp])
            p.op("vector", lambda v, qp=qp, nkc=nkc: v.tensor_reduce(
                nb[qp][:], scores[qp][:, 0:512 * nkc], AX.X, ALU.max),
                reads=["scores%d" % qp], writes=["nb%d" % qp])
            p.op("vector", lambda v, qp=qp: v.tensor_scalar_mul(nb[qp][:], nb[qp][:], -0.125),
                 reads=["nb%d" % qp], writes=["nb%d" % qp])

        def stageB(h, qt):
            hp, qp = h % 2, qt % 2
            nkc = (128 * qt) // 512 + 1
            kd = nkc - 1
            for kc in range(nkc):
                cp = kc % 2
                p.op("scalar", lambda s, qp=qp, kc=kc, cp=cp: s.activation(
                    out=Et[cp][:], in_=scores[qp][:, 512 * kc:512 * (kc + 1)], func=AF.Exp,
                    bias=nb[qp][:, 0:1], scale=0.125, accum_out=rsc[qp][:, kc:kc + 1]),
                    reads=["scores%d" % qp, "nb%d" % qp], writes=["Et%d" % cp, "rsc%d" % qp])
                nsub = 4 if kc < kd else ((128 * (qt + 1) - 512 * kd + 127) // 128)
                for sj in range(nsub):
                    p.op("tensor", lambda t, cp=cp, sj=sj: t.transpose(
                        ps_T[:], Et[cp][:, 128 * sj:128 * (sj + 1)], ident[:]),
                        reads=["Et%d" % cp, "ident"], writes=["ps_T"])
                    p.op("vector", lambda v, cp=cp, sj=sj: v.tensor_copy(
                        ETb[cp][:, 128 * sj:128 * (sj + 1)], ps_T[:]),
                        reads=["ps_T"], writes=["ETb%d" % cp])
                def mmpv(t, hp=hp, qp=qp, cp=cp, kc=kc, nsub=nsub, first=(kc == 0), last=(kc == nkc - 1)):
                    ins = None
                    for sj in range(nsub):
                        ins = t.matmul(ps_av[qp][:], ETb[cp][:, 128 * sj:128 * (sj + 1)],
                                       Vh[hp][:, (4 * kc + sj) * 64:(4 * kc + sj + 1) * 64],
                                       start=(first and sj == 0), stop=(last and sj == nsub - 1))
                    return ins
                p.op("tensor", mmpv, reads=["ETb%d" % cp, "Vh%d" % hp], writes=["ps_av%d" % qp])
            p.op("vector", lambda v, qp=qp, nkc=nkc: v.tensor_reduce(rcol[:], rsc[qp][:, 0:nkc], AX.X, ALU.add),
                 reads=["rsc%d" % qp], writes=["rcol"])
            p.op("vector", lambda v: v.reciprocal(rcol[:], rcol[:]), reads=["rcol"], writes=["rcol"])
            p.op("scalar", lambda s, qp=qp: s.activation(out=atile[:], in_=ps_av[qp][:], func=AF.Identity,
                                                         scale=rcol[:, 0:1]),
                 reads=["ps_av%d" % qp, "rcol"], writes=["atile"])
            p.op("tensor", lambda t: t.transpose(ps_T[0:64, 0:128], atile[:], ident[:]),
                 reads=["atile", "ident"], writes=["ps_T"])
            ro = 64 * (h % 2)
            cbase = 4096 * (h // 2) + 128 * qt
            if ro == 0:
                p.op("vector", lambda v, cbase=cbase: v.tensor_copy(
                    attnT[0:64, cbase:cbase + 128], ps_T[0:64, 0:128]),
                    reads=["ps_T"], writes=["attnT"])
            else:
                p.op("vector", lambda v, cbase=cbase: v.tensor_copy(
                    attnT[64:128, cbase:cbase + 128], ps_T[0:64, 0:128]),
                    reads=["ps_T"], writes=["attnT"])

        import os
        _mode = os.environ.get("ATTN_MODE", "skew")
        load_head(0)
        prev = None
        for h in range(NH):
            if _mode != "skew" and h + 1 < NH:
                load_head(h + 1)
            for qt in range(32):
                stageA(h, qt)
                if _mode == "noskew":
                    stageB(h, qt)
                else:
                    if prev is not None:
                        stageB(*prev)
                    prev = (h, qt)
                if _mode == "skew" and qt == 1 and h + 1 < NH:
                    load_head(h + 1)
            if _mode == "flush" and prev is not None:
                stageB(*prev)
                prev = None
        if _mode == "skew" and prev is not None:
            stageB(*prev)

        for mc in range(8):
            slot = mc % 2
            wkey = "woblk%d" % slot
            p.op("gpsimd", lambda g, mc=mc, slot=slot: g.dma_start(
                out=woblk[:, 1024 * slot:1024 * (slot + 1)].rearrange("p (dc f) -> p dc f", dc=8),
                in_=Wo[:, 128 * mc:128 * (mc + 1)].rearrange("(dc p) f -> p dc f", p=128)),
                writes=[wkey], dma=True)
            for tc in range(8):
                tsl = slice(512 * tc, 512 * (tc + 1))
                mm_group(p, ps_o[:], "ps_o",
                         [(woblk[:, 1024 * slot + 128 * dc:1024 * slot + 128 * (dc + 1)],
                           attnT[:, 4096 * dc + tsl.start:4096 * dc + tsl.stop]) for dc in range(8)],
                         [wkey, "attnT"])
                p.op("scalar", lambda s, mc=mc: s.activation(out=ps_o[:], in_=ps_o[:], func=AF.Identity,
                                                             bias=obcol[:, mc:mc + 1], scale=1.0),
                     reads=["ps_o", "obc"], writes=["ps_o"])
                p.op("sync", lambda s, mc=mc, tsl=tsl: s.dma_start(
                    out=xinb[:], in_=xin0[128 * mc:128 * (mc + 1), tsl]), writes=["xinb"], dma=True)
                p.op("vector", lambda v: v.tensor_add(ot[:], ps_o[:], xinb[:]),
                    reads=["ps_o", "xinb"], writes=["ot"])
                p.op("sync", lambda s, mc=mc, tsl=tsl: s.dma_start(
                    out=xin1[128 * mc:128 * (mc + 1), tsl], in_=ot[:]),
                    reads=["ot"], writes=[], dma=True)
        p.finish("vector")

def emit_l0mix_a(p, nc, pfx, xin, g2, b2, in_w, in_b, conv_w, conv_b, xp_w, xp_b, Bm, Cm, res_d):
    stack = ExitStack()
    with stack:
        sb = lambda name, shape, dt: stack.enter_context(nc.sbuf_tensor(pfx + name, shape, dt))
        psumt = lambda name, shape: stack.enter_context(nc.psum_tensor(pfx + name, [shape[0], 512], F32)).ap()[0:shape[0], 0:shape[1]]
        xn = sb("xn", [128, 8 * T], BF16)
        xbuf = sb("xbuf", [128, 2 * 512], F32)
        gcol = sb("gcol", [128, 8], F32)
        bcol = sb("bcol", [128, 8], F32)
        inbcol = sb("inbcol", [128, 32], F32)
        cwcol = sb("cwcol", [128, 16 * 4], F32)
        cbcol = sb("cbcol", [128, 16], F32)
        xpw = sb("xpw", [128, 16 * 32], F32)
        xpbB = sb("xpbB", [16, 1], F32)
        xpbC = sb("xpbC", [16, 1], F32)
        xm = sb("xm", [128, 16 * 512], F32)
        xm2 = sb("xm2", [128, 16 * 512], F32)
        halo = sb("halo", [128, 16 * 4], F32)
        halo2 = sb("halo2", [128, 16 * 4], F32)
        xc = sb("xc", [128, 512], F32)
        resb = sb("resb", [128, 512], BF16)
        w1b = sb("w1b", [128, 2 * 8 * 128], BF16)
        BmT = sb("BmT", [16, 512], F32)
        CmT = sb("CmT", [16, 512], F32)
        ps_h = psumt("ps_h", [128, 512])
        ps_bc = psumt("ps_bc", [16, 512])

        emit_col_load(p, gcol, "gb2", g2, 8)
        emit_col_load(p, bcol, "gb2", b2, 8)
        emit_col_load(p, inbcol, "inb", in_b[:DI], 16)
        emit_col_load(p, cbcol, "cb", conv_b, 16)
        inbcol2 = sb("inbcol2", [128, 16], F32)
        emit_col_load(p, inbcol2, "inb2", in_b[DI:], 16)
        for cc in range(16):
            p.op("sync", lambda s, cc=cc: s.dma_start(
                out=cwcol[:, 4 * cc:4 * (cc + 1)],
                in_=conv_w[128 * cc:128 * (cc + 1), :]), writes=["cw"], dma=True)
        for cc in range(16):
            p.op("gpsimd", lambda g, cc=cc: g.dma_start(out=xpw[:, 32 * cc:32 * (cc + 1)],
                                                        in_=xp_w[128 * cc:128 * (cc + 1), :]),
                 writes=["xpw"], dma=True)
        p.op("sync", lambda s: s.dma_start(out=xpbB[:], in_=xp_b[0:16].rearrange("(p o) -> p o", o=1)),
             writes=["xpb"], dma=True)
        p.op("sync", lambda s: s.dma_start(out=xpbC[:], in_=xp_b[16:32].rearrange("(p o) -> p o", o=1)),
             writes=["xpb"], dma=True)

        state = {"i": 0}
        def srcf(dc, tsl):
            slot = state["i"] % 2
            state["i"] += 1
            key = "xbuf%d" % slot
            ap = xbuf[:, 512 * slot:512 * (slot + 1)]
            p.op("sync", lambda s, dc=dc, tsl=tsl, ap=ap: s.dma_start(
                out=ap, in_=xin[128 * dc:128 * (dc + 1), tsl]), writes=[key], dma=True)
            return ap, key

        def dstf(dc, tsl):
            return xn[:, 4096 * dc + tsl.start:4096 * dc + tsl.stop], "xn"

        def gb(dc):
            return gcol[:, dc:dc + 1], bcol[:, dc:dc + 1], "gb2"

        with ExitStack() as lnstack:
            ws = alloc_ln_ws(nc, lnstack, tag=pfx)
            emit_ln_setup(p, ws)
            emit_ln(p, ws, srcf, dstf, gamma_beta=gb)
            for e in ENGINES:
                p.barrier_wait(e, ["xn", "sx", "sxx", "lnRb", "lnMRb", "lnsq", "lnmsq", "lnpsb", "psrow"])

        p.op("vector", lambda v: v.memset(halo[:], 0.0), writes=["halo"])
        for tc in range(8):
            tsl = slice(512 * tc, 512 * (tc + 1))
            for fc in range(16):
                slot = fc % 2
                wkey = "w1b%d" % slot
                p.op("gpsimd", lambda g, fc=fc, slot=slot: g.dma_start(
                    out=w1b[:, 1024 * slot:1024 * (slot + 1)].rearrange("p (dc f) -> p dc f", dc=8),
                    in_=in_w[:, 128 * fc:128 * (fc + 1)].rearrange("(dc p) f -> p dc f", p=128)),
                    writes=[wkey], dma=True)
                mm_group(p, ps_h[:], "ps_h",
                         [(w1b[:, 1024 * slot + 128 * dc:1024 * slot + 128 * (dc + 1)],
                           xn[:, 4096 * dc + tsl.start:4096 * dc + tsl.stop]) for dc in range(8)],
                         [wkey, "xn"])
                p.op("scalar", lambda s, fc=fc: s.activation(
                    out=xm[:, 512 * fc:512 * (fc + 1)], in_=ps_h[:], func=AF.Identity,
                    bias=inbcol[:, fc:fc + 1], scale=1.0), reads=["ps_h", "inb"], writes=["xm"])
            for fc in range(16):
                slot = fc % 2
                wkey = "w1b%d" % slot
                p.op("gpsimd", lambda g, fc=fc, slot=slot: g.dma_start(
                    out=w1b[:, 1024 * slot:1024 * (slot + 1)].rearrange("p (dc f) -> p dc f", dc=8),
                    in_=in_w[:, DI + 128 * fc:DI + 128 * (fc + 1)].rearrange("(dc p) f -> p dc f", p=128)),
                    writes=[wkey], dma=True)
                mm_group(p, ps_h[:], "ps_h",
                         [(w1b[:, 1024 * slot + 128 * dc:1024 * slot + 128 * (dc + 1)],
                           xn[:, 4096 * dc + tsl.start:4096 * dc + tsl.stop]) for dc in range(8)],
                         [wkey, "xn"])
                p.op("scalar", lambda s, fc=fc: s.activation(
                    out=resb[:], in_=ps_h[:], func=AF.Identity,
                    bias=inbcol2[:, fc:fc + 1], scale=1.0), reads=["ps_h", "inb2"], writes=["resb"])
                p.op("sync", lambda s, fc=fc, tsl=tsl: s.dma_start(
                    out=res_d[128 * fc:128 * (fc + 1), tsl], in_=resb[:]),
                    reads=["resb"], writes=[], dma=True)
            for cc in range(16):
                co = 512 * cc
                p.op("vector", lambda v, cc=cc, co=co: v.tensor_single_scalar(
                    xc[:], xm[:, co:co + 512], cwcol[:, 4 * cc + 3:4 * cc + 4], ALU.mult),
                    reads=["xm", "cw"], writes=["xc"])
                for j in range(3):
                    sh = 3 - j
                    p.op("vector", lambda v, cc=cc, co=co, j=j, sh=sh: v.scalar_tensor_tensor(
                        xc[:, sh:512], xm[:, co:co + 512 - sh], cwcol[:, 4 * cc + j:4 * cc + j + 1],
                        xc[:, sh:512], ALU.mult, ALU.add),
                        reads=["xm", "cw", "xc"], writes=["xc"])
                    p.op("vector", lambda v, cc=cc, j=j, sh=sh: v.scalar_tensor_tensor(
                        xc[:, 0:sh], halo[:, 4 * cc + 3 - sh:4 * cc + 3], cwcol[:, 4 * cc + j:4 * cc + j + 1],
                        xc[:, 0:sh], ALU.mult, ALU.add),
                        reads=["halo", "cw", "xc"], writes=["xc"])
                p.op("vector", lambda v, cc=cc, co=co: v.tensor_copy(
                    halo2[:, 4 * cc:4 * cc + 3], xm[:, co + 509:co + 512]), reads=["xm"], writes=["halo2"])
                p.op("scalar", lambda s, cc=cc, co=co: s.activation(
                    out=xm2[:, co:co + 512], in_=xc[:], func=AF.Silu,
                    bias=cbcol[:, cc:cc + 1], scale=1.0), reads=["xc", "cb"], writes=["xm2"])
            p.op("vector", lambda v: v.tensor_copy(halo[:], halo2[:]), reads=["halo2"], writes=["halo"])
            mm_group(p, ps_bc[:], "ps_bc",
                     [(xpw[:, 32 * cc:32 * cc + 16], xm2[:, 512 * cc:512 * (cc + 1)]) for cc in range(16)],
                     ["xpw", "xm2"])
            p.op("scalar", lambda s: s.activation(out=BmT[:], in_=ps_bc[:], func=AF.Identity,
                                                  bias=xpbB[:, 0:1], scale=1.0),
                 reads=["ps_bc", "xpb"], writes=["BmT"])
            p.op("sync", lambda s, tsl=tsl: s.dma_start(out=Bm[:, tsl], in_=BmT[:]),
                 reads=["BmT"], writes=[], dma=True)
            mm_group(p, ps_bc[:], "ps_bc",
                     [(xpw[:, 32 * cc + 16:32 * cc + 32], xm2[:, 512 * cc:512 * (cc + 1)]) for cc in range(16)],
                     ["xpw", "xm2"])
            p.op("scalar", lambda s: s.activation(out=CmT[:], in_=ps_bc[:], func=AF.Identity,
                                                  bias=xpbC[:, 0:1], scale=1.0),
                 reads=["ps_bc", "xpb"], writes=["CmT"])
            p.op("sync", lambda s, tsl=tsl: s.dma_start(out=Cm[:, tsl], in_=CmT[:]),
                 reads=["CmT"], writes=[], dma=True)
        p.finish("vector")

def emit_l0mix_b(p, nc, pfx, xin, Bm, Cm, res_d, dtv, Amat, out_w, out_b, xino):
    stack = ExitStack()
    with stack:
        sb = lambda name, shape, dt: stack.enter_context(nc.sbuf_tensor(pfx + name, shape, dt))
        psumt = lambda name, shape: stack.enter_context(nc.psum_tensor(pfx + name, [shape[0], 512], F32)).ap()[0:shape[0], 0:shape[1]]
        xinb2 = sb("xinb2", [128, 512], F32)
        BmT = sb("BmT", [16, T], F32)
        CmT = sb("CmT", [16, T], F32)
        thbuf = sb("thbuf", [16, T + 1], F32)
        ubuf = sb("ubuf", [16, T + 1], F32)
        bhcbuf = sb("bhcbuf", [16, T + 1], F32)
        hsb = sb("hsb", [16, T], BF16)
        Abank = sb("Abank", [16, 16 * 16], F32)
        AT = sb("AT", [16, 16], F32)
        dtrow = sb("dtrow", [1, 16], F32)
        ecol = sb("ecol", [16, 1], F32)
        ones1_f16 = sb("ones1_f16", [1, 16], F32)
        obcol = sb("obcol", [128, 8], F32)
        ident = sb("ident", [128, 128], F32)
        resb = sb("resb", [128, 16 * 512], BF16)
        owr = sb("owr", [128, 16 * 1024], BF16)
        RTa = sb("RTa", [128, 16], F32)
        Rj = sb("Rj", [16, 8 * 128], BF16)
        onescol = sb("onescol", [128, 1], BF16)
        ot = sb("ot", [128, 512], F32)
        ps = psumt("ps", [16, 16])
        ps2 = psumt("ps2", [16, 1])
        ps_scan = psumt("ps_scan", [16, 2])
        ps_R = psumt("ps_R", [128, 128])
        ps_o = psumt("ps_o", [128, 512])

        emit_identity(p, ident)
        p.op("vector", lambda v: v.memset(ones1_f16[:], 1.0), writes=["ones116"])
        p.op("vector", lambda v: v.memset(onescol[:], 1.0), writes=["onescol"])
        p.op("sync", lambda s: s.dma_start(out=BmT[:], in_=Bm[:]), writes=["BmT"], dma=True)
        p.op("sync", lambda s: s.dma_start(out=CmT[:], in_=Cm[:]), writes=["CmT"], dma=True)
        p.op("sync", lambda s: s.dma_start(out=dtrow[:], in_=dtv[0:16].rearrange("(o e) -> o e", o=1)),
             writes=["dtrow"], dma=True)
        p.op("sync", lambda s: s.dma_start(out=AT[:], in_=Amat[:]), writes=["ATraw"], dma=True)
        emit_col_load(p, obcol, "obc", out_b, 8)
        p.op("scalar", lambda s: s.activation(out=dtrow[:], in_=dtrow[:], func=AF.Exp),
             reads=["dtrow"], writes=["dtrow"])
        p.op("tensor", lambda t: t.transpose(ps[:], AT[:], ident[0:16, 0:16]),
             reads=["ATraw", "ident"], writes=["psA"])
        p.op("vector", lambda v: v.tensor_copy(AT[:], ps[:]), reads=["psA"], writes=["AT"])
        for k in range(16):
            p.op("tensor", lambda t, k=k: t.matmul(ps2[:], ones1_f16[:], dtrow[0:1, k:k + 1],
                                                   start=True, stop=True),
                 reads=["dtrow", "ones116"], writes=["ps2"])
            p.op("vector", lambda v: v.tensor_copy(ecol[:], ps2[:]), reads=["ps2"], writes=["ecol"])
            p.op("scalar", lambda s, k=k: s.activation(out=Abank[:, 16 * k:16 * (k + 1)], in_=AT[:],
                                                       func=AF.Exp, scale=ecol[:, 0:1]),
                 reads=["AT", "ecol"], writes=["Abank"])
        p.op("vector", lambda v: v.memset(thbuf[:, 0:1], 0.0), writes=["thbuf"])
        p.op("vector", lambda v: v.memset(bhcbuf[:, 0:1], 0.0), writes=["bhcbuf"])
        for t in range(1, T + 1):
            k = (t - 1) % 16
            p.op("vector", lambda v, t=t: v.scalar_tensor_tensor(
                ubuf[:, t:t + 1], bhcbuf[:, t - 1:t], BmT[:, t - 1:t], CmT[:, t - 1:t],
                ALU.mult, ALU.add), reads=["bhcbuf", "BmT", "CmT"], writes=["ubuf"])
            p.op("vector", lambda v, t=t: v.scalar_tensor_tensor(
                bhcbuf[:, t:t + 1], thbuf[:, t - 1:t], BmT[:, t - 1:t], ubuf[:, t:t + 1],
                ALU.mult, ALU.add), reads=["thbuf", "BmT", "ubuf"], writes=["bhcbuf"])
            def mmscan(t_eng, t=t, k=k):
                t_eng.matmul(ps_scan[:, 0:1], Abank[:, 16 * k:16 * (k + 1)], thbuf[:, t - 1:t],
                             start=True, stop=False)
                return t_eng.matmul(ps_scan[:, 0:1], Abank[:, 16 * k:16 * (k + 1)], bhcbuf[:, t - 1:t],
                                    start=False, stop=True)
            p.op("tensor", mmscan, reads=["Abank", "thbuf", "bhcbuf"], writes=["ps_scan"])
            p.op("scalar", lambda s, t=t: s.activation(out=thbuf[:, t:t + 1], in_=ps_scan[:, 0:1],
                                                       func=AF.Tanh),
                 reads=["ps_scan"], writes=["thbuf"])
        p.op("vector", lambda v: v.tensor_add(hsb[:], thbuf[:, 1:T + 1], bhcbuf[:, 1:T + 1]),
             reads=["thbuf", "bhcbuf"], writes=["hsb"])
        p.op("gpsimd", lambda g: g.dma_start(
            out=owr[:].rearrange("p (cc f) -> p cc f", cc=16),
            in_=out_w.rearrange("(cc p) f -> p cc f", p=128)), writes=["owr"], dma=True)
        for mc in range(8):
            for j in range(16):
                p.op("tensor", lambda t, mc=mc, j=j: t.matmul(
                    ps_R[:, 0:1], owr[:, 1024 * j + 128 * mc:1024 * j + 128 * (mc + 1)], onescol[:],
                    start=True, stop=True), reads=["owr", "onescol"], writes=["ps_R"])
                p.op("vector", lambda v, j=j: v.tensor_copy(RTa[:, j:j + 1], ps_R[:, 0:1]),
                     reads=["ps_R"], writes=["RTa"])
            p.op("tensor", lambda t: t.transpose(ps_R[0:16, 0:128], RTa[:], ident[:]),
                 reads=["RTa", "ident"], writes=["ps_R"])
            p.op("vector", lambda v, mc=mc: v.tensor_copy(Rj[0:16, 128 * mc:128 * (mc + 1)],
                                                          ps_R[0:16, 0:128]),
                 reads=["ps_R"], writes=["Rj"])
        for tc in range(8):
            tsl = slice(512 * tc, 512 * (tc + 1))
            p.op("sync", lambda s, tsl=tsl: s.dma_start(
                out=resb[:].rearrange("p (cc f) -> p cc f", cc=16),
                in_=res_d[:, tsl].rearrange("(cc p) f -> p cc f", p=128)),
                writes=["resb"], dma=True)
            for mc in range(8):
                def mmo(t, mc=mc, tsl=tsl):
                    for cc in range(16):
                        t.matmul(ps_o[:], owr[:, 1024 * cc + 128 * mc:1024 * cc + 128 * (mc + 1)],
                                 resb[:, 512 * cc:512 * (cc + 1)], start=(cc == 0), stop=False)
                    return t.matmul(ps_o[:], Rj[0:16, 128 * mc:128 * (mc + 1)], hsb[0:16, tsl],
                                    start=False, stop=True)
                p.op("tensor", mmo, reads=["owr", "resb", "Rj", "hsb"], writes=["ps_o"])
                p.op("scalar", lambda s, mc=mc: s.activation(out=ps_o[:], in_=ps_o[:], func=AF.Identity,
                                                             bias=obcol[:, mc:mc + 1], scale=1.0),
                     reads=["ps_o", "obc"], writes=["ps_o"])
                p.op("sync", lambda s, mc=mc, tsl=tsl: s.dma_start(
                    out=xinb2[:], in_=xin[128 * mc:128 * (mc + 1), tsl]), writes=["xinb2"], dma=True)
                p.op("vector", lambda v: v.tensor_add(ot[:], ps_o[:], xinb2[:]),
                    reads=["ps_o", "xinb2"], writes=["ot"])
                p.op("sync", lambda s, mc=mc, tsl=tsl: s.dma_start(
                    out=xino[128 * mc:128 * (mc + 1), tsl], in_=ot[:]),
                    reads=["ot"], writes=[], dma=True)
        p.finish("vector")

def emit_l0ffn_inner(p, nc, pfx, xin, xf, g3, b3, fc1w, fc1b, fc2w, fc2b, xfo):
    stack = ExitStack()
    with stack:
        sb = lambda name, shape, dt: stack.enter_context(nc.sbuf_tensor(pfx + name, shape, dt))
        psumt = lambda name, shape: stack.enter_context(nc.psum_tensor(pfx + name, [shape[0], 512], F32)).ap()[0:shape[0], 0:shape[1]]
        xn = sb("xn", [128, 8 * T], BF16)
        xbuf = sb("xbuf", [128, 2 * 512], F32)
        xinb = sb("xinb", [128, 512], F32)
        gcol = sb("gcol", [128, 8], F32)
        bcol = sb("bcol", [128, 8], F32)
        b1col = sb("b1col", [128, 32], F32)
        b2col = sb("b2col", [128, 8], F32)
        ps_h = psumt("ps_h", [128, 512])
        ps_o = psumt("ps_o", [128, 512])

        emit_col_load(p, gcol, "gb3", g3, 8)
        emit_col_load(p, bcol, "gb3", b3, 8)
        emit_col_load(p, b1col, "b1c", fc1b, 32)
        emit_col_load(p, b2col, "b2c", fc2b, 8)

        state = {"i": 0}
        def src(dc, tsl):
            slot = state["i"] % 2
            state["i"] += 1
            key = "xbuf%d" % slot
            ap = xbuf[:, 512 * slot:512 * (slot + 1)]
            p.op("sync", lambda s, dc=dc, tsl=tsl, ap=ap: s.dma_start(
                out=ap, in_=xin[128 * dc:128 * (dc + 1), tsl]), writes=[key], dma=True)
            return ap, key

        def dst(dc, tsl):
            return xn[:, 4096 * dc + tsl.start:4096 * dc + tsl.stop], "xn"

        def gb(dc):
            return gcol[:, dc:dc + 1], bcol[:, dc:dc + 1], "gb3"

        with ExitStack() as lnstack:
            ws = alloc_ln_ws(nc, lnstack, tag=pfx)
            emit_ln_setup(p, ws)
            emit_ln(p, ws, src, dst, gamma_beta=gb)
            for e in ENGINES:
                p.barrier_wait(e, ["xn", "sx", "sxx", "lnRb", "lnMRb", "lnsq", "lnmsq", "lnpsb", "psrow"])

        with ExitStack() as est:
            esb = lambda name, shape, dt: est.enter_context(nc.sbuf_tensor(pfx + name, shape, dt))
            h = esb("h", [128, 32 * 512], BF16)
            w1b = esb("w1b", [128, 2 * 8 * 128], BF16)
            w2r = esb("w2r", [128, 32 * 1024], BF16)
            ot = esb("ot", [128, 512], F32)
            for fc in range(32):
                p.op("gpsimd", lambda g, fc=fc: g.dma_start(
                    out=w2r[:, 1024 * fc:1024 * (fc + 1)],
                    in_=fc2w[128 * fc:128 * (fc + 1), :]), writes=["w2r"], dma=True)
            for tc in range(8):
                tsl = slice(512 * tc, 512 * (tc + 1))
                for fc in range(32):
                    slot = fc % 2
                    wkey = "w1b%d" % slot
                    p.op("gpsimd", lambda g, fc=fc, slot=slot: g.dma_start(
                        out=w1b[:, 1024 * slot:1024 * (slot + 1)].rearrange("p (dc f) -> p dc f", dc=8),
                        in_=fc1w[:, 128 * fc:128 * (fc + 1)].rearrange("(dc p) f -> p dc f", p=128)),
                        writes=[wkey], dma=True)
                    mm_group(p, ps_h[:], "ps_h",
                             [(w1b[:, 1024 * slot + 128 * dc:1024 * slot + 128 * (dc + 1)],
                               xn[:, 4096 * dc + tsl.start:4096 * dc + tsl.stop]) for dc in range(8)],
                             [wkey, "xn"])
                    p.op("scalar", lambda s, fc=fc: s.activation(
                        out=h[:, 512 * fc:512 * (fc + 1)], in_=ps_h[:], func=AF.Gelu_apprx_tanh,
                        bias=b1col[:, fc:fc + 1], scale=1.0), reads=["ps_h", "b1c"], writes=["h"])
                for mc in range(8):
                    mm_group(p, ps_o[:], "ps_o",
                             [(w2r[:, 1024 * fc + 128 * mc:1024 * fc + 128 * (mc + 1)],
                               h[:, 512 * fc:512 * (fc + 1)]) for fc in range(32)],
                             ["w2r", "h"])
                    p.op("scalar", lambda s, mc=mc: s.activation(
                        out=ps_o[:], in_=ps_o[:], func=AF.Identity, bias=b2col[:, mc:mc + 1], scale=1.0),
                        reads=["ps_o", "b2c"], writes=["ps_o"])
                    p.op("sync", lambda s, mc=mc, tsl=tsl: s.dma_start(
                        out=xinb[:], in_=xin[128 * mc:128 * (mc + 1), tsl]), writes=["xinb"], dma=True)
                    p.op("vector", lambda v: v.tensor_add(ps_o[:], ps_o[:], xinb[:]),
                         reads=["ps_o", "xinb"], writes=["ps_o"])
                    p.op("sync", lambda s, mc=mc, tsl=tsl: s.dma_start(
                        out=ot[:], in_=xf[128 * mc:128 * (mc + 1), tsl]), writes=["xft"], dma=True)
                    p.op("vector", lambda v: v.tensor_add(ot[:], ot[:], ps_o[:]),
                         reads=["xft", "ps_o"], writes=["xft"])
                    p.op("sync", lambda s, mc=mc, tsl=tsl: s.dma_start(
                        out=xfo[128 * mc:128 * (mc + 1), tsl], in_=ot[:]),
                        reads=["xft"], writes=[], dma=True)
            p.finish("vector")

def emit_l1ffn(p, nc, pfx, xf, Wg, bg, W1, b1, W2, b2, xfo):
    stack = ExitStack()
    with stack:
        sb = lambda name, shape, dt: stack.enter_context(nc.sbuf_tensor(pfx + name, shape, dt))
        psumt = lambda name, shape: stack.enter_context(nc.psum_tensor(pfx + name, [shape[0], 512], F32)).ap()[0:shape[0], 0:shape[1]]
        xn = sb("xn", [128, 8 * T], BF16)
        acc = sb("acc", [128, 8 * T], BF16)
        wgt = sb("wgt", [128, 8 * 8], BF16)
        bgr = sb("bgr", [1, 8], BF16)
        b2r = sb("b2r", [8, D], BF16)
        iota8 = sb("iota8", [128, 8], F32)
        logits = sb("logits", [128, 8], F32)
        m1 = sb("m1", [128, 1], F32)
        eq0 = sb("eq0", [128, 8], F32)
        idxc = sb("idxc", [128, 1], F32)
        maskrows = sb("maskrows", [8, T], BF16)
        mrowsT = sb("mrowsT", [8, 128], F32)
        ident = sb("ident", [128, 128], F32)
        ones1_bf = sb("ones1_bf", [1, 128], BF16)
        xbuf = sb("xbuf", [128, 2 * 512], F32)
        ps_log = psumt("ps_log", [128, 8])
        ps_t = psumt("ps_t", [8, 128])

        emit_identity(p, ident)
        p.op("vector", lambda v: v.memset(ones1_bf[:], 1.0), writes=["ones1bf"])
        p.op("gpsimd", lambda g: g.iota(iota8[:], pattern=[[1, 8]], base=0, channel_multiplier=0,
                                        allow_small_or_imprecise_dtypes=True), writes=["iota8"])
        for dc in range(8):
            p.op("gpsimd", lambda g, dc=dc: g.dma_start(out=wgt[:, 8 * dc:8 * (dc + 1)],
                                                        in_=Wg[128 * dc:128 * (dc + 1), :]),
                 writes=["wgt"], dma=True)
        p.op("gpsimd", lambda g: g.dma_start(out=bgr[:], in_=bg.rearrange("(o e) -> o e", o=1)),
             writes=["bgr"], dma=True)
        p.op("gpsimd", lambda g: g.dma_start(out=b2r[:], in_=b2[:, :]), writes=["b2r"], dma=True)

        state = {"i": 0}
        def src(dc, tsl):
            slot = state["i"] % 2
            state["i"] += 1
            key = "xbuf%d" % slot
            ap = xbuf[:, 512 * slot:512 * (slot + 1)]
            p.op("sync", lambda s, dc=dc, tsl=tsl, ap=ap: s.dma_start(
                out=ap, in_=xf[128 * dc:128 * (dc + 1), tsl]), writes=[key], dma=True)
            return ap, key

        def dst(dc, tsl):
            return xn[:, 4096 * dc + tsl.start:4096 * dc + tsl.stop], "xn"

        with ExitStack() as lnstack:
            ws = alloc_ln_ws(nc, lnstack, tag=pfx)
            emit_ln_setup(p, ws)
            emit_ln(p, ws, src, dst)
            p.barrier_wait("vector", ["xn"])
            for e in ENGINES:
                p.barrier_wait(e, ["xn", "sx", "sxx", "lnRb", "lnMRb", "lnsq", "lnpsb", "psrow"])

        for it in range(T // 128):
            def mm_logits(t, it=it):
                for dc in range(8):
                    t.matmul(ps_log[:], xn[:, 4096 * dc + 128 * it: 4096 * dc + 128 * (it + 1)],
                             wgt[:, 8 * dc:8 * (dc + 1)], start=(dc == 0), stop=False)
                return t.matmul(ps_log[:], ones1_bf[:], bgr[:], start=False, stop=True)
            p.op("tensor", mm_logits, reads=["xn", "wgt", "bgr", "ones1bf"], writes=["ps_log"])
            p.op("vector", lambda v: v.tensor_copy(logits[:], ps_log[:]), reads=["ps_log"], writes=["logits"])
            p.op("vector", lambda v: v.tensor_reduce(m1[:], logits[:], AX.X, ALU.max), reads=["logits"], writes=["m1"])
            p.op("vector", lambda v: v.tensor_single_scalar(eq0[:], logits[:], m1[:, 0:1], ALU.is_ge),
                 reads=["logits", "m1"], writes=["eq0"])
            p.op("vector", lambda v: v.scalar_tensor_tensor(eq0[:], eq0[:], -1000.0, iota8[:], ALU.mult, ALU.add),
                 reads=["eq0", "iota8"], writes=["eq0"])
            p.op("vector", lambda v: v.tensor_reduce(idxc[:], eq0[:], AX.X, ALU.min),
                 reads=["eq0"], writes=["idxc"])
            p.op("vector", lambda v: v.tensor_scalar_add(idxc[:], idxc[:], 1000.0),
                 reads=["idxc"], writes=["idxc"])
            p.op("vector", lambda v: v.tensor_single_scalar(eq0[:], iota8[:], idxc[:, 0:1], ALU.is_equal),
                 reads=["iota8", "idxc"], writes=["eq0"])
            p.op("tensor", lambda t: t.transpose(ps_t[:], eq0[:], ident[:]), reads=["eq0", "ident"], writes=["ps_t"])
            p.op("vector", lambda v: v.tensor_copy(mrowsT[:], ps_t[:]), reads=["ps_t"], writes=["mrowsT"])
            p.op("vector", lambda v, it=it: v.tensor_copy(maskrows[:, 128 * it:128 * (it + 1)], mrowsT[:]),
                 reads=["mrowsT"], writes=["maskrows"])

        for dc in range(8):
            p.op("vector", lambda v, dc=dc: v.memset(acc[:, 4096 * dc:4096 * (dc + 1)], 0.0), writes=["acc"])

        with ExitStack() as est:
            esb = lambda name, shape, dt: est.enter_context(nc.sbuf_tensor(pfx + name, shape, dt))
            w1t = esb("w1t", [128, 8 * D], BF16)
            w2t = esb("w2t", [128, 8 * D], BF16)
            b1c = esb("b1c", [128, 8], F32)
            mrow_e = esb("mrow_e", [1, T], BF16)
            mb = esb("mb", [128, 512], BF16)
            h_e = esb("h_e", [128, 8 * 512], BF16)
            ps_h = psumt("ps_h", [128, 512])
            ps_o = psumt("ps_o", [128, 512])
            ps_m = psumt("ps_m", [128, 512])
            for e in range(8):
                p.op("gpsimd", lambda g, e=e: g.dma_start(
                    out=w1t[:].rearrange("p (dc f) -> p dc f", dc=8),
                    in_=W1[e].rearrange("(dc p) f -> p dc f", p=128)),
                    writes=["w1t"], dma=True)
                p.op("gpsimd", lambda g, e=e: g.dma_start(
                    out=w2t[:].rearrange("p (dc f) -> p dc f", dc=8),
                    in_=W2[e].rearrange("(dc p) f -> p dc f", p=128)),
                    writes=["w2t"], dma=True)
                for ic in range(8):
                    p.op("sync", lambda s, e=e, ic=ic: s.dma_start(
                        out=b1c[:, ic:ic + 1], in_=b1[e, 128 * ic:128 * (ic + 1)].rearrange("(p o) -> p o", o=1)),
                        writes=["b1c"], dma=True)
                p.op("gpsimd", lambda g, e=e: g.dma_start(out=mrow_e[:], in_=maskrows[e:e + 1, :]),
                     reads=["maskrows"], writes=["mrow_e"], dma=True)
                for tc in range(T // 512):
                    tsl = slice(512 * tc, 512 * (tc + 1))
                    p.op("tensor", lambda t, tsl=tsl: t.matmul(ps_m[:], ones1_bf[:], mrow_e[0:1, tsl],
                                                               start=True, stop=True),
                         reads=["mrow_e", "ones1bf"], writes=["ps_m"])
                    p.op("vector", lambda v: v.tensor_copy(mb[:], ps_m[:]), reads=["ps_m"], writes=["mb"])
                    for ic in range(8):
                        def mm_h(t, ic=ic, tsl=tsl):
                            for dc in range(8):
                                ins = t.matmul(ps_h[:], w1t[:, D * dc + 128 * ic: D * dc + 128 * (ic + 1)],
                                               xn[:, 4096 * dc + tsl.start:4096 * dc + tsl.stop],
                                               start=(dc == 0), stop=(dc == 7))
                            return ins
                        p.op("tensor", mm_h, reads=["w1t", "xn"], writes=["ps_h"])
                        p.op("scalar", lambda s, ic=ic: s.activation(
                            out=h_e[:, 512 * ic:512 * (ic + 1)], in_=ps_h[:], func=AF.Gelu,
                            bias=b1c[:, ic:ic + 1], scale=1.0), reads=["ps_h", "b1c"], writes=["h_e"])
                        p.op("vector", lambda v, ic=ic: v.tensor_mul(
                            h_e[:, 512 * ic:512 * (ic + 1)], h_e[:, 512 * ic:512 * (ic + 1)], mb[:]),
                            reads=["h_e", "mb"], writes=["h_e"])
                    for mc in range(8):
                        def mm_o(t, mc=mc):
                            for ic in range(8):
                                ins = t.matmul(ps_o[:], w2t[:, D * ic + 128 * mc: D * ic + 128 * (mc + 1)],
                                               h_e[:, 512 * ic:512 * (ic + 1)], start=(ic == 0), stop=(ic == 7))
                            return ins
                        p.op("tensor", mm_o, reads=["w2t", "h_e"], writes=["ps_o"])
                        p.op("vector", lambda v, mc=mc, tsl=tsl: v.tensor_add(
                            acc[:, 4096 * mc + tsl.start:4096 * mc + tsl.stop],
                            acc[:, 4096 * mc + tsl.start:4096 * mc + tsl.stop], ps_o[:]),
                            reads=["ps_o", "acc"], writes=["acc"])

            xf_out = esb("xf_out", [128, 512], F32)
            ps_b2 = psumt("ps_b2", [128, 512])
            for mc in range(8):
                for tc in range(T // 512):
                    tsl = slice(512 * tc, 512 * (tc + 1))
                    ap, key = src(mc, tsl)
                    p.op("tensor", lambda t, mc=mc, tsl=tsl: t.matmul(
                        ps_b2[:], b2r[:, 128 * mc:128 * (mc + 1)], maskrows[:, tsl],
                        start=True, stop=True), reads=["b2r", "maskrows"], writes=["ps_b2"])
                    p.op("vector", lambda v, mc=mc, tsl=tsl: v.tensor_add(
                        ps_b2[:], ps_b2[:], acc[:, 4096 * mc + tsl.start:4096 * mc + tsl.stop]),
                        reads=["ps_b2", "acc"], writes=["ps_b2"])
                    p.op("vector", lambda v, ap=ap: v.scalar_tensor_tensor(
                        xf_out[:], ps_b2[:], 0.5, ap, ALU.mult, ALU.add),
                        reads=["ps_b2", key], writes=["xf_out"])
                    p.op("sync", lambda s, mc=mc, tsl=tsl: s.dma_start(
                        out=xfo[128 * mc:128 * (mc + 1), tsl], in_=xf_out[:]),
                        reads=["xf_out"], writes=[], dma=True)
            p.finish("vector")

def emit_l1mix_a(p, nc, pfx, xf, W_in, conv_w, conv_b, W_x, W_dt, b_dt, u_d, z_d, dt_d, Bm_d, Cm_d):
    Win_bf = nc.dram_tensor(pfx + "Win_bf", [2 * DI, D], BF16, kind="Internal").ap()
    Wdt_bf = nc.dram_tensor(pfx + "Wdt_bf", [DI, DI], BF16, kind="Internal").ap()
    stack = ExitStack()
    with stack:
        sb = lambda name, shape, dt: stack.enter_context(nc.sbuf_tensor(pfx + name, shape, dt))
        psumt = lambda name, shape: stack.enter_context(nc.psum_tensor(pfx + name, [shape[0], 512], F32)).ap()[0:shape[0], 0:shape[1]]
        xn = sb("xn", [128, 8 * T], BF16)
        utc = sb("utc", [128, 16 * 512], BF16)
        xbuf = sb("xbuf", [128, 2 * 512], F32)
        cwcol = sb("cwcol", [128, 16 * 4], F32)
        cbcol = sb("cbcol", [128, 16], F32)
        bdtcol = sb("bdtcol", [128, 16], F32)
        onescol1 = sb("onescol1", [128, 1], F32)
        wpan = sb("wpan", [128, 2 * 8 * 512], BF16)
        wdtp = sb("wdtp", [128, 2 * 16 * 512], BF16)
        wxTb = sb("wxTb", [128, 16 * 32], BF16)
        wxtmp = sb("wxtmp", [32, DI // 2], F32)
        xm = sb("xm", [128, 512], F32)
        xc = sb("xc", [128, 512], F32)
        halo = sb("halo", [128, 16 * 4], F32)
        halo2 = sb("halo2", [128, 16 * 4], F32)
        sp1 = sb("sp1", [128, 512], F32)
        sp2 = sb("sp2", [128, 512], F32)
        dtt = sb("dtt", [128, 512], F32)
        ztile = sb("ztile", [128, 512], BF16)
        bcT = sb("bcT", [16, 512], F32)
        ident = sb("ident", [128, 128], F32)
        ps_h = psumt("ps_h", [128, 512])
        ps_bc = psumt("ps_bc", [16, 512])
        ps_T = psumt("ps_T", [128, 128])

        emit_identity(p, ident)
        p.op("vector", lambda v: v.memset(onescol1[:], 1.0), writes=["ones11"])
        for cc in range(16):
            p.op("sync", lambda s, cc=cc: s.dma_start(
                out=cwcol[:, 4 * cc:4 * (cc + 1)], in_=conv_w[128 * cc:128 * (cc + 1), :]),
                writes=["cw"], dma=True)
        emit_col_load(p, cbcol, "cb", conv_b, 16)
        emit_col_load(p, bdtcol, "bdt", b_dt, 16)
        for hh in range(2):
            p.op("sync", lambda s, hh=hh: s.dma_start(out=wxtmp[:], in_=W_x[:, 1024 * hh:1024 * (hh + 1)]),
                 writes=["wxtmp"], dma=True)
            for c8 in range(8):
                cc = 8 * hh + c8
                p.op("tensor", lambda t, c8=c8: t.transpose(ps_T[:, 0:32], wxtmp[:, 128 * c8:128 * (c8 + 1)],
                                                            ident[0:32, 0:32]),
                     reads=["wxtmp", "ident"], writes=["psWT"])
                p.op("vector", lambda v, cc=cc: v.tensor_copy(wxTb[:, 32 * cc:32 * (cc + 1)], ps_T[:, 0:32]),
                     reads=["psWT"], writes=["wxTb"])
        for rb in range(4):
            p.op("gpsimd", lambda g, rb=rb: g.dma_start(
                out=Win_bf[1024 * rb:1024 * (rb + 1), :], in_=W_in[1024 * rb:1024 * (rb + 1), :]),
                writes=["Win_bf"], dma=True)
            p.op("gpsimd", lambda g, rb=rb: g.dma_start(
                out=Wdt_bf[512 * rb:512 * (rb + 1), :], in_=W_dt[512 * rb:512 * (rb + 1), :]),
                writes=["Wdt_bf"], dma=True)

        state = {"i": 0}
        def srcf(dc, tsl):
            slot = state["i"] % 2
            state["i"] += 1
            key = "xbuf%d" % slot
            ap = xbuf[:, 512 * slot:512 * (slot + 1)]
            p.op("sync", lambda s, dc=dc, tsl=tsl, ap=ap: s.dma_start(
                out=ap, in_=xf[128 * dc:128 * (dc + 1), tsl]), writes=[key], dma=True)
            return ap, key

        def dstf(dc, tsl):
            return xn[:, 4096 * dc + tsl.start:4096 * dc + tsl.stop], "xn"

        with ExitStack() as lnstack:
            ws = alloc_ln_ws(nc, lnstack, tag=pfx)
            emit_ln_setup(p, ws)
            emit_ln(p, ws, srcf, dstf)
            for e in ENGINES:
                p.barrier_wait(e, ["xn", "sx", "sxx", "lnRb", "lnMRb", "lnsq", "lnmsq", "lnpsb", "psrow"])

        p.op("vector", lambda v: v.memset(halo[:], 0.0), writes=["halo"])
        for tc in range(8):
            tsl = slice(512 * tc, 512 * (tc + 1))
            for fc in range(32):
                fcg, sub = fc // 4, fc % 4
                slot = fcg % 2
                wkey = "wpan%d" % slot
                if sub == 0:
                    for dc in range(8):
                        p.op("sync", lambda s, fcg=fcg, dc=dc, slot=slot: s.dma_start(
                            out=wpan[:, 4096 * slot + 512 * dc:4096 * slot + 512 * (dc + 1)],
                            in_=Win_bf[512 * fcg:512 * (fcg + 1), 128 * dc:128 * (dc + 1)],
                            transpose=True),
                            reads=["Win_bf"], writes=[wkey], dma=True)
                mm_group(p, ps_h[:], "ps_h",
                         [(wpan[:, 4096 * slot + 512 * dc + 128 * sub:4096 * slot + 512 * dc + 128 * (sub + 1)],
                           xn[:, 4096 * dc + tsl.start:4096 * dc + tsl.stop]) for dc in range(8)],
                         [wkey, "xn"])
                if fc >= 16:
                    cc = fc - 16
                    p.op("vector", lambda v: v.tensor_copy(ztile[:], ps_h[:]), reads=["ps_h"], writes=["ztile"])
                    p.op("sync", lambda s, cc=cc, tsl=tsl: s.dma_start(
                        out=z_d[128 * cc:128 * (cc + 1), tsl], in_=ztile[:]),
                        reads=["ztile"], writes=[], dma=True)
                else:
                    cc = fc
                    p.op("vector", lambda v: v.tensor_copy(xm[:], ps_h[:]), reads=["ps_h"], writes=["xm"])
                    p.op("vector", lambda v, cc=cc: v.tensor_single_scalar(
                        xc[:], xm[:], cwcol[:, 4 * cc + 3:4 * cc + 4], ALU.mult),
                        reads=["xm", "cw"], writes=["xc"])
                    for j in range(3):
                        sh = 3 - j
                        p.op("vector", lambda v, cc=cc, j=j, sh=sh: v.scalar_tensor_tensor(
                            xc[:, sh:512], xm[:, 0:512 - sh], cwcol[:, 4 * cc + j:4 * cc + j + 1],
                            xc[:, sh:512], ALU.mult, ALU.add),
                            reads=["xm", "cw", "xc"], writes=["xc"])
                        p.op("vector", lambda v, cc=cc, j=j, sh=sh: v.scalar_tensor_tensor(
                            xc[:, 0:sh], halo[:, 4 * cc + 3 - sh:4 * cc + 3], cwcol[:, 4 * cc + j:4 * cc + j + 1],
                            xc[:, 0:sh], ALU.mult, ALU.add),
                            reads=["halo", "cw", "xc"], writes=["xc"])
                    p.op("vector", lambda v, cc=cc: v.tensor_copy(
                        halo2[:, 4 * cc:4 * cc + 3], xm[:, 509:512]), reads=["xm"], writes=["halo2"])
                    p.op("scalar", lambda s, cc=cc: s.activation(
                        out=utc[:, 512 * cc:512 * (cc + 1)], in_=xc[:], func=AF.Silu,
                        bias=cbcol[:, cc:cc + 1], scale=1.0), reads=["xc", "cb"], writes=["utc"])
                    p.op("sync", lambda s, cc=cc, tsl=tsl: s.dma_start(
                        out=u_d[128 * cc:128 * (cc + 1), tsl], in_=utc[:, 512 * cc:512 * (cc + 1)]),
                        reads=["utc"], writes=[], dma=True)
            p.op("vector", lambda v: v.tensor_copy(halo[:], halo2[:]), reads=["halo2"], writes=["halo"])
            mm_group(p, ps_bc[:], "ps_bc",
                     [(wxTb[:, 32 * cc:32 * cc + 16], utc[:, 512 * cc:512 * (cc + 1)]) for cc in range(16)],
                     ["wxTb", "utc"])
            p.op("vector", lambda v: v.tensor_copy(bcT[:], ps_bc[:]), reads=["ps_bc"], writes=["bcT"])
            p.op("sync", lambda s, tsl=tsl: s.dma_start(out=Bm_d[:, tsl], in_=bcT[:]),
                 reads=["bcT"], writes=[], dma=True)
            mm_group(p, ps_bc[:], "ps_bc",
                     [(wxTb[:, 32 * cc + 16:32 * cc + 32], utc[:, 512 * cc:512 * (cc + 1)]) for cc in range(16)],
                     ["wxTb", "utc"])
            p.op("vector", lambda v: v.tensor_copy(bcT[:], ps_bc[:]), reads=["ps_bc"], writes=["bcT"])
            p.op("sync", lambda s, tsl=tsl: s.dma_start(out=Cm_d[:, tsl], in_=bcT[:]),
                 reads=["bcT"], writes=[], dma=True)
            for c2 in range(16):
                c2g, sub2 = c2 // 4, c2 % 4
                slot = c2g % 2
                wkey = "wdtp%d" % slot
                if sub2 == 0:
                    for cc in range(16):
                        p.op("sync", lambda s, c2g=c2g, cc=cc, slot=slot: s.dma_start(
                            out=wdtp[:, 8192 * slot + 512 * cc:8192 * slot + 512 * (cc + 1)],
                            in_=Wdt_bf[512 * c2g:512 * (c2g + 1), 128 * cc:128 * (cc + 1)],
                            transpose=True),
                            reads=["Wdt_bf"], writes=[wkey], dma=True)
                mm_group(p, ps_h[:], "ps_h",
                         [(wdtp[:, 8192 * slot + 512 * cc + 128 * sub2:8192 * slot + 512 * cc + 128 * (sub2 + 1)],
                           utc[:, 512 * cc:512 * (cc + 1)]) for cc in range(16)],
                         [wkey, "utc"])
                p.op("scalar", lambda s, c2=c2: s.activation(out=sp1[:], in_=ps_h[:], func=AF.Abs,
                                                             bias=bdtcol[:, c2:c2 + 1], scale=1.0),
                     reads=["ps_h", "bdt"], writes=["sp1"])
                p.op("scalar", lambda s: s.activation(out=sp2[:], in_=sp1[:], func=AF.Exp, scale=-1.0),
                     reads=["sp1"], writes=["sp2"])
                p.op("scalar", lambda s: s.activation(out=sp2[:], in_=sp2[:], func=AF.Ln,
                                                      bias=onescol1[:, 0:1], scale=1.0),
                     reads=["sp2", "ones11"], writes=["sp2"])
                p.op("vector", lambda v, c2=c2: v.tensor_scalar(dtt[:], ps_h[:], bdtcol[:, c2:c2 + 1],
                                                                0.0, ALU.add, ALU.max),
                     reads=["ps_h", "bdt"], writes=["dtt"])
                p.op("vector", lambda v: v.tensor_add(dtt[:], dtt[:], sp2[:]),
                     reads=["dtt", "sp2"], writes=["dtt"])
                p.op("sync", lambda s, c2=c2, tsl=tsl: s.dma_start(
                    out=dt_d[128 * c2:128 * (c2 + 1), tsl], in_=dtt[:]),
                    reads=["dtt"], writes=[], dma=True)
        p.finish("vector")

def emit_l1mix_b(p, nc, pfx, xf, u_d, z_d, dt_d, Bm_d, Cm_d, A_log, Dp, W_out, xfo):
    stack = ExitStack()
    TCH = 1024
    with stack:
        sb = lambda name, shape, dt: stack.enter_context(nc.sbuf_tensor(pfx + name, shape, dt))
        psumt = lambda name, shape: stack.enter_context(nc.psum_tensor(pfx + name, [shape[0], 512], F32)).ap()[0:shape[0], 0:shape[1]]
        Bb = sb("Bb", [128, 16 * TCH], BF16)
        Cb = sb("Cb", [128, 16 * TCH], BF16)
        ygb = sb("ygb", [128, 16 * TCH], BF16)
        woT = sb("woT", [128, 16 * 8 * 128], BF16)
        Aneg = sb("Aneg", [128, 16 * 16], F32)
        Dcol = sb("Dcol", [128, 16], F32)
        s_state = sb("s_state", [128, 16 * 16], F32)
        y_acc = sb("y_acc", [128, TCH], F32)
        dtc = sb("dtc", [128, TCH], F32)
        uc = sb("uc", [128, TCH], BF16)
        zc = sb("zc", [128, TCH], BF16)
        dtu = sb("dtu", [128, TCH], BF16)
        dA = sb("dA", [128, TCH], BF16)
        dBu = sb("dBu", [128, TCH], BF16)
        s_sc = sb("s_sc", [128, TCH], BF16)
        rowf = sb("rowf", [1, TCH], F32)
        wtmpf = sb("wtmpf", [128, 128], F32)
        ones1_f = sb("ones1_f", [1, 128], F32)
        ident = sb("ident", [128, 128], F32)
        ot = sb("ot", [128, 512], F32)
        ps_b = psumt("ps_b", [128, 512])
        ps_T = psumt("ps_T", [128, 128])
        ps_del = [psumt("ps_del%d" % i, [128, 512]) for i in range(4)]

        emit_identity(p, ident)
        p.op("vector", lambda v: v.memset(ones1_f[:], 1.0), writes=["ones1f"])
        emit_col_load(p, Dcol, "Dc", Dp, 16)
        for cc in range(16):
            p.op("sync", lambda s, cc=cc: s.dma_start(out=Aneg[:, 16 * cc:16 * (cc + 1)],
                                                      in_=A_log[128 * cc:128 * (cc + 1), :]),
                 writes=["Aneg"], dma=True)
        p.op("scalar", lambda s: s.activation(out=Aneg[:], in_=Aneg[:], func=AF.Exp, scale=1.0),
             reads=["Aneg"], writes=["Aneg"])
        p.op("vector", lambda v: v.tensor_scalar_mul(Aneg[:], Aneg[:], -1.0), reads=["Aneg"], writes=["Aneg"])
        Wout_bf = nc.dram_tensor(pfx + "Wout_bf", [D, DI], BF16, kind="Internal").ap()
        for rb in range(2):
            p.op("gpsimd", lambda g, rb=rb: g.dma_start(
                out=Wout_bf[512 * rb:512 * (rb + 1), :], in_=W_out[512 * rb:512 * (rb + 1), :]),
                writes=["Wout_bf"], dma=True)
        for cc in range(16):
            for mg in range(2):
                p.op("sync", lambda s, cc=cc, mg=mg: s.dma_start(
                    out=woT[:, 1024 * cc + 512 * mg:1024 * cc + 512 * (mg + 1)],
                    in_=Wout_bf[512 * mg:512 * (mg + 1), 128 * cc:128 * (cc + 1)], transpose=True),
                    reads=["Wout_bf"], writes=["woT"], dma=True)
        p.op("vector", lambda v: v.memset(s_state[:], 0.0), writes=["s_state"])

        for tci in range(T // TCH):
            tsl = slice(TCH * tci, TCH * (tci + 1))
            for n in range(16):
                for half in range(2):
                    hs = slice(512 * half, 512 * (half + 1))
                    gsl = slice(tsl.start + 512 * half, tsl.start + 512 * (half + 1))
                    p.op("sync", lambda s, n=n, gsl=gsl, hs=hs: s.dma_start(
                        out=rowf[0:1, hs], in_=Bm_d[n:n + 1, gsl]), writes=["rowf"], dma=True)
                    p.op("tensor", lambda t, hs=hs: t.matmul(ps_b[:], ones1_f[:], rowf[0:1, hs],
                                                             start=True, stop=True),
                         reads=["rowf", "ones1f"], writes=["ps_b"])
                    p.op("vector", lambda v, n=n, hs=hs: v.tensor_copy(Bb[:, TCH * n + hs.start:TCH * n + hs.stop], ps_b[:]),
                         reads=["ps_b"], writes=["Bb"])
                    p.op("sync", lambda s, n=n, gsl=gsl, hs=hs: s.dma_start(
                        out=rowf[0:1, hs], in_=Cm_d[n:n + 1, gsl]), writes=["rowf"], dma=True)
                    p.op("tensor", lambda t, hs=hs: t.matmul(ps_b[:], ones1_f[:], rowf[0:1, hs],
                                                             start=True, stop=True),
                         reads=["rowf", "ones1f"], writes=["ps_b"])
                    p.op("vector", lambda v, n=n, hs=hs: v.tensor_copy(Cb[:, TCH * n + hs.start:TCH * n + hs.stop], ps_b[:]),
                         reads=["ps_b"], writes=["Cb"])
            for cc in range(16):
                p.op("sync", lambda s, cc=cc, tsl=tsl: s.dma_start(
                    out=dtc[:], in_=dt_d[128 * cc:128 * (cc + 1), tsl]), writes=["dtc"], dma=True)
                p.op("sync", lambda s, cc=cc, tsl=tsl: s.dma_start(
                    out=uc[:], in_=u_d[128 * cc:128 * (cc + 1), tsl]), writes=["uc"], dma=True)
                p.op("sync", lambda s, cc=cc, tsl=tsl: s.dma_start(
                    out=zc[:], in_=z_d[128 * cc:128 * (cc + 1), tsl]), writes=["zc"], dma=True)
                p.op("vector", lambda v: v.tensor_mul(dtu[:], dtc[:], uc[:]), reads=["dtc", "uc"], writes=["dtu"])
                p.op("vector", lambda v, cc=cc: v.tensor_single_scalar(y_acc[:], uc[:], Dcol[:, cc:cc + 1],
                                                                       ALU.mult),
                     reads=["uc", "Dc"], writes=["y_acc"])
                for n in range(16):
                    p.op("scalar", lambda s, cc=cc, n=n: s.activation(
                        out=dA[:], in_=dtc[:], func=AF.Exp, scale=Aneg[:, 16 * cc + n:16 * cc + n + 1]),
                        reads=["dtc", "Aneg"], writes=["dA"])
                    p.op("vector", lambda v, n=n: v.tensor_mul(dBu[:], dtu[:], Bb[:, TCH * n:TCH * (n + 1)]),
                         reads=["dtu", "Bb"], writes=["dBu"])
                    p.op("vector", lambda v, cc=cc, n=n: v.tensor_tensor_scan(
                        out=s_sc[:], data0=dA[:], data1=dBu[:],
                        initial=s_state[:, 16 * cc + n:16 * cc + n + 1],
                        op0=ALU.mult, op1=ALU.add),
                        reads=["dA", "dBu", "s_state"], writes=["s_sc"])
                    p.op("vector", lambda v, cc=cc, n=n: v.tensor_copy(
                        s_state[:, 16 * cc + n:16 * cc + n + 1], s_sc[:, TCH - 1:TCH]),
                        reads=["s_sc"], writes=["s_state"])
                    p.op("vector", lambda v, n=n: v.tensor_mul(dBu[:], s_sc[:], Cb[:, TCH * n:TCH * (n + 1)]),
                         reads=["s_sc", "Cb"], writes=["dBu"])
                    p.op("vector", lambda v: v.tensor_add(y_acc[:], y_acc[:], dBu[:]),
                         reads=["y_acc", "dBu"], writes=["y_acc"])
                p.op("scalar", lambda s: s.activation(out=zc[:], in_=zc[:], func=AF.Silu),
                     reads=["zc"], writes=["zc"])
                p.op("vector", lambda v, cc=cc: v.tensor_mul(ygb[:, TCH * cc:TCH * (cc + 1)], y_acc[:], zc[:]),
                     reads=["y_acc", "zc"], writes=["ygb"])
            for tv in range(2):
                vs = slice(512 * tv, 512 * (tv + 1))
                for mc in range(8):
                    pd = ps_del[mc % 4]
                    mm_group(p, pd[:], "ps_del%d" % (mc % 4),
                             [(woT[:, 1024 * cc + 128 * mc:1024 * cc + 128 * (mc + 1)],
                               ygb[:, TCH * cc + vs.start:TCH * cc + vs.stop]) for cc in range(16)],
                             ["woT", "ygb"])
                    gsl = slice(tsl.start + vs.start, tsl.start + vs.stop)
                    p.op("sync", lambda s, mc=mc, gsl=gsl: s.dma_start(
                        out=ot[:], in_=xf[128 * mc:128 * (mc + 1), gsl]), writes=["xft"], dma=True)
                    p.op("vector", lambda v, pd=pd: v.tensor_add(ot[:], ot[:], pd[:]),
                         reads=["xft", "ps_del%d" % (mc % 4)], writes=["xft"])
                    p.op("sync", lambda s, mc=mc, gsl=gsl: s.dma_start(
                        out=xfo[128 * mc:128 * (mc + 1), gsl], in_=ot[:]),
                        reads=["xft"], writes=[], dma=True)
        p.finish("vector")

def emit_pre(p, nc, pfx, x, xf):
    with (
        nc.sbuf_tensor(pfx + "ident", [128, 128], F32) as ident,
        nc.sbuf_tensor(pfx + "xt", [128, D], F32) as xt,
        nc.sbuf_tensor(pfx + "xT", [128, 128], F32) as xT,
        nc.psum_tensor(pfx + "ps", [128, 128], F32) as ps,
    ):
        emit_identity(p, ident)
        for it in range(T // 128):
            p.op("sync", lambda s, it=it: s.dma_start(out=xt[:], in_=x[128*it:128*(it+1), :]),
                 reads=[], writes=["xt"], dma=True)
            for dc in range(D // 128):
                p.op("tensor", lambda t, dc=dc: t.transpose(ps[:], xt[:, 128*dc:128*(dc+1)], ident[:]),
                     reads=["xt", "ident"], writes=["ps"])
                p.op("vector", lambda v: v.tensor_copy(xT[:], ps[:]), reads=["ps"], writes=["xT"])
                p.op("gpsimd", lambda g, it=it, dc=dc: g.dma_start(
                    out=xf[128*dc:128*(dc+1), 128*it:128*(it+1)], in_=xT[:]),
                    reads=["xT"], writes=[], dma=True)
        p.finish("vector")

def emit_post(p, nc, pfx, xf, out):
    with (
        nc.sbuf_tensor(pfx + "ident", [128, 128], F32) as ident,
        nc.sbuf_tensor(pfx + "xt", [128, T], F32) as xt,
        nc.sbuf_tensor(pfx + "xT", [128, 128], F32) as xT,
        nc.psum_tensor(pfx + "ps", [128, 128], F32) as ps,
    ):
        emit_identity(p, ident)
        for dc in range(D // 128):
            p.op("sync", lambda s, dc=dc: s.dma_start(out=xt[:], in_=xf[128*dc:128*(dc+1), :]),
                 reads=[], writes=["xt"], dma=True)
            for it in range(T // 128):
                p.op("tensor", lambda t, it=it: t.transpose(ps[:], xt[:, 128*it:128*(it+1)], ident[:]),
                     reads=["xt", "ident"], writes=["ps"])
                p.op("vector", lambda v: v.tensor_copy(xT[:], ps[:]), reads=["ps"], writes=["xT"])
                p.op("gpsimd", lambda g, it=it, dc=dc: g.dma_start(
                    out=out[128*it:128*(it+1), 128*dc:128*(dc+1)], in_=xT[:]),
                    reads=["xT"], writes=[], dma=True)
        p.finish("vector")


_NC_CACHE = {}

def _get(name, builder):
    if name not in _NC_CACHE:
        _NC_CACHE[name] = builder()
    return _NC_CACHE[name]


def _moe_apply(xf_list, inputs, run):
    w_gate = inputs["L0_ffn_w_gate"]
    c_fc = inputs["L0_ffn_c_fc"]
    c_proj = inputs["L0_ffn_c_proj"]
    nc1 = _get("moe_gate", build_moe_gate)
    r1 = run(nc1, [{"xf": xf_list[b], "w_gate": w_gate} for b in range(NCORE)])
    all_e1 = np.stack([np.asarray(r1[b]["e1_d"], np.float32) for b in range(NCORE)])
    all_e2 = np.stack([np.asarray(r1[b]["e2_d"], np.float32) for b in range(NCORE)])
    all_pl1 = np.stack([np.asarray(r1[b]["pl1_d"], np.float32) for b in range(NCORE)])
    all_pl2 = np.stack([np.asarray(r1[b]["pl2_d"], np.float32) for b in range(NCORE)])
    tot_all = np.concatenate([r1[b]["tot_d"] for b in range(NCORE)], 0)
    xn_full = np.concatenate([r1[b]["xn_tm"] for b in range(NCORE)]
                             + [np.zeros((128, D), dtype=r1[0]["xn_tm"].dtype)], 0)
    nc2 = _get("moe_route", build_moe_route)
    in2 = []
    for b in range(NCORE):
        bsel = np.zeros((8, 1), np.float32); bsel[b] = 1
        eidv = np.full((1, 1), float(b), np.float32)
        in2.append({"all_e1": all_e1, "all_e2": all_e2, "all_pl1": all_pl1, "all_pl2": all_pl2,
                    "tot_all": tot_all, "own_w1": r1[b]["w1_d"], "own_w2": r1[b]["w2_d"],
                    "bsel": bsel, "eid": eidv, "own_e1": all_e1[b], "own_e2": all_e2[b],
                    "own_pl1": all_pl1[b], "own_pl2": all_pl2[b]})
    r2 = run(nc2, in2)
    nc3 = _get("moe_expert", build_moe_expert)
    in3 = [{"xn_full": xn_full, "slot_tbl": r2[e]["slot_tbl"],
            "c_fc": np.ascontiguousarray(c_fc[e]),
            "c_proj": np.ascontiguousarray(c_proj[e])} for e in range(NCORE)]
    r3 = run(nc3, in3)
    eo_full = np.concatenate([r3[e]["exp_out"] for e in range(NCORE)], 0)
    nc4 = _get("moe_combine", build_moe_combine)
    in4 = [{"xf": xf_list[b], "eo_full": eo_full, "gidx1_d": r2[b]["gidx1_d"],
            "gidx2_d": r2[b]["gidx2_d"], "vw1_d": r2[b]["vw1_d"], "vw2_d": r2[b]["vw2_d"]}
           for b in range(NCORE)]
    r4 = run(nc4, in4)
    return [r4[b]["xfo"] for b in range(NCORE)]




def build_mega():
    nc = bass.Bass("TRN2", target_bir_lowering=False, debug=False)
    RG = [list(range(8))]
    EIn = lambda n, sh, dt=F32: nc.dram_tensor(n, sh, dt, kind="ExternalInput").ap()
    Int = lambda n, sh, dt: nc.dram_tensor(n, sh, dt, kind="Internal").ap()
    x_in = EIn("x", [T, D])
    out = nc.dram_tensor("out", [T, D], F32, kind="ExternalOutput").ap()
    bsel = EIn("bsel", [8, 1])
    eid = EIn("eid", [1, 1])
    wg0 = EIn("wg0", [D, 8]); cfc0 = EIn("cfc0", [D, 3072]); cpj0 = EIn("cpj0", [1536, D])
    g1 = EIn("g1", [D]); b1v = EIn("b1v", [D])
    Wq = EIn("Wq", [D, D]); bq = EIn("bq", [D]); Wk = EIn("Wk", [D, D]); bk = EIn("bk", [D])
    Wv = EIn("Wv", [D, D]); bv = EIn("bv", [D]); Wo = EIn("Wo", [D, D]); bo = EIn("bo", [D])
    g2 = EIn("g2", [D]); b2v = EIn("b2v", [D])
    in_w = EIn("in_w", [D, 2 * DI]); in_b = EIn("in_b", [2 * DI])
    cw0 = EIn("cw0", [DI, 4]); cb0 = EIn("cb0", [DI])
    xp_w = EIn("xp_w", [DI, 32]); xp_b = EIn("xp_b", [32])
    dt0 = EIn("dt0", [DI]); A0 = EIn("A0", [16, 16])
    ow0 = EIn("ow0", [DI, D]); ob0 = EIn("ob0", [D])
    g3 = EIn("g3", [D]); b3v = EIn("b3v", [D])
    fc1w = EIn("fc1w", [D, 4 * D]); fc1b = EIn("fc1b", [4 * D])
    fc2w = EIn("fc2w", [4 * D, D]); fc2b = EIn("fc2b", [D])
    Wg1 = EIn("Wg1", [D, 8]); bg1 = EIn("bg1", [8])
    W11 = EIn("W11", [8, D, D]); b11 = EIn("b11", [8, D])
    W21 = EIn("W21", [8, D, D]); b21 = EIn("b21", [8, D])
    Win1 = EIn("Win1", [2 * DI, D]); cw1 = EIn("cw1", [DI, 4]); cb1 = EIn("cb1", [DI])
    Wx1 = EIn("Wx1", [32, DI]); Wdt1 = EIn("Wdt1", [DI, DI]); bdt1 = EIn("bdt1", [DI])
    Alog1 = EIn("Alog1", [DI, 16]); Dp1 = EIn("Dp1", [DI]); Wout1 = EIn("Wout1", [D, DI])
    xfA = Int("xfA", [D, T], F32); xfB = Int("xfB", [D, T], F32)
    xinA = Int("xinA", [D, T], F32); xinB = Int("xinB", [D, T], F32)
    Qd = Int("Qd_s", [D, T], BF16); Kd = Int("Kd_s", [D, T], BF16); Vd = Int("Vd_s", [T, D], BF16)
    BmS = Int("BmS", [16, T], F32); CmS = Int("CmS", [16, T], F32)
    res_d = Int("res_s", [DI, T], BF16)
    u_d = Int("u_s", [DI, T], BF16); z_d = Int("z_s", [DI, T], BF16); dtd = Int("dt_s", [DI, T], F32)
    import os as _os
    DBG = bool(_os.environ.get("MEGA_DEBUG"))
    rcb = Int("rcb", [4, 128, 32], F32)
    w1_loc = Int("w1_loc", [128, 32], F32); w2_loc = Int("w2_loc", [128, 32], F32)
    tot_loc = Int("tot_loc", [1, 8], F32)
    xn_loc = Int("xn_loc", [T, D], BF16)
    rc_all = Int("rc_all", [8, 4, 128, 32], F32)
    tot_all = Int("tot_all", [8, 8], F32)
    xnag = Int("xnag", [NCORE * T + 128, D], BF16)
    slot_tbl = Int("slot_tbl_s", [4224, 1], I32)
    vw1 = Int("vw1_s", [128, 32], F32); vw2 = Int("vw2_s", [128, 32], F32)
    gx1 = Int("gx1_s", [128, 32], I32); gx2 = Int("gx2_s", [128, 32], I32)
    eo_loc = Int("eo_loc", [4096, D], BF16)
    eoag = Int("eoag", [NCORE * 4096, D], BF16)

    p = P(nc)

    def flat2(ap):
        sh = ap.shape
        if len(sh) == 2:
            return ap
        if len(sh) == 3:
            return ap.rearrange("a b c -> (a b) c")
        return ap.rearrange("a b c d -> (a b c) d")

    def ag(in_ap, out_ap, wkey):
        p.fence()
        p.op("gpsimd", lambda g: g.collective_compute(
            "AllGather", ALU.bypass, ins=[flat2(in_ap).opt()], outs=[flat2(out_ap).opt()],
            replica_groups=RG), reads=[], writes=[wkey], coll=True)
        p.fence()

    def dbg_copy(name, src_ap, key):
        if not DBG:
            return
        d = nc.dram_tensor("d_" + name, list(src_ap.shape), src_ap.dtype, kind="ExternalOutput").ap()
        p.fence()
        p.op("gpsimd", lambda g: g.dma_start(out=flat2(d), in_=flat2(src_ap)),
             reads=[key], writes=["d_" + name], dma=True)
        p.fence()

    emit_pre(p, nc, "P0_", x_in, xfA)
    p.fence()

    import os
    STOP = int(os.environ.get("MEGA_STOP", "99"))

    def _finish_at(stream):
        emit_post(p, nc, "P1_", stream, out)
        p.finish("vector")
        p.finalize()
        return nc

    if STOP == 0:
        return _finish_at(xfA)

    def moe_app(pfx, xf_in, xf_out):
        dbgon = DBG and pfx == "M1_"
        emit_moe_gate(p, nc, pfx + "g_", xf_in, wg0, xn_loc,
                      rcb[0], rcb[1], w1_loc, w2_loc, rcb[2], rcb[3], tot_loc)
        if dbgon:
            dbg_copy("rcb", rcb, "rc_all")
            dbg_copy("tot_loc", tot_loc, "tot_all")
            dbg_copy("xn_loc", xn_loc, "xn_tm")
        ag(rcb, rc_all, "rc_all")
        ag(tot_loc, tot_all, "tot_all")
        ag(xn_loc, xnag[0:NCORE * T, :], "xnag")
        with nc.sbuf_tensor(pfx + "zpad", [128, D], BF16) as zpad:
            p.op("vector", lambda v: v.memset(zpad[:], 0.0), writes=["zpad"])
            p.op("sync", lambda s: s.dma_start(out=xnag[NCORE * T:NCORE * T + 128, :], in_=zpad[:]),
                 reads=["zpad"], writes=["xnag"], dma=True)
        p.fence()
        if dbgon:
            dbg_copy("rc_all", rc_all, "rc_all")
            dbg_copy("tot_all", tot_all, "tot_all")
            dbg_copy("xnag", xnag, "xnag")
        emit_moe_route(p, nc, pfx + "r_", rc_all[:, 0], rc_all[:, 1], rc_all[:, 2], rc_all[:, 3],
                       tot_all, w1_loc, w2_loc, bsel, eid, slot_tbl, vw1, vw2, gx1, gx2,
                       rcb[0], rcb[1], rcb[2], rcb[3])
        p.fence()
        if os.environ.get("MEGA_ABL") == "1":
            return
        if dbgon:
            dbg_copy("slot_tbl", slot_tbl, "slot_tbl")
            dbg_copy("vw1", vw1, "vw1")
            dbg_copy("vw2", vw2, "vw2")
            dbg_copy("gx1", gx1, "gx1")
            dbg_copy("gx2", gx2, "gx2")
        emit_moe_expert(p, nc, pfx + "e_", xnag, slot_tbl, cfc0, cpj0, eo_loc)
        if dbgon:
            dbg_copy("eo_loc", eo_loc, "eoag")
        ag(eo_loc, eoag, "eoag")
        emit_moe_combine(p, nc, pfx + "c_", xf_in, eoag, gx1, gx2, vw1, vw2, xf_out)
        p.fence()

    moe_app("M1_", xfA, xfB)
    if STOP == 1:
        return _finish_at(xfA if os.environ.get("MEGA_ABL") == "1" else xfB)
    emit_l0qkv(p, nc, "Q_", xfB, g1, b1v, Wq, bq, Wk, bk, Wv, bv, xinA, Qd, Kd, Vd)
    p.fence()
    emit_l0attn(p, nc, "A_", xinA, Qd, Kd, Vd, Wo, bo, xinB)
    p.fence()
    emit_l0mix_a(p, nc, "X_", xinB, g2, b2v, in_w, in_b, cw0, cb0, xp_w, xp_b, BmS, CmS, res_d)
    p.fence()
    emit_l0mix_b(p, nc, "Y_", xinB, BmS, CmS, res_d, dt0, A0, ow0, ob0, xinA)
    p.fence()
    emit_l0ffn_inner(p, nc, "F_", xinA, xfB, g3, b3v, fc1w, fc1b, fc2w, fc2b, xfA)
    p.fence()
    if STOP == 2:
        return _finish_at(xfA)
    moe_app("M2_", xfA, xfB)
    if STOP == 3:
        return _finish_at(xfB)
    emit_l1ffn(p, nc, "G_", xfB, Wg1, bg1, W11, b11, W21, b21, xfA)
    p.fence()
    if STOP == 4:
        return _finish_at(xfA)
    emit_l1mix_a(p, nc, "H_", xfA, Win1, cw1, cb1, Wx1, Wdt1, bdt1, u_d, z_d, dtd, BmS, CmS)
    p.fence()
    emit_l1mix_b(p, nc, "I_", xfA, u_d, z_d, dtd, BmS, CmS, Alog1, Dp1, Wout1, xfB)
    p.fence()
    if STOP == 5:
        return _finish_at(xfB)
    emit_l1ffn(p, nc, "J_", xfB, Wg1, bg1, W11, b11, W21, b21, xfA)
    p.fence()
    emit_post(p, nc, "P1_", xfA, out)
    p.finish("vector")
    p.finalize()
    return nc


def kernel(**inputs):
    inputs = {k: np.asarray(v) for k, v in inputs.items()}
    x = inputs["L0_attn_x"].astype(np.float32)
    nc = _get("mega", build_mega)
    shared = {
        "wg0": inputs["L0_ffn_w_gate"], "g1": inputs["L0_attn_ln1_g"], "b1v": inputs["L0_attn_ln1_b"],
        "Wq": inputs["L0_attn_Wq"], "bq": inputs["L0_attn_bq"], "Wk": inputs["L0_attn_Wk"],
        "bk": inputs["L0_attn_bk"], "Wv": inputs["L0_attn_Wv"], "bv": inputs["L0_attn_bv"],
        "Wo": inputs["L0_attn_Wo"], "bo": inputs["L0_attn_bo"], "g2": inputs["L0_attn_ln2_g"],
        "b2v": inputs["L0_attn_ln2_b"], "in_w": inputs["L0_attn_in_w"], "in_b": inputs["L0_attn_in_b"],
        "cw0": np.ascontiguousarray(inputs["L0_attn_conv_w"].reshape(DI, 4)),
        "cb0": inputs["L0_attn_conv_b"], "xp_w": inputs["L0_attn_xp_w"], "xp_b": inputs["L0_attn_xp_b"],
        "dt0": inputs["L0_attn_dt"], "A0": inputs["L0_attn_A"], "ow0": inputs["L0_attn_out_w"],
        "ob0": inputs["L0_attn_out_b"], "g3": inputs["L0_attn_ln3_g"], "b3v": inputs["L0_attn_ln3_b"],
        "fc1w": inputs["L0_attn_fc1_w"], "fc1b": inputs["L0_attn_fc1_b"],
        "fc2w": inputs["L0_attn_fc2_w"], "fc2b": inputs["L0_attn_fc2_b"],
        "Wg1": inputs["L1_ffn_Wg"], "bg1": inputs["L1_ffn_bg"], "W11": inputs["L1_ffn_W1"],
        "b11": inputs["L1_ffn_b1"], "W21": inputs["L1_ffn_W2"], "b21": inputs["L1_ffn_b2"],
        "Win1": inputs["L1_attn_W_in"],
        "cw1": np.ascontiguousarray(inputs["L1_attn_conv_w"].reshape(DI, 4)),
        "cb1": inputs["L1_attn_conv_b"], "Wx1": inputs["L1_attn_W_x"], "Wdt1": inputs["L1_attn_W_dt"],
        "bdt1": inputs["L1_attn_b_dt"], "Alog1": inputs["L1_attn_A_log"], "Dp1": inputs["L1_attn_D"],
        "Wout1": inputs["L1_attn_W_out"],
    }
    in_maps = []
    for b in range(NCORE):
        bsel = np.zeros((8, 1), np.float32); bsel[b] = 1
        m = dict(shared)
        m["x"] = np.ascontiguousarray(x[b])
        m["bsel"] = bsel
        m["eid"] = np.full((1, 1), float(b), np.float32)
        m["cfc0"] = np.ascontiguousarray(inputs["L0_ffn_c_fc"][b])
        m["cpj0"] = np.ascontiguousarray(inputs["L0_ffn_c_proj"][b])
        in_maps.append(m)
    res = run_bass_kernel_spmd(nc, in_maps, core_ids=list(range(NCORE)))
    return np.stack([res.results[b]["out"] for b in range(NCORE)], 0)


def kernel_unfused(**inputs):
    inputs = {k: np.asarray(v) for k, v in inputs.items()}
    x = inputs["L0_attn_x"].astype(np.float32)

    import os
    _trace = bool(os.environ.get("KERNEL_TRACE_LAUNCHES"))
    _times = []

    def run(nc, in_maps):
        res = run_bass_kernel_spmd(nc, in_maps, core_ids=list(range(NCORE)), trace=_trace)
        if _trace:
            _times.append((nc.name if hasattr(nc, "name") else "?", res.exec_time_ns))
            print("LAUNCH", len(_times), "exec_ns", res.exec_time_ns, flush=True)
        return res.results

    nc_pre = _get("pre", build_pre)
    r = run(nc_pre, [{"x": np.ascontiguousarray(x[b])} for b in range(NCORE)])
    xf = [r[b]["xf"] for b in range(NCORE)]

    xf = _moe_apply(xf, inputs, run)

    nc_qkv = _get("l0qkv", build_l0qkv)
    r = run(nc_qkv, [{"xf": xf[b], "g1": inputs["L0_attn_ln1_g"], "b1": inputs["L0_attn_ln1_b"],
                      "Wq": inputs["L0_attn_Wq"], "bq": inputs["L0_attn_bq"],
                      "Wk": inputs["L0_attn_Wk"], "bk": inputs["L0_attn_bk"],
                      "Wv": inputs["L0_attn_Wv"], "bv": inputs["L0_attn_bv"]} for b in range(NCORE)])
    nc_attn = _get("l0attn", build_l0attn)
    r2 = run(nc_attn, [{"xin0": r[b]["xin0"], "Qd": r[b]["Qd"], "Kd": r[b]["Kd"], "Vd": r[b]["Vd"],
                        "Wo": inputs["L0_attn_Wo"], "bo": inputs["L0_attn_bo"]} for b in range(NCORE)])
    xin = [r2[b]["xin1"] for b in range(NCORE)]
    nc_ma = _get("l0mix_a", build_l0mix_a)
    cw0 = np.ascontiguousarray(inputs["L0_attn_conv_w"].reshape(DI, 4))
    r = run(nc_ma, [{"xin": xin[b], "g2": inputs["L0_attn_ln2_g"], "b2": inputs["L0_attn_ln2_b"],
                     "in_w": inputs["L0_attn_in_w"], "in_b": inputs["L0_attn_in_b"],
                     "conv_w": cw0, "conv_b": inputs["L0_attn_conv_b"],
                     "xp_w": inputs["L0_attn_xp_w"], "xp_b": inputs["L0_attn_xp_b"]}
                    for b in range(NCORE)])
    nc_mb = _get("l0mix_b", build_l0mix_b)
    r2 = run(nc_mb, [{"xin": xin[b], "Bm": r[b]["Bm"], "Cm": r[b]["Cm"], "res_d": r[b]["res_d"],
                      "dtv": inputs["L0_attn_dt"], "Amat": inputs["L0_attn_A"],
                      "out_w": inputs["L0_attn_out_w"], "out_b": inputs["L0_attn_out_b"]}
                     for b in range(NCORE)])
    xin = [r2[b]["xino"] for b in range(NCORE)]
    nc_fi = _get("l0ffn_inner", build_l0ffn_inner)
    r = run(nc_fi, [{"xin": xin[b], "xf": xf[b], "g3": inputs["L0_attn_ln3_g"],
                     "b3": inputs["L0_attn_ln3_b"], "fc1w": inputs["L0_attn_fc1_w"],
                     "fc1b": inputs["L0_attn_fc1_b"], "fc2w": inputs["L0_attn_fc2_w"],
                     "fc2b": inputs["L0_attn_fc2_b"]} for b in range(NCORE)])
    xf = [r[b]["xfo"] for b in range(NCORE)]

    xf = _moe_apply(xf, inputs, run)

    nc_l1f = _get("l1ffn", build_l1ffn)
    def l1ffn_apply(xf):
        r = run(nc_l1f, [{"xf": xf[b], "Wg": inputs["L1_ffn_Wg"], "bg": inputs["L1_ffn_bg"],
                          "W1": inputs["L1_ffn_W1"], "b1": inputs["L1_ffn_b1"],
                          "W2": inputs["L1_ffn_W2"], "b2": inputs["L1_ffn_b2"]}
                         for b in range(NCORE)])
        return [r[b]["xfo"] for b in range(NCORE)]

    xf = l1ffn_apply(xf)
    nc_m1 = _get("l1mix_a", build_l1mix_a)
    cw1 = np.ascontiguousarray(inputs["L1_attn_conv_w"].reshape(DI, 4))
    r = run(nc_m1, [{"xf": xf[b], "W_in": inputs["L1_attn_W_in"], "conv_w": cw1,
                     "conv_b": inputs["L1_attn_conv_b"], "W_x": inputs["L1_attn_W_x"],
                     "W_dt": inputs["L1_attn_W_dt"], "b_dt": inputs["L1_attn_b_dt"]}
                    for b in range(NCORE)])
    nc_m2 = _get("l1mix_b", build_l1mix_b)
    r2 = run(nc_m2, [{"xf": xf[b], "u_d": r[b]["u_d"], "z_d": r[b]["z_d"], "dt_d": r[b]["dt_d"],
                      "Bm_d": r[b]["Bm_d"], "Cm_d": r[b]["Cm_d"], "A_log": inputs["L1_attn_A_log"],
                      "Dp": inputs["L1_attn_D"], "W_out": inputs["L1_attn_W_out"]}
                     for b in range(NCORE)])
    xf = [r2[b]["xfo"] for b in range(NCORE)]
    xf = l1ffn_apply(xf)

    nc_post = _get("post", build_post)
    r = run(nc_post, [{"xf": xf[b]} for b in range(NCORE)])
    out = np.stack([r[b]["out"] for b in range(NCORE)], 0)
    return out
